# version 03 (final)
# baseline (speedup 1.0000x reference)
import math
from contextlib import ExitStack

import numpy as np
import ml_dtypes
import concourse.bass as bass
import concourse.mybir as mybir
from concourse.bass_utils import run_bass_kernel_spmd

F32 = mybir.dt.float32
BF16 = mybir.dt.bfloat16
I32 = mybir.dt.int32
AF = mybir.ActivationFunctionType
ALU = mybir.AluOpType
AX = mybir.AxisListType.X

D = 1024
B = 4
L = 4096
TO = 2048
NT_O = 16
NT_A = 32
KC = 8


ENGINES = ("sync", "tensor", "vector", "scalar", "gpsimd")


class Tok:
    __slots__ = ("key", "eng", "dma", "used", "val")

    def __init__(self, key, eng, dma):
        self.key, self.eng, self.dma = key, eng, dma
        self.used = False
        self.val = None


class Buf:
    __slots__ = ("name", "ws", "rs")

    def __init__(self, name):
        self.name = name
        self.ws = {}
        self.rs = []


class Sched:
    def __init__(self):
        self.ops = []
        self.last_eng = {}
        self.last_dma = {}
        self.fence = None

    def emit(self, eng, fn, reads=(), writes=(), after=(), dma=False, sem=None,
             keep_same=False):
        raw = []
        for b in reads:
            raw.extend(b.ws.values())
        raw.extend(t for t in after if t is not None)
        war = []
        for b in writes:
            war.extend(b.rs)
            war.extend(b.ws.values())
        if self.fence is not None:
            raw.append(self.fence)
        need = []
        seen = set()
        for t in raw:
            if t is None or (t.eng == eng and not t.dma and eng == "tensor"):
                continue
            if id(t) in seen:
                continue
            seen.add(id(t))
            t.used = True
            need.append(t)
        for t in war:
            if t is None or (t.eng == eng and not t.dma and eng == "tensor"):
                continue
            if id(t) in seen:
                continue
            seen.add(id(t))
            t.used = True
            need.append(t)
        if sem is None:
            sem = ("dma_" + (writes[0].name if writes else "misc")) if dma else eng
        tok = Tok(sem, eng, dma)
        self.ops.append((eng, fn, need, tok))
        for b in reads:
            b.rs.append(tok)
        for b in writes:
            wkey = "dma" if dma else eng
            b.ws[wkey] = tok
            b.rs = []
        if dma:
            self.last_dma[sem] = tok
        else:
            self.last_eng[eng] = tok
        return tok

    def barrier(self, eng="vector"):
        toks = list(self.last_eng.values()) + list(self.last_dma.values())
        self.fence = None
        tok = self.emit(eng, lambda e: e.nop(), after=toks, keep_same=True)
        tok.used = True
        self.fence = tok
        return tok

    def dma(self, eng, dst_ap, src_ap, reads=(), writes=(), after=(), sem=None):
        return self.emit(
            eng, lambda e, d=dst_ap, s=src_ap: e.dma_start(out=d, in_=s),
            reads=reads, writes=writes, after=after, dma=True, sem=sem)

    def build(self, nc, es):
        counts = {}
        for eng, fn, need, tok in self.ops:
            if tok.dma or tok.used:
                inc = 16 if tok.dma else 1
                tok.val = counts.get(tok.key, 0) + inc
                counts[tok.key] = tok.val
        sems = {k: es.enter_context(nc.semaphore(k)) for k in counts}
        assert len(sems) < 180, f"too many semaphores: {len(sems)}"
        block = es.enter_context(nc.Block())
        per_eng = {e: [] for e in ENGINES}
        for op in self.ops:
            per_eng[op[0]].append(op)

        def emit_engine(ops):
            def body(eng):
                maxw = {}
                for _, fn, need, tok in ops:
                    for t in need:
                        if maxw.get(t.key, 0) >= t.val:
                            continue
                        eng.wait_ge(sems[t.key], t.val)
                        maxw[t.key] = t.val
                    inst = fn(eng)
                    if tok.val is not None:
                        inst.then_inc(sems[tok.key], 16 if tok.dma else 1)
            return body

        for e in ENGINES:
            if per_eng[e]:
                getattr(block, e)(emit_engine(per_eng[e]))
        return counts



def setup_consts(S, nc, es, values):
    values = sorted(set(float(v) for v in values))
    bank = es.enter_context(nc.sbuf_tensor("constbank", [128, max(1, len(values))], F32))
    b = Buf("constbank")
    for i, v in enumerate(values):
        ap = bank[:, i:i + 1]
        S.emit("vector", lambda e, a=ap, vv=v: e.memset(a, vv), writes=[b])
        nc.const_aps.register(F32, v, ap)
    S.barrier()


def cast_weights(S, nc, pairs, chunk_bytes=4 << 20, chain=None):
    for src, dst, b in pairs:
        rows = src.shape[0]
        rowbytes = 4
        for d_ in src.shape[1:]:
            rowbytes *= d_
        rb = max(1, min(rows, chunk_bytes // rowbytes))
        r0 = 0
        while r0 < rows:
            r1 = min(rows, r0 + rb)
            wl = [chain, b] if chain is not None else [b]
            S.dma("gpsimd", dst[r0:r1], src[r0:r1], writes=wl)
            r0 = r1


def bcast_row(S, nc, ones_bf, row_bf_ap, out_sb, out_buf, ps, ps_buf, width,
              after=(), reads=()):
    for c0 in range(0, width, 512):
        cw = min(512, width - c0)
        S.emit("tensor",
               lambda e, o=ps[:, 0:cw], l=ones_bf, r=row_bf_ap[0:1, c0:c0 + cw]:
               e.matmul(out=o, lhsT=l, rhs=r, start=True, stop=True),
               reads=list(reads), writes=[ps_buf], after=after)
        S.emit("vector",
               lambda e, o=out_sb[:, c0:c0 + cw], i=ps[:, 0:cw]:
               e.tensor_copy(out=o, in_=i),
               reads=[ps_buf], writes=[out_buf])



def emit_layernorm(S, nc, x_own, x_oth, xnT_own, b_xnT_own, xnT_oth, b_xnT_oth,
                   sident, b_ident, xn_d=None):
    with ExitStack() as p1:
        sbl = lambda name, shape, dt=F32: p1.enter_context(nc.sbuf_tensor(name, shape, dt))
        psl = lambda name, shape, dt=F32: p1.enter_context(nc.psum_tensor(name, shape, dt))
        xin = [sbl(f"xin{i}", [128, D]) for i in range(3)]
        b_xin = [Buf(f"xin{i}") for i in range(3)]
        sqs = [sbl(f"sq{i}", [128, D]) for i in range(2)]
        b_sq = [Buf(f"sq{i}") for i in range(2)]
        xnbf = [sbl(f"xnbf{i}", [128, D], BF16) for i in range(2)]
        b_xnbf = [Buf(f"xnbf{i}") for i in range(2)]
        stat = [sbl(f"stat{i}", [128, 4]) for i in range(2)]
        b_stat = [Buf(f"stat{i}") for i in range(2)]
        ptr = [psl(f"ptr{i}", [128, 1024], BF16) for i in range(2)]
        b_ptr = [Buf(f"ptr{i}") for i in range(2)]

        for tt in range(-2, NT_A):
            warm = tt < 0
            ttw = max(tt, 0)
            src = x_own[ttw * 128:(ttw + 1) * 128, :] if ttw < NT_O else \
                x_oth[(ttw - NT_O) * 128:(ttw - NT_O + 1) * 128, :]
            i3, i2 = ttw % 3, (tt + 2) % 2 if warm else ttw % 2
            tt = ttw
            S.dma("sync", xin[i3][:], src, writes=[b_xin[i3]])
            st, b_st = stat[i2], b_stat[i2]
            S.emit("vector", lambda e, o=st[:, 0:1], i=xin[i3][:]:
                   e.reduce_sum(out=o, in_=i, axis=AX),
                   reads=[b_xin[i3]], writes=[b_st])
            S.emit("vector", lambda e, o=st[:, 1:2], i=st[:, 0:1]:
                   e.tensor_scalar_mul(out=o, in0=i, scalar1=1.0 / D),
                   reads=[b_st], writes=[b_st])
            S.emit("vector", lambda e, o=st[:, 0:1], i=st[:, 1:2]:
                   e.tensor_scalar_mul(out=o, in0=i, scalar1=-1.0),
                   reads=[b_st], writes=[b_st])
            S.emit("scalar", lambda e, o=sqs[i2][:], i=xin[i3][:], bi=st[:, 0:1]:
                   e.activation(out=o, in_=i, func=AF.Square, bias=bi, scale=1.0),
                   reads=[b_xin[i3], b_st], writes=[b_sq[i2]])
            S.emit("vector", lambda e, o=st[:, 2:3], i=sqs[i2][:]:
                   e.reduce_sum(out=o, in_=i, axis=AX),
                   reads=[b_sq[i2]], writes=[b_st])
            S.emit("scalar", lambda e, o=st[:, 3:4], i=st[:, 2:3]:
                   e.activation(out=o, in_=i, func=AF.Sqrt, bias=1e-5, scale=1.0 / D),
                   reads=[b_st], writes=[b_st])
            S.emit("vector", lambda e, o=st[:, 2:3], i=st[:, 3:4]:
                   e.reciprocal(out=o, in_=i),
                   reads=[b_st], writes=[b_st])
            S.emit("vector", lambda e, o=xnbf[i2][:], i=xin[i3][:], m=st[:, 1:2],
                   r=st[:, 2:3]:
                   e.tensor_scalar(out=o, in0=i, scalar1=m, scalar2=r,
                                   op0=ALU.subtract, op1=ALU.mult),
                   reads=[b_xin[i3], b_st], writes=[b_xnbf[i2]])
            if warm:
                continue
            if xn_d is not None and tt < NT_O:
                S.dma("sync", xn_d[tt * 128:(tt + 1) * 128, :], xnbf[i2][:],
                      reads=[b_xnbf[i2]], writes=[Buf(f"xnd{i2}")],
                      sem=f"dma_xnd{i2}")
            for kc in range(KC):
                S.emit("tensor", lambda e, o=ptr[i2][:, kc * 128:(kc + 1) * 128],
                       i=xnbf[i2][:, kc * 128:(kc + 1) * 128], idn=sident[:]:
                       e.transpose(out=o, in_=i, identity=idn),
                       reads=[b_xnbf[i2], b_ident], writes=[b_ptr[i2]])
            dstT, b_dstT = (xnT_own, b_xnT_own) if tt < NT_O else (xnT_oth, b_xnT_oth)
            ttl = tt if tt < NT_O else tt - NT_O
            dview = dstT[:].rearrange("p (c t) -> p c t", c=KC)[:, :, ttl * 128:(ttl + 1) * 128]
            pview = ptr[i2][:].rearrange("p (c t) -> p c t", c=KC)
            S.emit("vector", lambda e, o=dview, i=pview:
                   e.tensor_copy(out=o, in_=i),
                   reads=[b_ptr[i2]], writes=[b_dstT])
    S.barrier()


def build_l0(debug=False):
    nc = bass.Bass("TRN2", target_bir_lowering=False)
    I = 2816
    NEXP = 8
    H, DH = 16, 64

    def din(name, shape, dt=F32):
        return nc.dram_tensor(name, shape, dt, kind="ExternalInput").ap()

    x_own = din("x_own", [TO, D])
    x_oth = din("x_oth", [TO, D])
    cosr = din("cosr", [L, D])
    sinr = din("sinr", [L, D])
    wqkvT = din("wqkvT", [D, 3 * D])
    bqkv = din("bqkv", [1, 3 * D])
    gq = din("gq", [1, D])
    gk = din("gk", [1, D])
    woutT = din("woutT", [D, D])
    bout = din("bout", [1, D])
    sgT = din("sgT", [D, I])
    suT = din("suT", [D, I])
    sdT = din("sdT", [I, D])
    rgT = din("rgT", [7, D, I])
    ruT = din("ruT", [7, D, I])
    rdT = din("rdT", [7, I, D])
    wrT = din("wrT", [D, 7])
    rb = din("rb", [1, 7])
    ident = din("ident", [128, 128], BF16)
    xo = nc.dram_tensor("xo", [TO, D], F32, kind="ExternalOutput").ap()
    dbg = {}
    if debug:
        for nm, shape, dt in [("d_xnT", [128, KC * TO], BF16), ("d_qT", [128, KC * TO], BF16),
                              ("d_kT", [128, KC * L], BF16), ("d_v", [L, D], BF16),
                              ("d_otok", [128, NT_O * D], BF16),
                              ("d_sfin", [128, NT_O * D], BF16),
                              ("d_wprob", [128, NT_O * 7], F32)]:
            dbg[nm] = nc.dram_tensor(nm, shape, dt, kind="ExternalOutput").ap()

    def dint(name, shape, dt=BF16):
        return nc.dram_tensor(name, shape, dt, kind="Internal").ap()

    wc_qkv = dint("wc_qkv", [D, 3 * D])
    wc_bqkv = dint("wc_bqkv", [1, 3 * D])
    wc_gq = dint("wc_gq", [1, D])
    wc_gk = dint("wc_gk", [1, D])
    wc_wout = dint("wc_wout", [D, D])
    wc_bout = dint("wc_bout", [1, D])
    wc_sg = dint("wc_sg", [D, I])
    wc_su = dint("wc_su", [D, I])
    wc_sd = dint("wc_sd", [I, D])
    wc_rg = dint("wc_rg", [7, D, I])
    wc_ru = dint("wc_ru", [7, D, I])
    wc_rd = dint("wc_rd", [7, I, D])
    wc_wr = dint("wc_wr", [D, 7])
    wc_rb = dint("wc_rb", [1, 7])
    v_d = dint("v_d", [L, D])

    S = Sched()
    with ExitStack() as es:
        sb = lambda name, shape, dt=F32: es.enter_context(nc.sbuf_tensor(name, shape, dt))
        ps = lambda name, shape, dt=F32: es.enter_context(nc.psum_tensor(name, shape, dt))

        sident = sb("sident", [128, 128], BF16)
        sones = sb("sones", [1, 128], BF16)
        b_ident = Buf("ident")
        b_ones = Buf("ones")
        S.dma("sync", sident[:], ident, writes=[b_ident])
        S.emit("vector", lambda e: e.memset(sones[:], 1.0), writes=[b_ones])
        setup_consts(S, nc, es, [1e-5, 1e-6, 1.0 / D, 1.0 / 64, -0.5, 0.125])

        b_wc = {}
        for nm, src, dst in [
            ("qkv", wqkvT, wc_qkv), ("bqkv", bqkv, wc_bqkv), ("gq", gq, wc_gq),
            ("gk", gk, wc_gk), ("wout", woutT, wc_wout), ("bout", bout, wc_bout),
            ("sg", sgT, wc_sg), ("su", suT, wc_su), ("sd", sdT, wc_sd),
            ("wr", wrT, wc_wr), ("rb", rb, wc_rb),
        ]:
            b = b_wc[nm] = Buf("wc_" + nm)
            cast_weights(S, nc, [(src, dst, b)])
        for nm, src, dst in [("rg", rgT, wc_rg), ("ru", ruT, wc_ru), ("rd", rdT, wc_rd)]:
            b = b_wc[nm] = Buf("wc_" + nm)
            for ei in range(7):
                cast_weights(S, nc, [(src[ei], dst[ei], b)])

        xnT_own = sb("xnT_own", [128, KC * TO], BF16)
        b_xnT_own = Buf("xnT_own")
        xnT_oth_g = nc.sbuf_tensor("xnT_oth", [128, KC * TO], BF16, side="right")
        xnT_oth = xnT_oth_g.__enter__()
        b_xnT_oth = Buf("xnT_oth")

        rinv_all = sb("rinv_all", [128, NT_O])
        b_rinv = Buf("rinv_all")

        emit_layernorm(S, nc, x_own, x_oth, xnT_own, b_xnT_own, xnT_oth, b_xnT_oth,
                       sident, b_ident)
        S.barrier()
        qT_g = nc.sbuf_tensor("qT", [128, KC * TO], BF16)
        qT = qT_g.__enter__()
        b_qT = Buf("qT")
        kT_g = nc.sbuf_tensor("kT", [128, KC * L], BF16)
        kT = kT_g.__enter__()
        b_kT = Buf("kT")

        with ExitStack() as p2:
            sbl = lambda name, shape, dt=F32: p2.enter_context(nc.sbuf_tensor(name, shape, dt))
            psl = lambda name, shape, dt=F32: p2.enter_context(nc.psum_tensor(name, shape, dt))
            gqb = sbl("gqb", [128, D], BF16)
            gkb = sbl("gkb", [128, D], BF16)
            b_gqb, b_gkb = Buf("gqb"), Buf("gkb")
            grow = sbl("grow", [1, 2 * D], BF16)
            b_grow = Buf("grow")
            S.dma("sync", grow[0:1, 0:D], wc_gq, reads=[b_wc["gq"]], writes=[b_grow])
            S.dma("sync", grow[0:1, D:2 * D], wc_gk, reads=[b_wc["gk"]], writes=[b_grow])
            ps_b = psl("ps_b", [128, 512])
            b_ps_b = Buf("ps_b")
            for c0 in range(0, 2 * D, 512):
                gb, gbuf, cc0 = (gqb, b_gqb, c0) if c0 < D else (gkb, b_gkb, c0 - D)
                S.emit("tensor", lambda e, o=ps_b[:, 0:512], l=sones[:],
                       r=grow[0:1, c0:c0 + 512]:
                       e.matmul(out=o, lhsT=l, rhs=r, start=True, stop=True),
                       reads=[b_grow, b_ones], writes=[b_ps_b])
                S.emit("vector", lambda e, o=gb[:, cc0:cc0 + 512], i=ps_b[:, 0:512]:
                       e.tensor_copy(out=o, in_=i),
                       reads=[b_ps_b], writes=[gbuf])
            bq_sb = sbl("bq_sb", [1, 3 * D], BF16)
            b_bq = Buf("bq_sb")
            S.dma("sync", bq_sb[:], wc_bqkv, reads=[b_wc["bqkv"]], writes=[b_bq])

            wcol = [sbl(f"wcol{i}", [128, KC * 512], BF16) for i in range(2)]
            b_wcol = [Buf(f"wcol{i}") for i in range(2)]
            ps_qkv = [psl(f"ps_qkv{i}", [128, 512]) for i in range(2)]
            b_ps_qkv = [Buf(f"ps_qkv{i}") for i in range(2)]
            ps_tr = [psl(f"ps_tr{i}", [128, 512], BF16) for i in range(2)]
            b_ps_tr = [Buf(f"ps_tr{i}") for i in range(2)]
            _qkh = sbl("qk_h0", [128, 512]); _bqkh = Buf("qk_h0")
            qk_h, b_qk_h = [_qkh, _qkh], [_bqkh, _bqkh]
            u_f = sbl("u_f", [128, 512])
            b_u_f = Buf("u_f")
            w_f = sbl("w_f", [128, 512])
            b_w_f = Buf("w_f")
            _rf = sbl("r_f0", [128, 512]); _brf = Buf("r_f0")
            r_f, b_r_f = [_rf, _rf], [_brf, _brf]
            _cs = sbl("cs_t0", [128, 1024]); _bcs = Buf("cs_t0")
            cs_t, b_cs = [_cs, _cs], [_bcs, _bcs]
            hs_t = sbl("hs_t", [128, 17])
            b_hs = Buf("hs_t")
            _qb = sbl("qkbf0", [128, 512], BF16); _bqb = Buf("qkbf0")
            qkbf, b_qkbf = [_qb, _qb], [_bqb, _bqb]
            _vb = sbl("vbf0", [128, 512], BF16); _bvb = Buf("vbf0")
            vbf, b_vbf = [_vb, _vb], [_bvb, _bvb]

            for sect in range(3):
                ntile = NT_O if sect == 0 else NT_A
                for half in range(2):
                    c0 = sect * D + half * 512
                    iw = half % 2
                    S.dma("sync",
                          wcol[iw][:].rearrange("p (c n) -> p c n", c=KC),
                          wc_qkv[:, c0:c0 + 512].rearrange("(c p) n -> p c n", p=128),
                          reads=[b_wc["qkv"]], writes=[b_wcol[iw]])
                    for tt in range(ntile):
                        i2 = tt % 2
                        lhs_base, b_lhs = (xnT_own, b_xnT_own) if tt < NT_O else                             (xnT_oth, b_xnT_oth)
                        ttl = tt % NT_O
                        for kc in range(KC):
                            S.emit("tensor", lambda e, o=ps_qkv[i2][:, :],
                                   l=lhs_base[:, kc * TO + ttl * 128: kc * TO + (ttl + 1) * 128],
                                   r=wcol[iw][:, kc * 512:(kc + 1) * 512],
                                   st=(kc == 0):
                                   e.matmul(out=o, lhsT=l, rhs=r, start=st, stop=False),
                                   reads=[b_lhs, b_wcol[iw]], writes=[b_ps_qkv[i2]])
                        S.emit("tensor", lambda e, o=ps_qkv[i2][:, :], l=sones[:],
                               r=bq_sb[0:1, c0:c0 + 512]:
                               e.matmul(out=o, lhsT=l, rhs=r, start=False, stop=True),
                               reads=[b_bq, b_ones], writes=[b_ps_qkv[i2]])
                        if sect == 2:
                            S.emit("vector", lambda e, o=vbf[i2][:], i=ps_qkv[i2][:, :]:
                                   e.tensor_copy(out=o, in_=i),
                                   reads=[b_ps_qkv[i2]], writes=[b_vbf[i2]])
                            S.dma("sync",
                                  v_d[tt * 128:(tt + 1) * 128, half * 512:(half + 1) * 512],
                                  vbf[i2][:], reads=[b_vbf[i2]], writes=[Buf("v_d_st")],
                                  sem="dma_vd")
                            continue
                        S.emit("vector", lambda e, o=qk_h[i2][:], i=ps_qkv[i2][:, :]:
                               e.tensor_copy(out=o, in_=i),
                               reads=[b_ps_qkv[i2]], writes=[b_qk_h[i2]])
                        S.dma("sync", cs_t[i2][:, 0:512],
                              cosr[tt * 128:(tt + 1) * 128, half * 512:(half + 1) * 512],
                              writes=[b_cs[i2]])
                        S.dma("sync", cs_t[i2][:, 512:1024],
                              sinr[tt * 128:(tt + 1) * 128, half * 512:(half + 1) * 512],
                              writes=[b_cs[i2]])
                        S.emit("vector", lambda e, o=u_f[:], a=qk_h[i2][:], c=cs_t[i2][:, 0:512]:
                               e.tensor_mul(out=o, in0=a, in1=c),
                               reads=[b_qk_h[i2], b_cs[i2]], writes=[b_u_f])
                        S.emit("vector", lambda e, o=w_f[:], a=qk_h[i2][:], c=cs_t[i2][:, 512:1024]:
                               e.tensor_mul(out=o, in0=a, in1=c),
                               reads=[b_qk_h[i2], b_cs[i2]], writes=[b_w_f])
                        uv = u_f[:].rearrange("p (h q) -> p h q", h=8)
                        wv = w_f[:].rearrange("p (h q) -> p h q", h=8)
                        rv = r_f[i2][:].rearrange("p (h q) -> p h q", h=8)
                        S.emit("vector", lambda e, o=rv[:, :, 0:32], a=uv[:, :, 0:32],
                               b_=wv[:, :, 32:64]:
                               e.tensor_sub(out=o, in0=a, in1=b_),
                               reads=[b_u_f, b_w_f], writes=[b_r_f[i2]])
                        S.emit("vector", lambda e, o=rv[:, :, 32:64], a=uv[:, :, 32:64],
                               b_=wv[:, :, 0:32]:
                               e.tensor_add(out=o, in0=a, in1=b_),
                               reads=[b_u_f, b_w_f], writes=[b_r_f[i2]])
                        S.emit("scalar", lambda e, o=u_f[:], i=r_f[i2][:]:
                               e.activation(out=o, in_=i, func=AF.Square),
                               reads=[b_r_f[i2]], writes=[b_u_f])
                        S.emit("vector", lambda e, o=hs_t[:, 0:8],
                               i=u_f[:].rearrange("p (h q) -> p h q", h=8):
                               e.reduce_sum(out=o, in_=i, axis=AX),
                               reads=[b_u_f], writes=[b_hs])
                        S.emit("scalar", lambda e, o=hs_t[:, 8:16], i=hs_t[:, 0:8]:
                               e.activation(out=o, in_=i, func=AF.Ln, bias=1e-6,
                                            scale=1.0 / 64),
                               reads=[b_hs], writes=[b_hs])
                        S.emit("scalar", lambda e, o=hs_t[:, 0:8], i=hs_t[:, 8:16]:
                               e.activation(out=o, in_=i, func=AF.Exp, scale=-0.5),
                               reads=[b_hs], writes=[b_hs])
                        gb, b_gb = (gqb, b_gqb) if sect == 0 else (gkb, b_gkb)
                        for h8 in range(8):
                            S.emit("vector", lambda e, o=qkbf[i2][:, h8 * 64:(h8 + 1) * 64],
                                   i=r_f[i2][:, h8 * 64:(h8 + 1) * 64], sc=hs_t[:, h8:h8 + 1],
                                   g=gb[:, half * 512 + h8 * 64: half * 512 + (h8 + 1) * 64]:
                                   e.scalar_tensor_tensor(out=o, in0=i, scalar=sc, in1=g,
                                                          op0=ALU.mult, op1=ALU.mult),
                                   reads=[b_r_f[i2], b_hs, b_gb], writes=[b_qkbf[i2]])
                        for j in range(4):
                            S.emit("tensor", lambda e, o=ps_tr[i2][:, j * 128:(j + 1) * 128],
                                   i=qkbf[i2][:, j * 128:(j + 1) * 128], idn=sident[:]:
                                   e.transpose(out=o, in_=i, identity=idn),
                                   reads=[b_qkbf[i2], b_ident], writes=[b_ps_tr[i2]])
                        if sect == 0:
                            dstT, b_dstT, per, ttl2 = qT, b_qT, TO, tt
                        else:
                            dstT, b_dstT, per, ttl2 = kT, b_kT, L, tt
                        dview = dstT[:].rearrange("p (c t) -> p c t", c=KC)[
                            :, half * 4:half * 4 + 4, ttl2 * 128:(ttl2 + 1) * 128]
                        S.emit("vector", lambda e, o=dview,
                               i=ps_tr[i2][:].rearrange("p (c t) -> p c t", c=4):
                               e.tensor_copy(out=o, in_=i),
                               reads=[b_ps_tr[i2]], writes=[b_dstT])

        S.barrier()
        if debug:
            S.dma("sync", dbg["d_xnT"], xnT_own[:], reads=[b_xnT_own], writes=[Buf("dbg1")])
            S.dma("sync", dbg["d_qT"], qT[:], reads=[b_qT], writes=[Buf("dbg2")])
            S.dma("sync", dbg["d_kT"], kT[:], reads=[b_kT], writes=[Buf("dbg3")])
            S.dma("sync", dbg["d_v"], v_d, writes=[Buf("dbg4")])
            S.barrier()
        xnT_oth_g.__exit__(None, None, None)

        o_tok_g = nc.sbuf_tensor("o_tok", [128, NT_O * D], BF16, side="right")
        o_tok = o_tok_g.__enter__()
        b_o_tok = Buf("o_tok")
        with ExitStack() as p3:
            sbl = lambda name, shape, dt=F32: p3.enter_context(nc.sbuf_tensor(name, shape, dt))
            psl = lambda name, shape, dt=F32: p3.enter_context(nc.psum_tensor(name, shape, dt))
            v_h = [sbl(f"v_h{i}", [128, 32 * 66], BF16) for i in range(2)]
            b_vh = [Buf(f"v_h{i}") for i in range(2)]
            ps_sc = psl("ps_sc", [128, 2048])
            b_ps_sc = Buf("ps_sc")
            ps_pt = psl("ps_pt", [128, 1024], BF16)
            b_ps_pt = Buf("ps_pt")
            ps_o = psl("ps_o", [128, 66])
            b_ps_o = Buf("ps_o")
            p_sb = sbl("p_sb", [128, 2048], BF16)
            b_p_sb = Buf("p_sb")
            ptT = sbl("ptT", [128, 32 * 512], BF16)
            b_ptT = Buf("ptT")
            mx = sbl("mx", [128, 16])
            b_mx = Buf("mx")
            sumc = sbl("sumc", [128, 8])
            b_sumc = Buf("sumc")

            for h in range(H):
                ih = h % 2
                S.dma("sync",
                      v_h[ih][:].rearrange("p (c n) -> p c n", c=32)[:, :, 0:64],
                      v_d[:, h * 64:(h + 1) * 64].rearrange("(c p) n -> p c n", p=128),
                      writes=[b_vh[ih]])
                S.emit("vector", lambda e,
                       o=v_h[ih][:].rearrange("p (c n) -> p c n", c=32)[:, :, 64:65]:
                       e.memset(o, 1.0), writes=[b_vh[ih]])
                kchunk, krow = h // 2, (h % 2) * 64
                for qtg in range(4):
                    for qi in range(4):
                        qt = qtg * 4 + qi
                        qsl = qT[krow:krow + 64,
                                 kchunk * TO + qt * 128: kchunk * TO + (qt + 1) * 128]
                        for half in range(2):
                            for nb in range(4):
                                j0 = half * 2048 + nb * 512
                                S.emit("tensor", lambda e, o=ps_sc[:, nb * 512:(nb + 1) * 512],
                                       l=qsl, r=kT[krow:krow + 64, kchunk * L + j0: kchunk * L + j0 + 512]:
                                       e.matmul(out=o, lhsT=l, rhs=r, start=True, stop=True),
                                       reads=[b_qT, b_kT], writes=[b_ps_sc])
                            S.emit("vector", lambda e, o=mx[:, half:half + 1], i=ps_sc[:]:
                                   e.reduce_max(out=o, in_=i, axis=AX),
                                   reads=[b_ps_sc], writes=[b_mx])
                        S.emit("vector", lambda e, o=mx[:, 2:3], a=mx[:, 0:1], b_=mx[:, 1:2]:
                               e.tensor_max(out=o, in0=a, in1=b_),
                               reads=[b_mx], writes=[b_mx])
                        S.emit("vector", lambda e, o=mx[:, 3:4], i=mx[:, 2:3]:
                               e.tensor_scalar_mul(out=o, in0=i, scalar1=-0.125),
                               reads=[b_mx], writes=[b_mx])
                        for half in range(2):
                            for nb in range(4):
                                j0 = half * 2048 + nb * 512
                                S.emit("tensor", lambda e, o=ps_sc[:, nb * 512:(nb + 1) * 512],
                                       l=qsl, r=kT[krow:krow + 64, kchunk * L + j0: kchunk * L + j0 + 512]:
                                       e.matmul(out=o, lhsT=l, rhs=r, start=True, stop=True),
                                       reads=[b_qT, b_kT], writes=[b_ps_sc])
                            for nb in range(4):
                                S.emit("scalar", lambda e, o=p_sb[:, nb * 512:(nb + 1) * 512],
                                       i=ps_sc[:, nb * 512:(nb + 1) * 512],
                                       bi=mx[:, 3:4]:
                                       e.activation(out=o, in_=i, func=AF.Exp, bias=bi,
                                                    scale=0.125),
                                       reads=[b_ps_sc, b_mx], writes=[b_p_sb])
                            for jc8 in range(2):
                                for j8 in range(8):
                                    jl = jc8 * 8 + j8
                                    S.emit("tensor", lambda e, o=ps_pt[:, j8 * 128:(j8 + 1) * 128],
                                           i=p_sb[:, jl * 128:(jl + 1) * 128], idn=sident[:]:
                                           e.transpose(out=o, in_=i, identity=idn),
                                           reads=[b_p_sb, b_ident], writes=[b_ps_pt])
                                ov = ptT[:].rearrange("p (c q) -> p c q", c=32)[
                                    :, half * 16 + jc8 * 8: half * 16 + (jc8 + 1) * 8,
                                    qi * 128:(qi + 1) * 128]
                                S.emit("vector", lambda e, o=ov,
                                       i=ps_pt[:].rearrange("p (c q) -> p c q", c=8):
                                       e.tensor_copy(out=o, in_=i),
                                       reads=[b_ps_pt], writes=[b_ptT])
                        for jc in range(32):
                            S.emit("tensor", lambda e, o=ps_o[:, 0:65],
                                   l=ptT[:, jc * 512 + qi * 128: jc * 512 + (qi + 1) * 128],
                                   r=v_h[ih][:, jc * 66: jc * 66 + 65],
                                   st=(jc == 0), sp=(jc == 31):
                                   e.matmul(out=o, lhsT=l, rhs=r, start=st, stop=sp),
                                   reads=[b_vh[ih], b_ptT], writes=[b_ps_o])
                        S.emit("vector", lambda e, o=mx[:, 5:6], i=ps_o[:, 64:65]:
                               e.reciprocal(out=o, in_=i),
                               reads=[b_ps_o], writes=[b_mx])
                        S.emit("vector", lambda e,
                               o=o_tok[:, qt * D + h * 64: qt * D + (h + 1) * 64],
                               i=ps_o[:, 0:64], sc=mx[:, 5:6]:
                               e.tensor_scalar_mul(out=o, in0=i, scalar1=sc),
                               reads=[b_ps_o, b_mx], writes=[b_o_tok])

        S.barrier()
        if debug:
            S.dma("sync", dbg["d_otok"], o_tok[:], reads=[b_o_tok], writes=[Buf("dbg5")])
            S.barrier()
        kT_g.__exit__(None, None, None)
        qT_g.__exit__(None, None, None)
        oT_g = nc.sbuf_tensor("oT", [128, KC * TO], BF16)
        oT = oT_g.__enter__()
        b_oT = Buf("oT")
        with ExitStack() as p3t:
            ps_tro = p3t.enter_context(nc.psum_tensor("ps_tro", [128, 1024], BF16))
            b_ps_tro = Buf("ps_tro")
            for tt in range(NT_O):
                for kc in range(KC):
                    S.emit("tensor", lambda e, o=ps_tro[:, kc * 128:(kc + 1) * 128],
                           i=o_tok[:, tt * D + kc * 128: tt * D + (kc + 1) * 128],
                           idn=sident[:]:
                           e.transpose(out=o, in_=i, identity=idn),
                           reads=[b_o_tok, b_ident], writes=[b_ps_tro])
                dview = oT[:].rearrange("p (c t) -> p c t", c=KC)[:, :, tt * 128:(tt + 1) * 128]
                S.emit("vector", lambda e, o=dview,
                       i=ps_tro[:].rearrange("p (c t) -> p c t", c=KC):
                       e.tensor_copy(out=o, in_=i),
                       reads=[b_ps_tro], writes=[b_oT])
        S.barrier()
        o_tok_g.__exit__(None, None, None)
        s_fin = es.enter_context(nc.sbuf_tensor("s_fin", [128, NT_O * D], BF16, side="right"))
        b_s_fin = Buf("s_fin")
        with ExitStack() as p35:
            sbl = lambda name, shape, dt=F32: p35.enter_context(nc.sbuf_tensor(name, shape, dt))
            psl = lambda name, shape, dt=F32: p35.enter_context(nc.psum_tensor(name, shape, dt))
            boutb = sbl("boutb", [128, D])
            b_boutb = Buf("boutb")
            brow = sbl("brow", [1, D], BF16)
            b_brow = Buf("brow")
            S.dma("sync", brow[:], wc_bout, reads=[b_wc["bout"]], writes=[b_brow])
            ps_bb = psl("ps_bb", [128, 512])
            b_ps_bb = Buf("ps_bb")
            bcast_row(S, nc, sones[:], brow, boutb, b_boutb, ps_bb, b_ps_bb, D,
                      reads=[b_brow, b_ones])
            wo_sb = sbl("wo_sb", [128, KC * D], BF16)
            b_wo = Buf("wo_sb")
            S.dma("sync", wo_sb[:].rearrange("p (c n) -> p c n", c=KC),
                  wc_wout.rearrange("(c p) n -> p c n", p=128),
                  reads=[b_wc["wout"]], writes=[b_wo])
            ps_op = [psl(f"ps_op{i}", [128, 512]) for i in range(2)]
            b_ps_op = [Buf(f"ps_op{i}") for i in range(2)]
            for tt in range(NT_O):
                for half in range(2):
                    for kc in range(KC):
                        S.emit("tensor", lambda e, o=ps_op[half][:, :],
                               l=oT[:, kc * TO + tt * 128: kc * TO + (tt + 1) * 128],
                               r=wo_sb[:, kc * D + half * 512: kc * D + half * 512 + 512],
                               st=(kc == 0), sp=(kc == KC - 1):
                               e.matmul(out=o, lhsT=l, rhs=r, start=st, stop=sp),
                               reads=[b_oT, b_wo], writes=[b_ps_op[half]])
                    S.emit("vector", lambda e,
                           o=s_fin[:, tt * D + half * 512: tt * D + half * 512 + 512],
                           i=ps_op[half][:, :],
                           bb=boutb[:, half * 512:half * 512 + 512]:
                           e.tensor_add(out=o, in0=i, in1=bb),
                           reads=[b_ps_op[half], b_boutb], writes=[b_s_fin])

        S.barrier()
        if debug:
            S.dma("sync", dbg["d_sfin"], s_fin[:], reads=[b_s_fin], writes=[Buf("dbg6")])
            S.barrier()
        oT_g.__exit__(None, None, None)
        with ExitStack() as p4:
            sbl = lambda name, shape, dt=F32: p4.enter_context(nc.sbuf_tensor(name, shape, dt))
            psl = lambda name, shape, dt=F32: p4.enter_context(nc.psum_tensor(name, shape, dt))
            wr_sb = sbl("wr_sb", [128, KC * 7], BF16)
            b_wr = Buf("wr_sb")
            S.dma("sync", wr_sb[:].rearrange("p (c n) -> p c n", c=KC),
                  wc_wr.rearrange("(c p) n -> p c n", p=128),
                  reads=[b_wc["wr"]], writes=[b_wr])
            rb_sb = sbl("rb_sb", [1, 7], BF16)
            b_rb = Buf("rb_sb")
            S.dma("sync", rb_sb[:], wc_rb, reads=[b_wc["rb"]], writes=[b_rb])
            wprob = sbl("wprob", [128, NT_O * 7])
            b_wprob = Buf("wprob")
            ps_r = psl("ps_r", [128, 7])
            b_ps_r = Buf("ps_r")
            rtmp = sbl("rtmp", [128, 16])
            b_rt = Buf("rtmp")
            rpr = sbl("rpr", [128, 7])
            b_rpr = Buf("rpr")
            for tt in range(NT_O):
                for kc in range(KC):
                    S.emit("tensor", lambda e, o=ps_r[:, :],
                           l=xnT_own[:, kc * TO + tt * 128: kc * TO + (tt + 1) * 128],
                           r=wr_sb[:, kc * 7:(kc + 1) * 7], st=(kc == 0):
                           e.matmul(out=o, lhsT=l, rhs=r, start=st, stop=False),
                           reads=[b_xnT_own, b_wr], writes=[b_ps_r])
                S.emit("tensor", lambda e, o=ps_r[:, :], l=sones[:], r=rb_sb[:]:
                       e.matmul(out=o, lhsT=l, rhs=r, start=False, stop=True),
                       reads=[b_rb, b_ones], writes=[b_ps_r])
                S.emit("vector", lambda e, o=rtmp[:, 0:1], i=ps_r[:, :]:
                       e.reduce_max(out=o, in_=i, axis=AX), reads=[b_ps_r], writes=[b_rt])
                S.emit("vector", lambda e, o=rtmp[:, 1:2], i=rtmp[:, 0:1]:
                       e.tensor_scalar_mul(out=o, in0=i, scalar1=-1.0),
                       reads=[b_rt], writes=[b_rt])
                S.emit("scalar", lambda e, o=rpr[:, :], i=ps_r[:, :], bi=rtmp[:, 1:2]:
                       e.activation(out=o, in_=i, func=AF.Exp, bias=bi),
                       reads=[b_ps_r, b_rt], writes=[b_rpr])
                S.emit("vector", lambda e, o=rtmp[:, 2:3], i=rpr[:, :]:
                       e.reduce_sum(out=o, in_=i, axis=AX),
                       reads=[b_rpr], writes=[b_rt])
                S.emit("vector", lambda e, o=rtmp[:, 3:4], i=rtmp[:, 2:3]:
                       e.reciprocal(out=o, in_=i), reads=[b_rt], writes=[b_rt])
                S.emit("vector", lambda e, o=rpr[:, :], i=rpr[:, :], sc=rtmp[:, 3:4]:
                       e.tensor_scalar_mul(out=o, in0=i, scalar1=sc),
                       reads=[b_rt, b_rpr], writes=[b_rpr])
                S.emit("vector", lambda e, o=rtmp[:, 4:5], i=rpr[:, :]:
                       e.reduce_max(out=o, in_=i, axis=AX), reads=[b_rpr], writes=[b_rt])
                S.emit("vector", lambda e, o=rtmp[:, 8:15], i=rpr[:, :], sc=rtmp[:, 4:5]:
                       e.tensor_single_scalar(out=o, in_=i, scalar=sc, op=ALU.is_ge),
                       reads=[b_rpr, b_rt], writes=[b_rt])
                S.emit("vector", lambda e, o=rtmp[:, 8:15], i=rtmp[:, 8:15], p=rpr[:, :]:
                       e.scalar_tensor_tensor(out=o, in0=i, scalar=-2.0, in1=p,
                                              op0=ALU.mult, op1=ALU.add),
                       reads=[b_rt, b_rpr], writes=[b_rt])
                S.emit("vector", lambda e, o=rtmp[:, 5:6], i=rtmp[:, 8:15]:
                       e.reduce_max(out=o, in_=i, axis=AX), reads=[b_rt], writes=[b_rt])
                S.emit("vector", lambda e, o=rtmp[:, 8:15], i=rpr[:, :], sc=rtmp[:, 5:6]:
                       e.tensor_single_scalar(out=o, in_=i, scalar=sc, op=ALU.is_ge),
                       reads=[b_rpr, b_rt], writes=[b_rt])
                S.emit("vector", lambda e, o=rtmp[:, 6:7], a=rtmp[:, 4:5], b_=rtmp[:, 5:6]:
                       e.tensor_add(out=o, in0=a, in1=b_), reads=[b_rt], writes=[b_rt])
                S.emit("vector", lambda e, o=rtmp[:, 7:8], i=rtmp[:, 6:7]:
                       e.reciprocal(out=o, in_=i), reads=[b_rt], writes=[b_rt])
                S.emit("vector", lambda e, o=rtmp[:, 8:15], i=rtmp[:, 8:15], p=rpr[:, :]:
                       e.tensor_mul(out=o, in0=i, in1=p), reads=[b_rt, b_rpr], writes=[b_rt])
                S.emit("vector", lambda e, o=wprob[:, tt * 7:(tt + 1) * 7], i=rtmp[:, 8:15],
                       sc=rtmp[:, 7:8]:
                       e.tensor_scalar_mul(out=o, in0=i, scalar1=sc),
                       reads=[b_rt], writes=[b_wprob])

            if debug:
                S.dma("sync", dbg["d_wprob"], wprob[:], reads=[b_wprob], writes=[Buf("dbg7")])
            NBLK = 4
            OC = I // 128
            wgu = [sbl(f"wgu{i}", [128, KC * 512], BF16) for i in range(4)]
            b_wgu = [Buf(f"wgu{i}") for i in range(4)]
            wd_sb = [sbl(f"wd_sb{i}", [128, OC * D // 2], BF16) for i in range(2)]
            b_wd = [Buf(f"wd_sb{i}") for i in range(2)]
            actT = sbl("actT", [128, OC * 512], BF16)
            b_act = Buf("actT")
            g_acc = sbl("g_acc", [128, 4 * D])
            b_gacc = Buf("g_acc")
            ps_g = psl("ps_g", [128, 512])
            ps_u = psl("ps_u", [128, 512])
            b_ps_g, b_ps_u = Buf("ps_g"), Buf("ps_u")
            ps_d = [psl(f"ps_d{i}", [128, 512]) for i in range(2)]
            b_ps_d = [Buf(f"ps_d{i}") for i in range(2)]
            silu_t = sbl("silu_t", [128, 512])
            b_silu = Buf("silu_t")
            xot = [sbl(f"xot{i}", [128, D]) for i in range(2)]
            b_xot = [Buf(f"xot{i}") for i in range(2)]
            gs_t = sbl("gs_t", [128, D])
            b_gs = Buf("gs_t")

            for tb in range(NBLK):
                for ei in range(NEXP):
                    if ei == 0:
                        gsrc, usrc, dsrc = wc_sg, wc_su, wc_sd
                        bg, bu, bd = b_wc["sg"], b_wc["su"], b_wc["sd"]
                    else:
                        gsrc, usrc, dsrc = wc_rg[ei - 1], wc_ru[ei - 1], wc_rd[ei - 1]
                        bg, bu, bd = b_wc["rg"], b_wc["ru"], b_wc["rd"]
                    for dh in range(2):
                        S.dma("sync",
                              wd_sb[dh][:].rearrange("p (c n) -> p c n", c=OC),
                              dsrc[:, dh * 512:(dh + 1) * 512].rearrange(
                                  "(c p) n -> p c n", p=128),
                              reads=[bd], writes=[b_wd[dh]])
                    for og in range((OC + 3) // 4):
                        noc = min(4, OC - og * 4)
                        iw = og % 2
                        S.dma("sync",
                              wgu[iw][:, 0:KC * noc * 128].rearrange(
                                  "p (c n) -> p c n", c=KC),
                              gsrc[:, og * 512: og * 512 + noc * 128].rearrange(
                                  "(c p) n -> p c n", p=128),
                              reads=[bg], writes=[b_wgu[iw]])
                        S.dma("sync",
                              wgu[2 + iw][:, 0:KC * noc * 128].rearrange(
                                  "p (c n) -> p c n", c=KC),
                              usrc[:, og * 512: og * 512 + noc * 128].rearrange(
                                  "(c p) n -> p c n", p=128),
                              reads=[bu], writes=[b_wgu[2 + iw]])
                        for oc_i in range(noc):
                            oc = og * 4 + oc_i
                            for kc in range(KC):
                                S.emit("tensor", lambda e, o=ps_g[:, :],
                                       l=wgu[iw][:, kc * noc * 128 + oc_i * 128:
                                                 kc * noc * 128 + (oc_i + 1) * 128],
                                       r=xnT_own[:, kc * TO + tb * 512: kc * TO + (tb + 1) * 512],
                                       st=(kc == 0), sp=(kc == KC - 1):
                                       e.matmul(out=o, lhsT=l, rhs=r, start=st, stop=sp),
                                       reads=[b_wgu[iw], b_xnT_own], writes=[b_ps_g])
                            for kc in range(KC):
                                S.emit("tensor", lambda e, o=ps_u[:, :],
                                       l=wgu[2 + iw][:, kc * noc * 128 + oc_i * 128:
                                                     kc * noc * 128 + (oc_i + 1) * 128],
                                       r=xnT_own[:, kc * TO + tb * 512: kc * TO + (tb + 1) * 512],
                                       st=(kc == 0), sp=(kc == KC - 1):
                                       e.matmul(out=o, lhsT=l, rhs=r, start=st, stop=sp),
                                       reads=[b_wgu[2 + iw], b_xnT_own], writes=[b_ps_u])
                            S.emit("scalar", lambda e, o=silu_t[:], i=ps_g[:, :]:
                                   e.activation(out=o, in_=i, func=AF.Silu),
                                   reads=[b_ps_g], writes=[b_silu])
                            S.emit("vector", lambda e, o=actT[:, oc * 512:(oc + 1) * 512],
                                   a=silu_t[:], b_=ps_u[:, :]:
                                   e.tensor_mul(out=o, in0=a, in1=b_),
                                   reads=[b_silu, b_ps_u], writes=[b_act])
                    for tt4 in range(4):
                        tt = tb * 4 + tt4
                        for dh in range(2):
                            for ic in range(OC):
                                S.emit("tensor", lambda e, o=ps_d[dh][:, :],
                                       l=actT[:, ic * 512 + tt4 * 128: ic * 512 + (tt4 + 1) * 128],
                                       r=wd_sb[dh][:, ic * 512:(ic + 1) * 512],
                                       st=(ic == 0), sp=(ic == OC - 1):
                                       e.matmul(out=o, lhsT=l, rhs=r, start=st, stop=sp),
                                       reads=[b_act, b_wd[dh]], writes=[b_ps_d[dh]])
                            gv = g_acc[:, tt4 * D + dh * 512: tt4 * D + (dh + 1) * 512]
                            if ei == 0:
                                S.emit("vector", lambda e, o=gv, i=ps_d[dh][:, :]:
                                       e.tensor_copy(out=o, in_=i),
                                       reads=[b_ps_d[dh]], writes=[b_gacc])
                            else:
                                S.emit("vector", lambda e, o=gv, i=ps_d[dh][:, :],
                                       sc=wprob[:, tt * 7 + ei - 1: tt * 7 + ei],
                                       g=gv:
                                       e.scalar_tensor_tensor(out=o, in0=i, scalar=sc,
                                                              in1=g, op0=ALU.mult,
                                                              op1=ALU.add),
                                       reads=[b_ps_d[dh], b_wprob, b_gacc], writes=[b_gacc])
                for tt4 in range(4):
                    tt = tb * 4 + tt4
                    i2 = tt4 % 2
                    S.dma("sync", xot[i2][:], x_own[tt * 128:(tt + 1) * 128, :],
                          writes=[b_xot[i2]])
                    S.emit("scalar", lambda e, o=gs_t[:], i=g_acc[:, tt4 * D:(tt4 + 1) * D]:
                           e.activation(out=o, in_=i, func=AF.Silu),
                           reads=[b_gacc], writes=[b_gs])
                    S.emit("vector", lambda e, o=gs_t[:], a=s_fin[:, tt * D:(tt + 1) * D],
                           b_=gs_t[:]:
                           e.tensor_mul(out=o, in0=a, in1=b_),
                           reads=[b_s_fin, b_gs], writes=[b_gs])
                    S.emit("vector", lambda e, o=xot[i2][:], a=xot[i2][:], b_=gs_t[:]:
                           e.tensor_add(out=o, in0=a, in1=b_),
                           reads=[b_gs, b_xot[i2]], writes=[b_xot[i2]])
                    S.dma("sync", xo[tt * 128:(tt + 1) * 128, :], xot[i2][:],
                          reads=[b_xot[i2]], writes=[Buf(f"xo_d{i2}")],
                          sem=f"dma_xo{i2}")

        S.barrier("sync")
        counts = S.build(nc, es)
    return nc






def build_l1(debug=False):
    nc = bass.Bass("TRN2", target_bir_lowering=False)
    DH = 4096
    SCALE = float(D) ** (-0.25)

    def din(name, shape, dt=F32):
        return nc.dram_tensor(name, shape, dt, kind="ExternalInput").ap()

    x_own = din("x_own", [TO, D])
    x_oth = din("x_oth", [TO, D])
    wq12 = din("wq12", [D, 2 * D])
    wk12 = din("wk12", [D, 2 * D])
    wv = din("wv", [D, D])
    lamqk = din("lamqk", [128, 32])
    w1T = din("w1T", [D, DH])
    w2T = din("w2T", [D, DH])
    h1T = din("h1T", [DH, DH])
    h2T = din("h2T", [DH, DH])
    w3T = din("w3T", [DH, D])
    s1c = din("s1c", [128, 32])
    s2c = din("s2c", [128, 32])
    h1c = din("h1c", [128, 32])
    h2c = din("h2c", [128, 32])
    s3row = din("s3row", [1, D])
    ident = din("ident", [128, 128], BF16)
    xo = nc.dram_tensor("xo", [TO, D], F32, kind="ExternalOutput").ap()

    def dint(name, shape, dt=BF16):
        return nc.dram_tensor(name, shape, dt, kind="Internal").ap()

    wc = {}
    for nm, srcw in [("q12", wq12), ("k12", wk12), ("v", wv), ("w1", w1T),
                     ("w2", w2T), ("h1", h1T), ("h2", h2T), ("w3", w3T)]:
        wc[nm] = dint("wc_" + nm, list(srcw.shape))
    q12_d = dint("q12_d", [2 * D, TO])
    k12_d = dint("k12_d", [2 * D, L])
    v_d = dint("v_d", [L, D])
    p1_d = dint("p1_d", [TO, L])
    xnT_d = dint("xnT_d", [128, KC * TO])
    h_d = dint("h_d", [DH, TO])
    h1_d = dint("h1_d", [DH, TO])
    s_d = dint("s_d", [TO, D])

    S = Sched()
    with ExitStack() as es:
        sb = lambda name, shape, dt=F32: es.enter_context(nc.sbuf_tensor(name, shape, dt))

        sident = sb("sident", [128, 128], BF16)
        sones = sb("sones", [1, 128], BF16)
        sonesf = sb("sonesf", [1, 128])
        sonescf = sb("sonescf", [128, 1])
        b_ident, b_ones, b_onesf, b_onescf = (Buf("ident"), Buf("ones"),
                                              Buf("onesf"), Buf("onescf"))
        S.dma("sync", sident[:], ident, writes=[b_ident])
        S.emit("vector", lambda e: e.memset(sones[:], 1.0), writes=[b_ones])
        S.emit("vector", lambda e: e.memset(sonesf[:], 1.0), writes=[b_onesf])
        S.emit("vector", lambda e: e.memset(sonescf[:], 1.0), writes=[b_onescf])
        setup_consts(S, nc, es, [1e-5, 1.0 / D, SCALE, 0.2])

        b_wc = {}
        for nm, srcw in [("q12", wq12), ("k12", wk12), ("v", wv), ("w1", w1T),
                         ("w2", w2T), ("h1", h1T), ("h2", h2T), ("w3", w3T)]:
            b = b_wc[nm] = Buf("wc_" + nm)
            cast_weights(S, nc, [(srcw, wc[nm], b)])

        xnT_own_g = nc.sbuf_tensor("xnT_own", [128, KC * TO], BF16, side="right")
        xnT_own = xnT_own_g.__enter__()
        b_xnT_own = Buf("xnT_own")
        xnT_oth_g = nc.sbuf_tensor("xnT_oth", [128, KC * TO], BF16, side="right")
        xnT_oth = xnT_oth_g.__enter__()
        b_xnT_oth = Buf("xnT_oth")
        emit_layernorm(S, nc, x_own, x_oth, xnT_own, b_xnT_own, xnT_oth, b_xnT_oth,
                       sident, b_ident)
        S.dma("sync", xnT_d, xnT_own[:], reads=[b_xnT_own], writes=[Buf("xnT_d")])

        lam_sb = sb("lam_sb", [128, 34])
        b_lam = Buf("lam_sb")
        negLamB = sb("negLamB", [128, 1])
        b_negLam = Buf("negLamB")
        with ExitStack() as pl:
            psl = lambda name, shape, dt=F32: pl.enter_context(nc.psum_tensor(name, shape, dt))
            ps_l = psl("ps_l", [128, 2])
            b_ps_l = Buf("ps_l")
            S.dma("sync", lam_sb[:, 0:32], lamqk, writes=[b_lam])
            tmp8 = sb("tmp8", [128, 16])
            b_t8 = Buf("tmp8")
            for pi in range(2):
                S.emit("vector", lambda e, o=tmp8[:, pi * 8:(pi + 1) * 8],
                       a=lam_sb[:, pi * 16:pi * 16 + 8],
                       b_=lam_sb[:, pi * 16 + 8:pi * 16 + 16]:
                       e.tensor_mul(out=o, in0=a, in1=b_),
                       reads=[b_lam], writes=[b_t8])
                S.emit("vector", lambda e, o=lam_sb[:, 32 + pi:33 + pi],
                       i=tmp8[:, pi * 8:(pi + 1) * 8]:
                       e.reduce_sum(out=o, in_=i, axis=AX),
                       reads=[b_t8], writes=[b_lam])
                S.emit("tensor", lambda e, o=ps_l[0:1, pi:pi + 1],
                       l=lam_sb[:, 32 + pi:33 + pi], r=sonescf[:]:
                       e.matmul(out=o, lhsT=l, rhs=r, start=True, stop=True),
                       reads=[b_lam, b_onescf], writes=[b_ps_l])
            S.emit("scalar", lambda e, o=lam_sb[0:1, 32:34], i=ps_l[0:1, 0:2]:
                   e.activation(out=o, in_=i, func=AF.Exp),
                   reads=[b_ps_l], writes=[b_lam])
            S.emit("vector", lambda e, o=lam_sb[0:1, 32:33], a=lam_sb[0:1, 32:33],
                   b_=lam_sb[0:1, 33:34]:
                   e.tensor_sub(out=o, in0=a, in1=b_),
                   reads=[b_lam], writes=[b_lam])
            S.emit("vector", lambda e, o=lam_sb[0:1, 32:33], i=lam_sb[0:1, 32:33]:
                   e.tensor_scalar(out=o, in0=i, scalar1=0.8, scalar2=-1.0,
                                   op0=ALU.add, op1=ALU.mult),
                   reads=[b_lam], writes=[b_lam])
            S.emit("tensor", lambda e, o=ps_l[:, 0:1], l=sonesf[:],
                   r=lam_sb[0:1, 32:33]:
                   e.matmul(out=o, lhsT=l, rhs=r, start=True, stop=True),
                   reads=[b_lam, b_onesf], writes=[b_ps_l])
            S.emit("vector", lambda e, o=negLamB[:], i=ps_l[:, 0:1]:
                   e.tensor_copy(out=o, in_=i),
                   reads=[b_ps_l], writes=[b_negLam])
            S.barrier()

        with ExitStack() as p2:
            sbl = lambda name, shape, dt=F32: p2.enter_context(nc.sbuf_tensor(name, shape, dt))
            psl = lambda name, shape, dt=F32: p2.enter_context(nc.psum_tensor(name, shape, dt))
            wcol = [sbl(f"wcol{i}", [128, KC * 128], BF16) for i in range(2)]
            b_wcol = [Buf(f"wcol{i}") for i in range(2)]
            ps_p = [psl(f"ps_p{i}", [128, 512]) for i in range(2)]
            b_ps_p = [Buf(f"ps_p{i}") for i in range(2)]
            stg = [sbl(f"stg{i}", [128, 512], BF16) for i in range(2)]
            b_stg = [Buf(f"stg{i}") for i in range(2)]

            for which, wcd, b_wcd, dst, ntb in [(0, wc["q12"], b_wc["q12"], q12_d, 4),
                                                (1, wc["k12"], b_wc["k12"], k12_d, 8)]:
                for ec in range(16):
                    iw = ec % 2
                    S.dma("sync",
                          wcol[iw][:].rearrange("p (c n) -> p c n", c=KC),
                          wcd[:, ec * 128:(ec + 1) * 128].rearrange(
                              "(c p) n -> p c n", p=128),
                          reads=[b_wcd], writes=[b_wcol[iw]])
                    for tb in range(ntb):
                        ip = tb % 2
                        for kc in range(KC):
                            if which == 0:
                                lhs, b_l = xnT_own, b_xnT_own
                                rr = lhs[:, kc * TO + tb * 512: kc * TO + (tb + 1) * 512]
                            else:
                                half_is_oth = tb >= 4
                                lhs = xnT_oth if half_is_oth else xnT_own
                                b_l = b_xnT_oth if half_is_oth else b_xnT_own
                                tbl = tb % 4
                                rr = lhs[:, kc * TO + tbl * 512: kc * TO + (tbl + 1) * 512]
                            S.emit("tensor", lambda e, o=ps_p[ip][:, :],
                                   ll=wcol[iw][:, kc * 128:(kc + 1) * 128], r=rr,
                                   st=(kc == 0), sp=(kc == KC - 1):
                                   e.matmul(out=o, lhsT=ll, rhs=r, start=st, stop=sp),
                                   reads=[b_wcol[iw], b_l], writes=[b_ps_p[ip]])
                        S.emit("vector", lambda e, o=stg[ip][:], i=ps_p[ip][:, :]:
                               e.tensor_copy(out=o, in_=i),
                               reads=[b_ps_p[ip]], writes=[b_stg[ip]])
                        S.dma("sync", dst[ec * 128:(ec + 1) * 128, tb * 512:(tb + 1) * 512],
                              stg[ip][:], reads=[b_stg[ip]],
                              writes=[Buf(f"pd{which}_{ip}")], sem=f"dma_pd{ip}")
            vcol = sbl("vcol", [128, KC * 512], BF16)
            b_vcol = Buf("vcol")
            for half in range(2):
                S.dma("sync",
                      vcol[:].rearrange("p (c n) -> p c n", c=KC),
                      wc["v"][:, half * 512:(half + 1) * 512].rearrange(
                          "(c p) n -> p c n", p=128),
                      reads=[b_wc["v"]], writes=[b_vcol])
                for tt in range(NT_A):
                    ip = tt % 2
                    lhs = xnT_own if tt < NT_O else xnT_oth
                    b_l = b_xnT_own if tt < NT_O else b_xnT_oth
                    ttl = tt % NT_O
                    for kc in range(KC):
                        S.emit("tensor", lambda e, o=ps_p[ip][:, :],
                               ll=lhs[:, kc * TO + ttl * 128: kc * TO + (ttl + 1) * 128],
                               r=vcol[:, kc * 512:(kc + 1) * 512],
                               st=(kc == 0), sp=(kc == KC - 1):
                               e.matmul(out=o, lhsT=ll, rhs=r, start=st, stop=sp),
                               reads=[b_vcol, b_l], writes=[b_ps_p[ip]])
                    S.emit("vector", lambda e, o=stg[ip][:], i=ps_p[ip][:, :]:
                           e.tensor_copy(out=o, in_=i),
                           reads=[b_ps_p[ip]], writes=[b_stg[ip]])
                    S.dma("sync", v_d[tt * 128:(tt + 1) * 128, half * 512:(half + 1) * 512],
                          stg[ip][:], reads=[b_stg[ip]],
                          writes=[Buf(f"vd_{ip}")], sem=f"dma_vd{ip}")

        S.barrier()
        xnT_oth_g.__exit__(None, None, None)
        xnT_own_g.__exit__(None, None, None)

        rinv_all = sb("rinv_all", [128, 32])
        b_rinv = Buf("rinv_all")
        kT_g = nc.sbuf_tensor("kT", [128, KC * L], BF16)
        kT = kT_g.__enter__()
        b_kT = Buf("kT")
        with ExitStack() as p3:
            sbl = lambda name, shape, dt=F32: p3.enter_context(nc.sbuf_tensor(name, shape, dt))
            psl = lambda name, shape, dt=F32: p3.enter_context(nc.psum_tensor(name, shape, dt))
            qtl = [sbl(f"qtl{i}", [128, KC * 128], BF16) for i in range(2)]
            b_qtl = [Buf(f"qtl{i}") for i in range(2)]
            ps_s = [psl(f"ps_s{i}", [128, 512]) for i in range(2)]
            b_ps_s = [Buf(f"ps_s{i}") for i in range(2)]
            sc = sbl("sc", [128, L])
            b_sc = Buf("sc")
            pbf = sbl("pbf", [128, L], BF16)
            b_pbf = Buf("pbf")
            mx = sbl("mx", [128, 4])
            b_mx = Buf("mx")

            S.dma("sync", kT[:].rearrange("p (c n) -> p c n", c=KC),
                  k12_d[0:D, :].rearrange("(c p) n -> p c n", p=128),
                  writes=[b_kT])

            def attn_scores(qslot, b_qslot, qt, which):
                for nb in range(8):
                    ip = nb % 2
                    for kc in range(KC):
                        S.emit("tensor", lambda e, o=ps_s[ip][:, :],
                               ll=qslot[:, kc * 128:(kc + 1) * 128],
                               r=kT[:, kc * L + nb * 512: kc * L + (nb + 1) * 512],
                               st=(kc == 0), sp=(kc == KC - 1):
                               e.matmul(out=o, lhsT=ll, rhs=r, start=st, stop=sp),
                               reads=[b_qslot, b_kT], writes=[b_ps_s[ip]])
                    S.emit("vector", lambda e, o=sc[:, nb * 512:(nb + 1) * 512],
                           i=ps_s[ip][:, :]:
                           e.tensor_copy(out=o, in_=i),
                           reads=[b_ps_s[ip]], writes=[b_sc])
                S.emit("vector", lambda e, o=mx[:, 0:1], i=sc[:]:
                       e.reduce_max(out=o, in_=i, axis=AX),
                       reads=[b_sc], writes=[b_mx])
                S.emit("vector", lambda e, o=mx[:, 1:2], i=mx[:, 0:1]:
                       e.tensor_scalar_mul(out=o, in0=i, scalar1=-SCALE),
                       reads=[b_mx], writes=[b_mx])
                for nb in range(8):
                    S.emit("scalar", lambda e, o=pbf[:, nb * 512:(nb + 1) * 512],
                           i=sc[:, nb * 512:(nb + 1) * 512], bi=mx[:, 1:2]:
                           e.activation(out=o, in_=i, func=AF.Exp, bias=bi, scale=SCALE),
                           reads=[b_sc, b_mx], writes=[b_pbf])
                S.emit("vector", lambda e, o=mx[:, 2:3], i=pbf[:]:
                       e.reduce_sum(out=o, in_=i, axis=AX),
                       reads=[b_pbf], writes=[b_mx])
                S.emit("vector", lambda e, o=rinv_all[:, which * 16 + qt: which * 16 + qt + 1],
                       i=mx[:, 2:3]:
                       e.reciprocal(out=o, in_=i),
                       reads=[b_mx], writes=[b_rinv])

            for qt in range(NT_O):
                iq = qt % 2
                S.dma("sync", qtl[iq][:].rearrange("p (c n) -> p c n", c=KC),
                      q12_d[0:D, qt * 128:(qt + 1) * 128].rearrange(
                          "(c p) n -> p c n", p=128),
                      writes=[b_qtl[iq]])
                attn_scores(qtl[iq], b_qtl[iq], qt, 0)
                S.dma("sync", p1_d[qt * 128:(qt + 1) * 128, :], pbf[:],
                      reads=[b_pbf], writes=[Buf("p1d")], sem="dma_p1d")

            S.barrier()
            S.dma("sync", kT[:].rearrange("p (c n) -> p c n", c=KC),
                  k12_d[D:2 * D, :].rearrange("(c p) n -> p c n", p=128),
                  writes=[b_kT])
            v_sb_g = nc.sbuf_tensor("v_sb", [128, 32 * 1024], BF16, side="right")
            v_sb = v_sb_g.__enter__()
            b_vsb = Buf("v_sb")
            S.dma("sync", v_sb[:].rearrange("p (c n) -> p c n", c=32),
                  v_d.rearrange("(c p) n -> p c n", p=128),
                  writes=[b_vsb])
            s_tile = sbl("s_tile", [128, D], BF16)
            b_s_tile = Buf("s_tile")
            p1re = sbl("p1re", [128, L], BF16)
            b_p1re = Buf("p1re")
            diffb = sbl("diffb", [128, L], BF16)
            b_diffb = Buf("diffb")
            ps_pt = psl("ps_pt", [128, 1024], BF16)
            b_ps_pt = Buf("ps_pt")
            diffT = sbl("diffT", [128, 32 * 128], BF16)
            b_diffT = Buf("diffT")
            o_f = sbl("o_f", [128, D])
            b_o_f = Buf("o_f")
            sq_f = sbl("sq_f", [128, D])
            b_sq_f = Buf("sq_f")

            for qt in range(NT_O):
                iq = qt % 2
                S.dma("sync", qtl[iq][:].rearrange("p (c n) -> p c n", c=KC),
                      q12_d[D:2 * D, qt * 128:(qt + 1) * 128].rearrange(
                          "(c p) n -> p c n", p=128),
                      writes=[b_qtl[iq]])
                attn_scores(qtl[iq], b_qtl[iq], qt, 1)
                S.dma("sync", p1re[:], p1_d[qt * 128:(qt + 1) * 128, :],
                      writes=[b_p1re])
                S.emit("vector", lambda e, o=mx[:, 3:4], a=rinv_all[:, 16 + qt:17 + qt],
                       b_=negLamB[:]:
                       e.tensor_mul(out=o, in0=a, in1=b_),
                       reads=[b_rinv, b_negLam], writes=[b_mx])
                S.emit("vector", lambda e, o=diffb[:], i=p1re[:],
                       sc_=rinv_all[:, qt:qt + 1]:
                       e.tensor_scalar_mul(out=o, in0=i, scalar1=sc_),
                       reads=[b_p1re, b_rinv], writes=[b_diffb])
                S.emit("vector", lambda e, o=diffb[:], i=pbf[:], sc_=mx[:, 3:4],
                       g=diffb[:]:
                       e.scalar_tensor_tensor(out=o, in0=i, scalar=sc_, in1=g,
                                              op0=ALU.mult, op1=ALU.add),
                       reads=[b_pbf, b_mx, b_diffb], writes=[b_diffb])
                for g8 in range(4):
                    for j8 in range(8):
                        jc = g8 * 8 + j8
                        S.emit("tensor", lambda e, o=ps_pt[:, j8 * 128:(j8 + 1) * 128],
                               i=diffb[:, jc * 128:(jc + 1) * 128], idn=sident[:]:
                               e.transpose(out=o, in_=i, identity=idn),
                               reads=[b_diffb, b_ident], writes=[b_ps_pt])
                    S.emit("vector", lambda e,
                           o=diffT[:, g8 * 1024:(g8 + 1) * 1024],
                           i=ps_pt[:]:
                           e.tensor_copy(out=o, in_=i),
                           reads=[b_ps_pt], writes=[b_diffT])
                for half in range(2):
                    ip = half
                    for jc in range(32):
                        S.emit("tensor", lambda e, o=ps_s[ip][:, :],
                               ll=diffT[:, jc * 128:(jc + 1) * 128],
                               r=v_sb[:, jc * 1024 + half * 512: jc * 1024 + half * 512 + 512],
                               st=(jc == 0), sp=(jc == 31):
                               e.matmul(out=o, lhsT=ll, rhs=r, start=st, stop=sp),
                               reads=[b_diffT, b_vsb], writes=[b_ps_s[ip]])
                    S.emit("vector", lambda e, o=o_f[:, half * 512:(half + 1) * 512],
                           i=ps_s[ip][:, :]:
                           e.tensor_copy(out=o, in_=i),
                           reads=[b_ps_s[ip]], writes=[b_o_f])
                S.emit("scalar", lambda e, o=sq_f[:], i=o_f[:]:
                       e.activation(out=o, in_=i, func=AF.Square),
                       reads=[b_o_f], writes=[b_sq_f])
                S.emit("vector", lambda e, o=mx[:, 0:1], i=sq_f[:]:
                       e.reduce_sum(out=o, in_=i, axis=AX),
                       reads=[b_sq_f], writes=[b_mx])
                S.emit("scalar", lambda e, o=mx[:, 1:2], i=mx[:, 0:1]:
                       e.activation(out=o, in_=i, func=AF.Sqrt, bias=1e-5, scale=1.0 / D),
                       reads=[b_mx], writes=[b_mx])
                S.emit("vector", lambda e, o=mx[:, 2:3], i=mx[:, 1:2]:
                       e.reciprocal(out=o, in_=i),
                       reads=[b_mx], writes=[b_mx])
                S.emit("vector", lambda e, o=mx[:, 3:4], i=mx[:, 2:3]:
                       e.tensor_scalar_mul(out=o, in0=i, scalar1=0.2),
                       reads=[b_mx], writes=[b_mx])
                S.emit("vector", lambda e, o=s_tile[:], i=o_f[:],
                       sc_=mx[:, 3:4]:
                       e.tensor_scalar_mul(out=o, in0=i, scalar1=sc_),
                       reads=[b_o_f, b_mx], writes=[b_s_tile])
                S.dma("sync", s_d[qt * 128:(qt + 1) * 128, :], s_tile[:],
                      reads=[b_s_tile], writes=[Buf("s_d_st")], sem="dma_sd_st")
            S.barrier()
            v_sb_g.__exit__(None, None, None)
        S.barrier()
        kT_g.__exit__(None, None, None)

        with ExitStack() as p4:
            sbl = lambda name, shape, dt=F32: p4.enter_context(nc.sbuf_tensor(name, shape, dt))
            psl = lambda name, shape, dt=F32: p4.enter_context(nc.psum_tensor(name, shape, dt))
            xnT2 = sbl("xnT2", [128, KC * TO], BF16)
            b_xnT2 = Buf("xnT2")
            S.dma("sync", xnT2[:], xnT_d, writes=[b_xnT2])
            scl = sbl("scl", [128, 128])
            b_scl = Buf("scl")
            S.dma("sync", scl[:, 0:32], s1c, writes=[b_scl])
            S.dma("sync", scl[:, 32:64], s2c, writes=[b_scl])
            S.dma("sync", scl[:, 64:96], h1c, writes=[b_scl])
            S.dma("sync", scl[:, 96:128], h2c, writes=[b_scl])
            ps_g = psl("ps_g", [128, 512])
            ps_u = psl("ps_u", [128, 512])
            b_ps_g, b_ps_u = Buf("ps_g"), Buf("ps_u")
            silu_t = sbl("silu_t", [128, 512])
            b_silu = Buf("silu_t")
            stg2 = [sbl(f"stg2{i}", [128, 512], BF16) for i in range(2)]
            b_stg2 = [Buf(f"stg2{i}") for i in range(2)]
            wcolA = [sbl(f"wcolA{i}", [128, KC * 512], BF16) for i in range(2)]
            b_wcolA = [Buf(f"wcolA{i}") for i in range(2)]
            wcolB = [sbl(f"wcolB{i}", [128, KC * 512], BF16) for i in range(2)]
            b_wcolB = [Buf(f"wcolB{i}") for i in range(2)]

            for tb in range(4):
                for og in range(8):
                    iw = og % 2
                    S.dma("sync", wcolA[iw][:].rearrange("p (c n) -> p c n", c=KC),
                          wc["w1"][:, og * 512:(og + 1) * 512].rearrange(
                              "(c p) n -> p c n", p=128),
                          reads=[b_wc["w1"]], writes=[b_wcolA[iw]])
                    S.dma("sync", wcolB[iw][:].rearrange("p (c n) -> p c n", c=KC),
                          wc["w2"][:, og * 512:(og + 1) * 512].rearrange(
                              "(c p) n -> p c n", p=128),
                          reads=[b_wc["w2"]], writes=[b_wcolB[iw]])
                    for o4 in range(4):
                        oc = og * 4 + o4
                        for kc in range(KC):
                            S.emit("tensor", lambda e, o=ps_g[:, :],
                                   ll=wcolA[iw][:, kc * 512 + o4 * 128: kc * 512 + (o4 + 1) * 128],
                                   r=xnT2[:, kc * TO + tb * 512: kc * TO + (tb + 1) * 512],
                                   st=(kc == 0), sp=(kc == KC - 1):
                                   e.matmul(out=o, lhsT=ll, rhs=r, start=st, stop=sp),
                                   reads=[b_wcolA[iw], b_xnT2], writes=[b_ps_g])
                        for kc in range(KC):
                            S.emit("tensor", lambda e, o=ps_u[:, :],
                                   ll=wcolB[iw][:, kc * 512 + o4 * 128: kc * 512 + (o4 + 1) * 128],
                                   r=xnT2[:, kc * TO + tb * 512: kc * TO + (tb + 1) * 512],
                                   st=(kc == 0), sp=(kc == KC - 1):
                                   e.matmul(out=o, lhsT=ll, rhs=r, start=st, stop=sp),
                                   reads=[b_wcolB[iw], b_xnT2], writes=[b_ps_u])
                        S.emit("scalar", lambda e, o=silu_t[:], i=ps_g[:, :],
                               sc_=scl[:, oc:oc + 1]:
                               e.activation(out=o, in_=i, func=AF.Silu, scale=sc_),
                               reads=[b_ps_g, b_scl], writes=[b_silu])
                        ip = o4 % 2
                        S.emit("vector", lambda e, o=stg2[ip][:], i=ps_u[:, :],
                               sc_=scl[:, 32 + oc:33 + oc], g=silu_t[:]:
                               e.scalar_tensor_tensor(out=o, in0=i, scalar=sc_, in1=g,
                                                      op0=ALU.mult, op1=ALU.mult),
                               reads=[b_ps_u, b_scl, b_silu], writes=[b_stg2[ip]])
                        S.dma("sync", h_d[oc * 128:(oc + 1) * 128, tb * 512:(tb + 1) * 512],
                              stg2[ip][:], reads=[b_stg2[ip]],
                              writes=[Buf(f"hd{ip}")], sem=f"dma_hd{ip}")

            hin = sbl("hin", [128, 32 * 512], BF16)
            b_hin = Buf("hin")
            wcol32 = [sbl(f"wcol32{i}", [128, 32 * 128], BF16) for i in range(2)]
            b_wcol32 = [Buf(f"wcol32{i}") for i in range(2)]
            for stage, wsrc, b_ws, scoff, src_d, dst_d in [
                    (0, wc["h1"], b_wc["h1"], 64, h_d, h1_d),
                    (1, wc["h2"], b_wc["h2"], 96, h1_d, h_d)]:
                S.barrier()
                for tb in range(4):
                    S.dma("sync", hin[:].rearrange("p (c n) -> p c n", c=32),
                          src_d[:, tb * 512:(tb + 1) * 512].rearrange(
                              "(c p) n -> p c n", p=128),
                          writes=[b_hin])
                    for oc in range(32):
                        iw = oc % 2
                        S.dma("sync", wcol32[iw][:].rearrange("p (c n) -> p c n", c=32),
                              wsrc[:, oc * 128:(oc + 1) * 128].rearrange(
                                  "(c p) n -> p c n", p=128),
                              reads=[b_ws], writes=[b_wcol32[iw]])
                        ip = oc % 2
                        for kc in range(32):
                            S.emit("tensor", lambda e, o=ps_g[:, :],
                                   ll=wcol32[iw][:, kc * 128:(kc + 1) * 128],
                                   r=hin[:, kc * 512:(kc + 1) * 512],
                                   st=(kc == 0), sp=(kc == 31):
                                   e.matmul(out=o, lhsT=ll, rhs=r, start=st, stop=sp),
                                   reads=[b_wcol32[iw], b_hin], writes=[b_ps_g])
                        S.emit("scalar", lambda e, o=stg2[ip][:], i=ps_g[:, :],
                               sc_=scl[:, scoff + oc:scoff + oc + 1]:
                               e.activation(out=o, in_=i, func=AF.Gelu, scale=sc_),
                               reads=[b_ps_g, b_scl], writes=[b_stg2[ip]])
                        S.dma("sync", dst_d[oc * 128:(oc + 1) * 128, tb * 512:(tb + 1) * 512],
                              stg2[ip][:], reads=[b_stg2[ip]],
                              writes=[Buf(f"sd{stage}_{ip}")], sem=f"dma_sd{ip}")

            S.barrier()
            w3sb = sbl("w3sb", [128, 32 * D], BF16)
            b_w3sb = Buf("w3sb")
            S.dma("sync", w3sb[:].rearrange("p (c n) -> p c n", c=32),
                  wc["w3"].rearrange("(c p) n -> p c n", p=128),
                  reads=[b_wc["w3"]], writes=[b_w3sb])
            s3b = sbl("s3b", [128, D])
            b_s3b = Buf("s3b")
            s3bf = sbl("s3bf", [1, D], BF16)
            b_s3bf = Buf("s3bf")
            S.dma("gpsimd", s3bf[:], s3row, writes=[b_s3bf])
            ps_bb = psl("ps_bb", [128, 512])
            b_ps_bb = Buf("ps_bb")
            bcast_row(S, nc, sones[:], s3bf, s3b, b_s3b, ps_bb, b_ps_bb, D,
                      reads=[b_s3bf, b_ones])
            gr_f = sbl("gr_f", [128, D])
            b_gr = Buf("gr_f")
            gs_t = sbl("gs_t", [128, D])
            b_gs = Buf("gs_t")
            xot = [sbl(f"xot{i}", [128, D]) for i in range(2)]
            b_xot = [Buf(f"xot{i}") for i in range(2)]
            _sre = sbl("sre0", [128, D], BF16); _bsre = Buf("sre0")
            sre, b_sre = [_sre, _sre], [_bsre, _bsre]
            ps_d = [psl(f"ps_d{i}", [128, 512]) for i in range(2)]
            b_ps_d = [Buf(f"ps_d{i}") for i in range(2)]
            for tb in range(4):
                S.dma("sync", hin[:].rearrange("p (c n) -> p c n", c=32),
                      h_d[:, tb * 512:(tb + 1) * 512].rearrange(
                          "(c p) n -> p c n", p=128),
                      writes=[b_hin])
                for t4 in range(4):
                    tt = tb * 4 + t4
                    i2 = t4 % 2
                    for half in range(2):
                        for kc in range(32):
                            S.emit("tensor", lambda e, o=ps_d[half][:, :],
                                   ll=hin[:, kc * 512 + t4 * 128: kc * 512 + (t4 + 1) * 128],
                                   r=w3sb[:, kc * D + half * 512: kc * D + half * 512 + 512],
                                   st=(kc == 0), sp=(kc == 31):
                                   e.matmul(out=o, lhsT=ll, rhs=r, start=st, stop=sp),
                                   reads=[b_hin, b_w3sb], writes=[b_ps_d[half]])
                        S.emit("vector", lambda e, o=gr_f[:, half * 512:(half + 1) * 512],
                               i=ps_d[half][:, :], g=s3b[:, half * 512:(half + 1) * 512]:
                               e.tensor_mul(out=o, in0=i, in1=g),
                               reads=[b_ps_d[half], b_s3b], writes=[b_gr])
                    S.dma("sync", xot[i2][:], x_own[tt * 128:(tt + 1) * 128, :],
                          writes=[b_xot[i2]])
                    S.emit("scalar", lambda e, o=gs_t[:], i=gr_f[:]:
                           e.activation(out=o, in_=i, func=AF.Silu),
                           reads=[b_gr], writes=[b_gs])
                    S.dma("sync", sre[i2][:], s_d[tt * 128:(tt + 1) * 128, :],
                          writes=[b_sre[i2]])
                    S.emit("vector", lambda e, o=gs_t[:], a=sre[i2][:],
                           b_=gs_t[:]:
                           e.tensor_mul(out=o, in0=a, in1=b_),
                           reads=[b_sre[i2], b_gs], writes=[b_gs])
                    S.emit("vector", lambda e, o=xot[i2][:], a=xot[i2][:], b_=gs_t[:]:
                           e.tensor_add(out=o, in0=a, in1=b_),
                           reads=[b_gs, b_xot[i2]], writes=[b_xot[i2]])
                    S.dma("sync", xo[tt * 128:(tt + 1) * 128, :], xot[i2][:],
                          reads=[b_xot[i2]], writes=[Buf(f"xod{i2}")],
                          sem=f"dma_xo{i2}")
        S.barrier("sync")
        S.build(nc, es)
    return nc


def _l1_in_maps(inputs):
    wq12 = np.ascontiguousarray(np.asarray(inputs["L1_attn_w_q12"]))
    wk12 = np.ascontiguousarray(np.asarray(inputs["L1_attn_w_k12"]))
    wv = np.ascontiguousarray(np.asarray(inputs["L1_attn_w_v"]))
    lam = np.stack([np.asarray(inputs["L1_attn_lambda_q1"]).reshape(8, 128),
                    np.asarray(inputs["L1_attn_lambda_k1"]).reshape(8, 128),
                    np.asarray(inputs["L1_attn_lambda_q2"]).reshape(8, 128),
                    np.asarray(inputs["L1_attn_lambda_k2"]).reshape(8, 128)])
    lamqk = np.ascontiguousarray(lam.transpose(2, 0, 1).reshape(128, 32))
    w1T = np.ascontiguousarray(np.asarray(inputs["L1_ffn_w1_w"]).T)
    w2T = np.ascontiguousarray(np.asarray(inputs["L1_ffn_w2_w"]).T)
    h1T = np.ascontiguousarray(np.asarray(inputs["L1_ffn_h1_w"]).T)
    h2T = np.ascontiguousarray(np.asarray(inputs["L1_ffn_h2_w"]).T)
    w3T = np.ascontiguousarray(np.asarray(inputs["L1_ffn_w3_w"]).T)
    col = lambda s: np.ascontiguousarray(np.asarray(s).reshape(32, 128).T)
    s1c, s2c = col(inputs["L1_ffn_w1_s"]), col(inputs["L1_ffn_w2_s"])
    h1c, h2c = col(inputs["L1_ffn_h1_s"]), col(inputs["L1_ffn_h2_s"])
    s3row = np.asarray(inputs["L1_ffn_w3_s"]).reshape(1, D)
    ident = np.eye(128, dtype=np.float32).astype(ml_dtypes.bfloat16)
    shared = dict(wq12=wq12, wk12=wk12, wv=wv, lamqk=lamqk, w1T=w1T, w2T=w2T,
                  h1T=h1T, h2T=h2T, w3T=w3T, s1c=s1c, s2c=s2c, h1c=h1c,
                  h2c=h2c, s3row=s3row, ident=ident)
    return shared


def _mk_layer_maps(shared, x):
    maps = []
    for c in range(8):
        b, half = c // 2, c % 2
        t0 = half * TO
        m = dict(shared)
        m["x_own"] = np.ascontiguousarray(x[b, t0:t0 + TO])
        m["x_oth"] = np.ascontiguousarray(x[b, TO - t0:2 * TO - t0])
        maps.append(m)
    return maps





def build_l2(debug=False):
    nc = bass.Bass("TRN2", target_bir_lowering=False)
    DS = 16
    ECH = 8

    def din(name, shape, dt=F32):
        return nc.dram_tensor(name, shape, dt, kind="ExternalInput").ap()

    x_own = din("x_own", [TO, D])
    x_oth = din("x_oth", [TO, D])
    x_nat = din("x_nat", [L, D])
    g_row = din("g_row", [1, D])
    winx = din("winx", [D, D])
    winz = din("winz", [D, D])
    convw = din("convw", [128, 32])
    convb = din("convb", [128, 8])
    wxT = din("wxT", [D, 33])
    wdt = din("wdt", [128, 8])
    bdt = din("bdt", [128, 8])
    a_sl = din("a_sl", [128, 128])
    d_sl = din("d_sl", [128, 8])
    woutT = din("woutT", [D, D])
    wg = din("wg", [D, 8])
    wexp = din("wexp", [8, D, D])
    brows = din("brows", [1, 8 * D])
    sel8 = din("sel8", [8, 1])
    idxme = din("idxme", [128, 16], I32)
    idxpr = din("idxpr", [128, 16], I32)
    idxpar = din("idxpar", [33, 1], I32)
    ident = din("ident", [128, 128], BF16)
    xo = nc.dram_tensor("xo", [TO, D], F32, kind="ExternalOutput").ap()

    def dint(name, shape, dt=BF16, **kw):
        return nc.dram_tensor(name, shape, dt, kind="Internal", **kw).ap()

    wc = {}
    for nm, srcw in [("winx", winx), ("winz", winz), ("wxT", wxT),
                     ("woutT", woutT), ("wg", wg), ("brows", brows)]:
        wc[nm] = dint("wc_" + nm, list(srcw.shape))
    wc["wexp"] = dint("wc_wexp", [8, D, D])
    z_d = dint("z_d", [D, L])
    xc_d = dint("xc_d", [D, L])
    xn_d = dint("xn_d", [TO, D])
    xnT_d = dint("xnT_d", [128, KC * TO])
    cc_p_in = dint("cc_p_in", [33, L], F32)
    cc_p_out = dint("cc_p_out", [8 * 33, L], F32)
    cc_s_in = dint("cc_s_in", [L, D], BF16)
    cc_s_out = dint("cc_s_out", [8 * L, D], BF16)
    cnt_d = dint("cnt_d", [8, 1], F32)
    off_d = dint("off_d", [1, 8], F32)
    cc_c_in = dint("cc_c_in", [1, 8], F32)
    cc_c_out = dint("cc_c_out", [8, 8], F32)

    S = Sched()
    with ExitStack() as es:
        sb = lambda name, shape, dt=F32: es.enter_context(nc.sbuf_tensor(name, shape, dt))
        sident = sb("sident", [128, 128], BF16)
        sidentf = sb("sidentf", [128, 128])
        sones = sb("sones", [1, 128], BF16)
        sonesf = sb("sonesf", [1, 128])
        b_ident, b_ones, b_onesf = Buf("ident"), Buf("ones"), Buf("onesf")
        b_identf = Buf("identf")
        S.dma("sync", sident[:], ident, writes=[b_ident])
        S.emit("vector", lambda e: e.tensor_copy(out=sidentf[:], in_=sident[:]),
               reads=[b_ident], writes=[b_identf])
        S.emit("vector", lambda e: e.memset(sones[:], 1.0), writes=[b_ones])
        S.emit("vector", lambda e: e.memset(sonesf[:], 1.0), writes=[b_onesf])
        setup_consts(S, nc, es, [1e-5, 1e-6, 1.0 / D, 1.0, -1.0])

        b_wc = {}
        for nm, srcw in [("winx", winx), ("winz", winz), ("wxT", wxT),
                         ("woutT", woutT), ("wg", wg), ("brows", brows)]:
            b = b_wc[nm] = Buf("wc_" + nm)
            cast_weights(S, nc, [(srcw, wc[nm], b)])
        b_wc["wexp"] = Buf("wc_wexp")
        for ei in range(8):
            cast_weights(S, nc, [(wexp[ei], wc["wexp"][ei], b_wc["wexp"])])

        xnT_own_g = nc.sbuf_tensor("xnT_own", [128, KC * TO], BF16, side="right")
        xnT_own = xnT_own_g.__enter__()
        b_xnT_own = Buf("xnT_own")
        xnT_oth_g = nc.sbuf_tensor("xnT_oth", [128, KC * TO], BF16, side="right")
        xnT_oth = xnT_oth_g.__enter__()
        b_xnT_oth = Buf("xnT_oth")
        emit_layernorm(S, nc, x_own, x_oth, xnT_own, b_xnT_own, xnT_oth, b_xnT_oth,
                       sident, b_ident, xn_d=xn_d)
        S.dma("sync", xnT_d, xnT_own[:], reads=[b_xnT_own], writes=[Buf("xnT_d")])
        S.barrier()
        xnT_oth_g.__exit__(None, None, None)
        xnT_own_g.__exit__(None, None, None)

        n2T_g = nc.sbuf_tensor("n2T", [128, KC * L], BF16, side="right")
        n2T = n2T_g.__enter__()
        b_n2T = Buf("n2T")
        with ExitStack() as p1:
            sbl = lambda name, shape, dt=F32: p1.enter_context(nc.sbuf_tensor(name, shape, dt))
            psl = lambda name, shape, dt=F32: p1.enter_context(nc.psum_tensor(name, shape, dt))
            gb = sbl("gb", [128, D])
            b_gb = Buf("gb")
            grw = sbl("grw", [1, D], BF16)
            b_grw = Buf("grw")
            S.dma("gpsimd", grw[:], g_row, writes=[b_grw])
            ps_b = psl("ps_b", [128, 512])
            b_ps_b = Buf("ps_b")
            bcast_row(S, nc, sones[:], grw, gb, b_gb, ps_b, b_ps_b, D,
                      reads=[b_grw, b_ones])
            xin = [sbl(f"n_xin{i}", [128, D]) for i in range(3)]
            b_xin = [Buf(f"n_xin{i}") for i in range(3)]
            sqs = [sbl(f"n_sq{i}", [128, D]) for i in range(2)]
            b_sq = [Buf(f"n_sq{i}") for i in range(2)]
            xnf = [sbl(f"n_xnf{i}", [128, D]) for i in range(2)]
            b_xnf = [Buf(f"n_xnf{i}") for i in range(2)]
            n2b = [sbl(f"n_n2b{i}", [128, D], BF16) for i in range(2)]
            b_n2b = [Buf(f"n_n2b{i}") for i in range(2)]
            stat = [sbl(f"n_stat{i}", [128, 8]) for i in range(2)]
            b_stat = [Buf(f"n_stat{i}") for i in range(2)]
            ptr = [psl(f"n_ptr{i}", [128, 1024], BF16) for i in range(2)]
            b_ptr = [Buf(f"n_ptr{i}") for i in range(2)]
            for tt in range(-2, NT_A):
                warm = tt < 0
                ttw = max(tt, 0)
                i3, i2 = ttw % 3, (tt + 2) % 2 if warm else ttw % 2
                tt = ttw
                st, b_st = stat[i2], b_stat[i2]
                S.dma("sync", xin[i3][:], x_nat[tt * 128:(tt + 1) * 128, :],
                      writes=[b_xin[i3]])
                S.emit("vector", lambda e, o=st[:, 0:1], i=xin[i3][:]:
                       e.reduce_sum(out=o, in_=i, axis=AX),
                       reads=[b_xin[i3]], writes=[b_st])
                S.emit("vector", lambda e, o=st[:, 1:2], i=st[:, 0:1]:
                       e.tensor_scalar_mul(out=o, in0=i, scalar1=1.0 / D),
                       reads=[b_st], writes=[b_st])
                S.emit("vector", lambda e, o=st[:, 0:1], i=st[:, 1:2]:
                       e.tensor_scalar_mul(out=o, in0=i, scalar1=-1.0),
                       reads=[b_st], writes=[b_st])
                S.emit("scalar", lambda e, o=sqs[i2][:], i=xin[i3][:], bi=st[:, 0:1]:
                       e.activation(out=o, in_=i, func=AF.Square, bias=bi, scale=1.0),
                       reads=[b_xin[i3], b_st], writes=[b_sq[i2]])
                S.emit("vector", lambda e, o=st[:, 2:3], i=sqs[i2][:]:
                       e.reduce_sum(out=o, in_=i, axis=AX),
                       reads=[b_sq[i2]], writes=[b_st])
                S.emit("scalar", lambda e, o=st[:, 3:4], i=st[:, 2:3]:
                       e.activation(out=o, in_=i, func=AF.Sqrt, bias=1e-5, scale=1.0 / D),
                       reads=[b_st], writes=[b_st])
                S.emit("vector", lambda e, o=st[:, 2:3], i=st[:, 3:4]:
                       e.reciprocal(out=o, in_=i),
                       reads=[b_st], writes=[b_st])
                S.emit("vector", lambda e, o=xnf[i2][:], i=xin[i3][:], m=st[:, 1:2],
                       r=st[:, 2:3]:
                       e.tensor_scalar(out=o, in0=i, scalar1=m, scalar2=r,
                                       op0=ALU.subtract, op1=ALU.mult),
                       reads=[b_xin[i3], b_st], writes=[b_xnf[i2]])
                S.emit("scalar", lambda e, o=sqs[i2][:], i=xnf[i2][:]:
                       e.activation(out=o, in_=i, func=AF.Square),
                       reads=[b_xnf[i2]], writes=[b_sq[i2]])
                S.emit("vector", lambda e, o=st[:, 4:5], i=sqs[i2][:]:
                       e.reduce_sum(out=o, in_=i, axis=AX),
                       reads=[b_sq[i2]], writes=[b_st])
                S.emit("scalar", lambda e, o=st[:, 5:6], i=st[:, 4:5]:
                       e.activation(out=o, in_=i, func=AF.Sqrt, bias=1e-6, scale=1.0 / D),
                       reads=[b_st], writes=[b_st])
                S.emit("vector", lambda e, o=st[:, 4:5], i=st[:, 5:6]:
                       e.reciprocal(out=o, in_=i),
                       reads=[b_st], writes=[b_st])
                S.emit("vector", lambda e, o=n2b[i2][:], i=xnf[i2][:], sc_=st[:, 4:5],
                       g=gb[:]:
                       e.scalar_tensor_tensor(out=o, in0=i, scalar=sc_, in1=g,
                                              op0=ALU.mult, op1=ALU.mult),
                       reads=[b_xnf[i2], b_st, b_gb], writes=[b_n2b[i2]])
                if warm:
                    continue
                for kc in range(KC):
                    S.emit("tensor", lambda e, o=ptr[i2][:, kc * 128:(kc + 1) * 128],
                           i=n2b[i2][:, kc * 128:(kc + 1) * 128], idn=sident[:]:
                           e.transpose(out=o, in_=i, identity=idn),
                           reads=[b_n2b[i2], b_ident], writes=[b_ptr[i2]])
                dview = n2T[:].rearrange("p (c t) -> p c t", c=KC)[:, :, tt * 128:(tt + 1) * 128]
                S.emit("vector", lambda e, o=dview,
                       i=ptr[i2][:].rearrange("p (c t) -> p c t", c=KC):
                       e.tensor_copy(out=o, in_=i),
                       reads=[b_ptr[i2]], writes=[b_n2T])
            S.barrier()

        CW = 4100
        pmine_g = nc.sbuf_tensor("pmine", [33, L], F32)
        pmine = pmine_g.__enter__()
        b_pm = Buf("pmine")
        xsT_g = nc.sbuf_tensor("xsT", [128, ECH * CW], BF16)
        xsT = xsT_g.__enter__()
        b_xsT = Buf("xsT")
        with ExitStack() as p2:
            sbl = lambda name, shape, dt=F32: p2.enter_context(nc.sbuf_tensor(name, shape, dt))
            psl = lambda name, shape, dt=F32: p2.enter_context(nc.psum_tensor(name, shape, dt))
            wcol = [sbl(f"wcol{i}", [128, KC * 128], BF16) for i in range(2)]
            b_wcol = [Buf(f"wcol{i}") for i in range(2)]
            ps_p = [psl(f"ps_p{i}", [128, 512]) for i in range(2)]
            b_ps_p = [Buf(f"ps_p{i}") for i in range(2)]
            stg = [sbl(f"stg{i}", [128, 512], BF16) for i in range(2)]
            b_stg = [Buf(f"stg{i}") for i in range(2)]
            for which, wcd, b_wcd in [(0, wc["winx"], b_wc["winx"]),
                                      (1, wc["winz"], b_wc["winz"])]:
                for ec in range(ECH):
                    iw = ec % 2
                    S.dma("sync", wcol[iw][:].rearrange("p (c n) -> p c n", c=KC),
                          wcd[:, ec * 128:(ec + 1) * 128].rearrange(
                              "(c p) n -> p c n", p=128),
                          reads=[b_wcd], writes=[b_wcol[iw]])
                    if which == 0:
                        S.emit("vector", lambda e, o=xsT[:, ec * CW:ec * CW + 4]:
                               e.memset(o, 0.0), writes=[b_xsT])
                    for tb in range(8):
                        ip = tb % 2
                        for kc in range(KC):
                            S.emit("tensor", lambda e, o=ps_p[ip][:, :],
                                   ll=wcol[iw][:, kc * 128:(kc + 1) * 128],
                                   r=n2T[:, kc * L + tb * 512: kc * L + (tb + 1) * 512],
                                   st=(kc == 0), sp=(kc == KC - 1):
                                   e.matmul(out=o, lhsT=ll, rhs=r, start=st, stop=sp),
                                   reads=[b_wcol[iw], b_n2T], writes=[b_ps_p[ip]])
                        if which == 0:
                            S.emit("vector", lambda e,
                                   o=xsT[:, ec * CW + 4 + tb * 512: ec * CW + 4 + (tb + 1) * 512],
                                   i=ps_p[ip][:, :]:
                                   e.tensor_copy(out=o, in_=i),
                                   reads=[b_ps_p[ip]], writes=[b_xsT])
                        else:
                            S.emit("vector", lambda e, o=stg[ip][:], i=ps_p[ip][:, :]:
                                   e.tensor_copy(out=o, in_=i),
                                   reads=[b_ps_p[ip]], writes=[b_stg[ip]])
                            S.dma("sync", z_d[ec * 128:(ec + 1) * 128, tb * 512:(tb + 1) * 512],
                                  stg[ip][:], reads=[b_stg[ip]],
                                  writes=[Buf(f"zd{ip}")], sem=f"dma_zd{ip}")
            S.barrier()
        n2T_g.__exit__(None, None, None)

        with ExitStack() as p3a:
            sbl = lambda name, shape, dt=F32: p3a.enter_context(nc.sbuf_tensor(name, shape, dt))
            psl = lambda name, shape, dt=F32: p3a.enter_context(nc.psum_tensor(name, shape, dt))
            cvw = sbl("cvw", [128, 48])
            b_cvw = Buf("cvw")
            S.dma("sync", cvw[:, 0:32], convw, writes=[b_cvw])
            S.dma("sync", cvw[:, 32:40], convb, writes=[b_cvw])
            acc = sbl("acc", [128, L])
            b_acc = Buf("acc")
            for ec in range(ECH):
                base = ec * CW
                S.emit("vector", lambda e, o=acc[:],
                       i=xsT[:, base + 1: base + 1 + L], sc_=cvw[:, ec * 4:ec * 4 + 1]:
                       e.tensor_scalar_mul(out=o, in0=i, scalar1=sc_),
                       reads=[b_xsT, b_cvw], writes=[b_acc])
                for k in range(1, 4):
                    S.emit("vector", lambda e, o=acc[:],
                           i=xsT[:, base + 1 + k: base + 1 + k + L],
                           sc_=cvw[:, ec * 4 + k:ec * 4 + k + 1], g=acc[:]:
                           e.scalar_tensor_tensor(out=o, in0=i, scalar=sc_, in1=g,
                                                  op0=ALU.mult, op1=ALU.add),
                           reads=[b_xsT, b_cvw, b_acc], writes=[b_acc])
                S.emit("scalar", lambda e, o=xsT[:, base + 4: base + 4 + L], i=acc[:],
                       bi=cvw[:, 32 + ec:33 + ec]:
                       e.activation(out=o, in_=i, func=AF.Silu, bias=bi),
                       reads=[b_acc, b_cvw], writes=[b_xsT])
                S.dma("sync", xc_d[ec * 128:(ec + 1) * 128, :],
                      xsT[:, base + 4: base + 4 + L],
                      reads=[b_xsT], writes=[Buf(f"xcd{ec}")], sem="dma_xcd")
            wxs = sbl("wxs", [128, KC * 33], BF16)
            b_wxs = Buf("wxs")
            S.dma("sync", wxs[:].rearrange("p (c n) -> p c n", c=KC),
                  wc["wxT"].rearrange("(c p) n -> p c n", p=128),
                  reads=[b_wc["wxT"]], writes=[b_wxs])
            ps_x = psl("ps_x", [33, 512])
            b_ps_x = Buf("ps_x")
            for tb in range(8):
                for ec in range(ECH):
                    S.emit("tensor", lambda e, o=ps_x[:, :],
                           ll=wxs[:, ec * 33:(ec + 1) * 33],
                           r=xsT[:, ec * CW + 4 + tb * 512: ec * CW + 4 + (tb + 1) * 512],
                           st=(ec == 0), sp=(ec == ECH - 1):
                           e.matmul(out=o, lhsT=ll, rhs=r, start=st, stop=sp),
                           reads=[b_wxs, b_xsT], writes=[b_ps_x])
                S.emit("vector", lambda e, o=pmine[:, tb * 512:(tb + 1) * 512],
                       i=ps_x[:, :]:
                       e.tensor_copy(out=o, in_=i),
                       reads=[b_ps_x], writes=[b_pm])
            S.dma("sync", cc_p_in, pmine[:], reads=[b_pm], writes=[Buf("ccp")])
            b_ccpo = Buf("cc_p_out")
            S.emit("gpsimd", lambda e: e.collective_compute(
                "AllGather", ALU.bypass,
                replica_groups=[[0, 1, 2, 3, 4, 5, 6, 7]],
                ins=[cc_p_in.opt()], outs=[cc_p_out.opt()]),
                after=[S.last_dma.get("dma_ccp")], writes=[b_ccpo])
            idxp = sbl("idxp", [33, 1], I32)
            b_idxp = Buf("idxp")
            S.dma("sync", idxp[:], idxpar, writes=[b_idxp])
            S.emit("gpsimd", lambda e: e.indirect_dma_start(
                out=acc[0:33, :], out_offset=None, in_=cc_p_out,
                in_offset=bass.IndirectOffsetOnAxis(idxp[:, 0:1], 0)),
                reads=[b_ccpo, b_idxp], writes=[b_acc], dma=True)
            S.emit("vector", lambda e, o=pmine[:], a=pmine[:], b_=acc[0:33, :]:
                   e.tensor_add(out=o, in0=a, in1=b_),
                   reads=[b_acc, b_pm], writes=[b_pm])
            S.barrier()
        xsT_g.__exit__(None, None, None)

        with ExitStack() as p3b:
            sbl = lambda name, shape, dt=F32: p3b.enter_context(nc.sbuf_tensor(name, shape, dt))
            psl = lambda name, shape, dt=F32: p3b.enter_context(nc.psum_tensor(name, shape, dt))
            QL = 1024
            ps_f = psl("ps_f", [128, 512])
            b_ps_f = Buf("ps_f")
            xcq = sbl("xcq", [128, ECH * QL], BF16)
            b_xcq = Buf("xcq")
            ldbq = sbl("ldbq", [128, QL])
            b_ldb = Buf("ldbq")
            dtb_c = sbl("dtb_c", [128, ECH * QL], BF16)
            b_dt = Buf("dtb_c")
            u_c = sbl("u_c", [128, ECH * QL], BF16)
            b_u = Buf("u_c")
            wdt_sb = sbl("wdt_sb", [128, 16])
            b_wdt = Buf("wdt_sb")
            S.dma("sync", wdt_sb[:, 0:8], wdt, writes=[b_wdt])
            S.dma("sync", wdt_sb[:, 8:16], bdt, writes=[b_wdt])
            asl_sb = sbl("asl_sb", [128, 128])
            b_asl = Buf("asl_sb")
            S.dma("sync", asl_sb[:], a_sl, writes=[b_asl])
            S.emit("scalar", lambda e, o=asl_sb[:], i=asl_sb[:]:
                   e.activation(out=o, in_=i, func=AF.Exp),
                   reads=[b_asl], writes=[b_asl])
            S.emit("vector", lambda e, o=asl_sb[:], i=asl_sb[:]:
                   e.tensor_scalar_mul(out=o, in0=i, scalar1=-1.0),
                   reads=[b_asl], writes=[b_asl])
            dsl_sb = sbl("dsl_sb", [128, 8])
            b_dsl = Buf("dsl_sb")
            S.dma("sync", dsl_sb[:], d_sl, writes=[b_dsl])
            raw = sbl("raw", [128, QL])
            b_raw = Buf("raw")
            t_a = sbl("t_a", [128, QL])
            b_ta = Buf("t_a")
            t_r = sbl("t_r", [128, QL])
            b_tr = Buf("t_r")
            bcb = sbl("bcb", [128, 2 * QL])
            b_bcb = Buf("bcb")
            rowstg = [sbl(f"rowstg{i}", [1, QL]) for i in range(2)]
            b_rowstg = [Buf(f"rowstg{i}") for i in range(2)]
            dA_t = sbl("dA_t", [128, QL])
            b_dA = Buf("dA_t")
            dBx = sbl("dBx", [128, QL])
            b_dBx = Buf("dBx")
            h_t = sbl("h_t", [128, QL])
            b_h = Buf("h_t")
            t2 = sbl("t2", [128, QL])
            b_t2 = Buf("t2")
            y_q = sbl("y_q", [128, ECH * QL])
            b_y = Buf("y_q")
            h_st = sbl("h_st", [128, 128])
            b_hst = Buf("h_st")
            S.emit("vector", lambda e: e.memset(h_st[:], 0.0), writes=[b_hst])
            zch = sbl("zch", [128, QL], BF16)
            b_zch = Buf("zch")
            ySgT = sbl("ySgT", [128, ECH * QL], BF16)
            b_ySg = Buf("ySgT")
            wout_sb = sbl("wout_sb", [128, ECH * D], BF16)
            b_wout = Buf("wout_sb")
            S.dma("sync", wout_sb[:].rearrange("p (c n) -> p c n", c=ECH),
                  wc["woutT"].rearrange("(c p) n -> p c n", p=128),
                  reads=[b_wc["woutT"]], writes=[b_wout])
            ps_p2 = [psl(f"ps_p2{i}", [128, 512]) for i in range(2)]
            b_ps_p2 = [Buf(f"ps_p2{i}") for i in range(2)]
            stg3 = [sbl(f"stg3{i}", [128, 512], BF16) for i in range(2)]
            b_stg3 = [Buf(f"stg3{i}") for i in range(2)]

            for tq in range(4):
                q0 = tq * QL
                S.dma("sync", xcq[:].rearrange("p (c n) -> p c n", c=ECH),
                      xc_d[:, q0:q0 + QL].rearrange("(c p) n -> p c n", p=128),
                      writes=[b_xcq])
                S.dma("sync", rowstg[0][:], pmine[32:33, q0:q0 + QL],
                      reads=[b_pm], writes=[b_rowstg[0]])
                for c0 in range(0, QL, 512):
                    S.emit("tensor", lambda e, o=ps_f[:, :], ll=sonesf[:],
                           r=rowstg[0][0:1, c0:c0 + 512]:
                           e.matmul(out=o, lhsT=ll, rhs=r, start=True, stop=True),
                           reads=[b_rowstg[0], b_onesf], writes=[b_ps_f])
                    S.emit("vector", lambda e, o=ldbq[:, c0:c0 + 512], i=ps_f[:, :]:
                           e.tensor_copy(out=o, in_=i),
                           reads=[b_ps_f], writes=[b_ldb])
                for ec in range(ECH):
                    S.emit("vector", lambda e, o=raw[:], i=ldbq[:],
                           w=wdt_sb[:, ec:ec + 1], bb=wdt_sb[:, 8 + ec:9 + ec]:
                           e.tensor_scalar(out=o, in0=i, scalar1=w, scalar2=bb,
                                           op0=ALU.mult, op1=ALU.add),
                           reads=[b_ldb, b_wdt], writes=[b_raw])
                    S.emit("scalar", lambda e, o=t_a[:], i=raw[:]:
                           e.activation(out=o, in_=i, func=AF.Abs),
                           reads=[b_raw], writes=[b_ta])
                    S.emit("scalar", lambda e, o=t_a[:], i=t_a[:]:
                           e.activation(out=o, in_=i, func=AF.Exp, scale=-1.0),
                           reads=[b_ta], writes=[b_ta])
                    S.emit("scalar", lambda e, o=t_a[:], i=t_a[:]:
                           e.activation(out=o, in_=i, func=AF.Ln, bias=1.0),
                           reads=[b_ta], writes=[b_ta])
                    S.emit("vector", lambda e, o=t_r[:], i=raw[:]:
                           e.tensor_relu(out=o, in_=i),
                           reads=[b_raw], writes=[b_tr])
                    S.emit("vector", lambda e, o=dtb_c[:, ec * QL:(ec + 1) * QL],
                           a=t_r[:], b_=t_a[:]:
                           e.tensor_add(out=o, in0=a, in1=b_),
                           reads=[b_tr, b_ta], writes=[b_dt])
                    S.emit("vector", lambda e, o=u_c[:, ec * QL:(ec + 1) * QL],
                           a=dtb_c[:, ec * QL:(ec + 1) * QL],
                           b_=xcq[:, ec * QL:(ec + 1) * QL]:
                           e.tensor_mul(out=o, in0=a, in1=b_),
                           reads=[b_dt, b_xcq], writes=[b_u])
                for s in range(DS):
                    for half, roff in ((0, s), (1, 16 + s)):
                        S.dma("sync", rowstg[half][:],
                              pmine[roff:roff + 1, q0:q0 + QL],
                              reads=[b_pm], writes=[b_rowstg[half]])
                        for c0 in range(0, QL, 512):
                            S.emit("tensor", lambda e, o=ps_f[:, :], ll=sonesf[:],
                                   r=rowstg[half][0:1, c0:c0 + 512]:
                                   e.matmul(out=o, lhsT=ll, rhs=r, start=True, stop=True),
                                   reads=[b_rowstg[half], b_onesf], writes=[b_ps_f])
                            S.emit("vector", lambda e,
                                   o=bcb[:, half * QL + c0: half * QL + c0 + 512],
                                   i=ps_f[:, :]:
                                   e.tensor_copy(out=o, in_=i),
                                   reads=[b_ps_f], writes=[b_bcb])
                    for ec in range(ECH):
                        S.emit("scalar", lambda e, o=dA_t[:],
                               i=dtb_c[:, ec * QL:(ec + 1) * QL],
                               sc_=asl_sb[:, ec * 16 + s: ec * 16 + s + 1]:
                               e.activation(out=o, in_=i, func=AF.Exp, scale=sc_),
                               reads=[b_dt, b_asl], writes=[b_dA])
                        S.emit("vector", lambda e, o=dBx[:],
                               a=u_c[:, ec * QL:(ec + 1) * QL], b_=bcb[:, 0:QL]:
                               e.tensor_mul(out=o, in0=a, in1=b_),
                               reads=[b_u, b_bcb], writes=[b_dBx])
                        S.emit("vector", lambda e, o=h_t[:], a=dA_t[:], b_=dBx[:],
                               ini=h_st[:, ec * 16 + s: ec * 16 + s + 1]:
                               e.tensor_tensor_scan(out=o, data0=a, data1=b_,
                                                    initial=ini, op0=ALU.mult,
                                                    op1=ALU.add),
                               reads=[b_dA, b_dBx, b_hst], writes=[b_h])
                        S.emit("vector", lambda e,
                               o=h_st[:, ec * 16 + s: ec * 16 + s + 1],
                               i=h_t[:, QL - 1:QL]:
                               e.tensor_copy(out=o, in_=i),
                               reads=[b_h], writes=[b_hst])
                        S.emit("vector", lambda e, o=t2[:], a=h_t[:], b_=bcb[:, QL:2 * QL]:
                               e.tensor_mul(out=o, in0=a, in1=b_),
                               reads=[b_h, b_bcb], writes=[b_t2])
                        if s == 0:
                            S.emit("vector", lambda e, o=y_q[:, ec * QL:(ec + 1) * QL],
                                   i=t2[:]:
                                   e.tensor_copy(out=o, in_=i),
                                   reads=[b_t2], writes=[b_y])
                        else:
                            S.emit("vector", lambda e, o=y_q[:, ec * QL:(ec + 1) * QL],
                                   a=t2[:], b_=y_q[:, ec * QL:(ec + 1) * QL]:
                                   e.tensor_add(out=o, in0=a, in1=b_),
                                   reads=[b_t2, b_y], writes=[b_y])
                for ec in range(ECH):
                    S.dma("sync", zch[:], z_d[ec * 128:(ec + 1) * 128, q0:q0 + QL],
                          writes=[b_zch])
                    S.emit("scalar", lambda e, o=t_a[:], i=zch[:]:
                           e.activation(out=o, in_=i, func=AF.Silu),
                           reads=[b_zch], writes=[b_ta])
                    S.emit("vector", lambda e, o=t2[:],
                           i=xcq[:, ec * QL:(ec + 1) * QL],
                           sc_=dsl_sb[:, ec:ec + 1], g=y_q[:, ec * QL:(ec + 1) * QL]:
                           e.scalar_tensor_tensor(out=o, in0=i, scalar=sc_, in1=g,
                                                  op0=ALU.mult, op1=ALU.add),
                           reads=[b_xcq, b_dsl, b_y], writes=[b_t2])
                    S.emit("vector", lambda e, o=ySgT[:, ec * QL:(ec + 1) * QL],
                           a=t2[:], b_=t_a[:]:
                           e.tensor_mul(out=o, in0=a, in1=b_),
                           reads=[b_t2, b_ta], writes=[b_ySg])
                for t8 in range(8):
                    for half in range(2):
                        for ec in range(ECH):
                            S.emit("tensor", lambda e, o=ps_p2[half][:, :],
                                   ll=ySgT[:, ec * QL + t8 * 128: ec * QL + (t8 + 1) * 128],
                                   r=wout_sb[:, ec * D + half * 512: ec * D + half * 512 + 512],
                                   st=(ec == 0), sp=(ec == ECH - 1):
                                   e.matmul(out=o, lhsT=ll, rhs=r, start=st, stop=sp),
                                   reads=[b_ySg, b_wout], writes=[b_ps_p2[half]])
                        S.emit("vector", lambda e, o=stg3[half][:], i=ps_p2[half][:, :]:
                               e.tensor_copy(out=o, in_=i),
                               reads=[b_ps_p2[half]], writes=[b_stg3[half]])
                        S.dma("sync",
                              cc_s_in[q0 + t8 * 128: q0 + (t8 + 1) * 128,
                                      half * 512:(half + 1) * 512],
                              stg3[half][:], reads=[b_stg3[half]],
                              writes=[Buf(f"ccs{half}")], sem=f"dma_ccs{half}")
            S.barrier()
        pmine_g.__exit__(None, None, None)
        s_fin = sb("s_fin", [128, NT_O * D], BF16)
        b_s_fin = Buf("s_fin")

        b_ccso = Buf("cc_s_out")
        S.emit("gpsimd", lambda e: e.collective_compute(
            "AllGather", ALU.bypass,
            replica_groups=[[0, 1, 2, 3, 4, 5, 6, 7]],
            ins=[cc_s_in.opt()], outs=[cc_s_out.opt()]),
            writes=[b_ccso])
        with ExitStack() as p4:
            sbl = lambda name, shape, dt=F32: p4.enter_context(nc.sbuf_tensor(name, shape, dt))
            psl = lambda name, shape, dt=F32: p4.enter_context(nc.psum_tensor(name, shape, dt))
            idxs = sbl("idxs", [128, 32], I32)
            b_idxs = Buf("idxs")
            S.dma("sync", idxs[:, 0:16], idxme, writes=[b_idxs])
            S.dma("sync", idxs[:, 16:32], idxpr, writes=[b_idxs])
            sga = [sbl(f"sga{i}", [128, D], BF16) for i in range(2)]
            b_sga = [Buf(f"sga{i}") for i in range(2)]
            for tt in range(NT_O):
                for j in range(2):
                    S.emit("gpsimd", lambda e, o=sga[j][:],
                           ix=idxs[:, j * 16 + tt: j * 16 + tt + 1]:
                           e.indirect_dma_start(
                               out=o, out_offset=None, in_=cc_s_out,
                               in_offset=bass.IndirectOffsetOnAxis(ix, 0)),
                           reads=[b_ccso, b_idxs], writes=[b_sga[j]], dma=True)
                S.emit("vector", lambda e, o=s_fin[:, tt * D:(tt + 1) * D],
                       a=sga[0][:], b_=sga[1][:]:
                       e.tensor_add(out=o, in0=a, in1=b_),
                       reads=[b_sga[0], b_sga[1]], writes=[b_s_fin])
            S.barrier()

        with ExitStack() as p5:
            sbl = lambda name, shape, dt=F32: p5.enter_context(nc.sbuf_tensor(name, shape, dt))
            psl = lambda name, shape, dt=F32: p5.enter_context(nc.psum_tensor(name, shape, dt))
            xnT2 = sbl("xnT2", [128, KC * TO], BF16)
            b_xnT2 = Buf("xnT2")
            S.dma("sync", xnT2[:], xnT_d, writes=[b_xnT2])
            wg_sb = sbl("wg_sb", [128, KC * 8], BF16)
            b_wg = Buf("wg_sb")
            S.dma("sync", wg_sb[:].rearrange("p (c n) -> p c n", c=KC),
                  wc["wg"].rearrange("(c p) n -> p c n", p=128),
                  reads=[b_wc["wg"]], writes=[b_wg])
            probs_a = sbl("probs_a", [128, NT_O * 8])
            b_pra = Buf("probs_a")
            mask_a = sbl("mask_a", [128, NT_O * 8])
            b_ma = Buf("mask_a")
            maskT = sbl("maskT", [8, TO])
            b_mT = Buf("maskT")
            keepT = sbl("keepT", [8, TO])
            b_kT2 = Buf("keepT")
            csum = sbl("csum", [8, TO])
            b_cs2 = Buf("csum")
            onesr = sbl("onesr", [8, TO])
            b_onr = Buf("onesr")
            S.emit("vector", lambda e: e.memset(onesr[:], 1.0), writes=[b_onr])
            rtmp = sbl("rtmp", [128, 8])
            b_rt = Buf("rtmp")
            ps_r = psl("ps_r", [128, 8])
            b_ps_r = Buf("ps_r")
            ps_t = psl("ps_t", [8, 128])
            b_ps_t = Buf("ps_t")
            ps_t2 = psl("ps_t2", [128, 8], BF16)
            b_ps_t2 = Buf("ps_t2")
            for tt in range(NT_O):
                for kc in range(KC):
                    S.emit("tensor", lambda e, o=ps_r[:, :],
                           ll=xnT2[:, kc * TO + tt * 128: kc * TO + (tt + 1) * 128],
                           r=wg_sb[:, kc * 8:(kc + 1) * 8],
                           st=(kc == 0), sp=(kc == KC - 1):
                           e.matmul(out=o, lhsT=ll, rhs=r, start=st, stop=sp),
                           reads=[b_xnT2, b_wg], writes=[b_ps_r])
                S.emit("vector", lambda e, o=rtmp[:, 0:1], i=ps_r[:, :]:
                       e.reduce_max(out=o, in_=i, axis=AX),
                       reads=[b_ps_r], writes=[b_rt])
                S.emit("vector", lambda e, o=rtmp[:, 1:2], i=rtmp[:, 0:1]:
                       e.tensor_scalar_mul(out=o, in0=i, scalar1=-1.0),
                       reads=[b_rt], writes=[b_rt])
                S.emit("scalar", lambda e, o=probs_a[:, tt * 8:(tt + 1) * 8],
                       i=ps_r[:, :], bi=rtmp[:, 1:2]:
                       e.activation(out=o, in_=i, func=AF.Exp, bias=bi),
                       reads=[b_ps_r, b_rt], writes=[b_pra])
                S.emit("vector", lambda e, o=rtmp[:, 2:3],
                       i=probs_a[:, tt * 8:(tt + 1) * 8]:
                       e.reduce_sum(out=o, in_=i, axis=AX),
                       reads=[b_pra], writes=[b_rt])
                S.emit("vector", lambda e, o=rtmp[:, 3:4], i=rtmp[:, 2:3]:
                       e.reciprocal(out=o, in_=i),
                       reads=[b_rt], writes=[b_rt])
                S.emit("vector", lambda e, o=probs_a[:, tt * 8:(tt + 1) * 8],
                       i=probs_a[:, tt * 8:(tt + 1) * 8], sc_=rtmp[:, 3:4]:
                       e.tensor_scalar_mul(out=o, in0=i, scalar1=sc_),
                       reads=[b_rt, b_pra], writes=[b_pra])
                S.emit("vector", lambda e, o=rtmp[:, 4:5],
                       i=probs_a[:, tt * 8:(tt + 1) * 8]:
                       e.reduce_max(out=o, in_=i, axis=AX),
                       reads=[b_pra], writes=[b_rt])
                S.emit("vector", lambda e, o=mask_a[:, tt * 8:(tt + 1) * 8],
                       i=probs_a[:, tt * 8:(tt + 1) * 8], sc_=rtmp[:, 4:5]:
                       e.tensor_single_scalar(out=o, in_=i, scalar=sc_, op=ALU.is_ge),
                       reads=[b_pra, b_rt], writes=[b_ma])
                S.emit("tensor", lambda e, o=ps_t[:, :],
                       i=mask_a[:, tt * 8:(tt + 1) * 8], idn=sidentf[:]:
                       e.transpose(out=o, in_=i, identity=idn),
                       reads=[b_ma, b_identf], writes=[b_ps_t])
                S.emit("vector", lambda e, o=maskT[:, tt * 128:(tt + 1) * 128],
                       i=ps_t[0:8, :]:
                       e.tensor_copy(out=o, in_=i),
                       reads=[b_ps_t], writes=[b_mT])
            S.emit("vector", lambda e, o=csum[:], a=onesr[:], b_=maskT[:]:
                   e.tensor_tensor_scan(out=o, data0=a, data1=b_, initial=0.0,
                                        op0=ALU.mult, op1=ALU.add),
                   reads=[b_onr, b_mT], writes=[b_cs2])
            cnt_r = sbl("cnt_r", [8, 2])
            b_cnt = Buf("cnt_r")
            S.emit("vector", lambda e, o=cnt_r[:, 0:1], i=csum[:, TO - 1:TO]:
                   e.tensor_copy(out=o, in_=i),
                   reads=[b_cs2], writes=[b_cnt])
            ps_c = psl("ps_c", [8, 8])
            b_ps_c = Buf("ps_c")
            crow = sbl("crow", [1, 8])
            b_crow = Buf("crow")
            S.dma("sync", cnt_d, cnt_r[:, 0:1], reads=[b_cnt], writes=[Buf("cntd")])
            S.dma("sync", crow[:], cnt_d.rearrange("e o -> o e"),
                  after=[S.last_dma.get("dma_cntd")], writes=[b_crow])
            S.dma("sync", cc_c_in, crow[:], reads=[b_crow], writes=[Buf("ccc")])
            b_ccco = Buf("cc_c_out")
            S.emit("gpsimd", lambda e: e.collective_compute(
                "AllGather", ALU.bypass,
                replica_groups=[[0, 1, 2, 3, 4, 5, 6, 7]],
                ins=[cc_c_in.opt()], outs=[cc_c_out.opt()]),
                after=[S.last_dma.get("dma_ccc")], writes=[b_ccco])
            cnts8 = sbl("cnts8", [8, 8])
            b_c8 = Buf("cnts8")
            S.dma("sync", cnts8[:], cc_c_out, reads=[b_ccco], writes=[b_c8])
            sel_sb = sbl("sel_sb", [8, 1])
            b_sel = Buf("sel_sb")
            S.dma("sync", sel_sb[:], sel8, writes=[b_sel])
            S.emit("tensor", lambda e, o=ps_c[0:1, 0:8], ll=sel_sb[:], r=cnts8[:]:
                   e.matmul(out=o, lhsT=ll, rhs=r, start=True, stop=True),
                   reads=[b_sel, b_c8], writes=[b_ps_c])
            S.emit("vector", lambda e, o=crow[:], i=ps_c[0:1, 0:8]:
                   e.tensor_copy(out=o, in_=i),
                   reads=[b_ps_c], writes=[b_crow])
            offc = sbl("offc", [8, 1])
            b_off = Buf("offc")
            S.dma("sync", off_d, crow[:], reads=[b_crow], writes=[Buf("offd")])
            S.dma("sync", offc[:], off_d.rearrange("o e -> e o"),
                  after=[S.last_dma.get("dma_offd")], writes=[b_off])
            S.emit("vector", lambda e, o=csum[:], a=csum[:], b_=maskT[:]:
                   e.tensor_sub(out=o, in0=a, in1=b_),
                   reads=[b_mT, b_cs2], writes=[b_cs2])
            S.emit("vector", lambda e, o=csum[:], i=csum[:], sc_=offc[:, 0:1]:
                   e.tensor_scalar_add(out=o, in0=i, scalar1=sc_),
                   reads=[b_off, b_cs2], writes=[b_cs2])
            S.emit("vector", lambda e, o=keepT[:], i=csum[:]:
                   e.tensor_single_scalar(out=o, in_=i, scalar=float(TO), op=ALU.is_lt),
                   reads=[b_cs2], writes=[b_kT2])
            we_a = sbl("we_a", [128, NT_O * 8])
            b_we = Buf("we_a")
            ps_t3 = psl("ps_t3", [128, 8])
            b_ps_t3 = Buf("ps_t3")
            for tt in range(NT_O):
                S.emit("tensor", lambda e, o=ps_t3[:, 0:8],
                       i=keepT[0:8, tt * 128:(tt + 1) * 128], idn=sidentf[0:8, 0:8]:
                       e.transpose(out=o, in_=i, identity=idn),
                       reads=[b_kT2, b_identf], writes=[b_ps_t3])
                S.emit("vector", lambda e, o=we_a[:, tt * 8:(tt + 1) * 8],
                       a=probs_a[:, tt * 8:(tt + 1) * 8],
                       b_=mask_a[:, tt * 8:(tt + 1) * 8]:
                       e.tensor_mul(out=o, in0=a, in1=b_),
                       reads=[b_pra, b_ma], writes=[b_we])
                S.emit("vector", lambda e, o=we_a[:, tt * 8:(tt + 1) * 8],
                       a=we_a[:, tt * 8:(tt + 1) * 8], b_=ps_t3[:, 0:8]:
                       e.tensor_mul(out=o, in0=a, in1=b_),
                       reads=[b_ps_t3, b_we], writes=[b_we])
            brow_sb = sbl("brow_sb", [1, 8 * D], BF16)
            b_brsb = Buf("brow_sb")
            S.dma("sync", brow_sb[:], wc["brows"], reads=[b_wc["brows"]], writes=[b_brsb])
            wex = [sbl(f"wex{i}", [128, KC * D], BF16) for i in range(2)]
            b_wex = [Buf(f"wex{i}") for i in range(2)]
            g_acc = sbl("g_acc", [128, 4 * D])
            b_gacc = Buf("g_acc")
            ps_e = [psl(f"ps_e{i}", [128, 512]) for i in range(2)]
            b_ps_e = [Buf(f"ps_e{i}") for i in range(2)]
            gs_t = sbl("gs_t", [128, D])
            b_gs = Buf("gs_t")
            sv_t = sbl("sv_t", [128, D])
            b_sv = Buf("sv_t")
            xot = [sbl(f"xot{i}", [128, D]) for i in range(2)]
            b_xot = [Buf(f"xot{i}") for i in range(2)]
            xnre = [sbl(f"xnre{i}", [128, D], BF16) for i in range(2)]
            b_xnre = [Buf(f"xnre{i}") for i in range(2)]
            for tb in range(4):
                S.emit("vector", lambda e, o=g_acc[:]: e.memset(o, 0.0),
                       writes=[b_gacc])
                for ei in range(8):
                    iw = ei % 2
                    S.dma("sync", wex[iw][:].rearrange("p (c n) -> p c n", c=KC),
                          wc["wexp"][ei].rearrange("(c p) n -> p c n", p=128),
                          reads=[b_wc["wexp"]], writes=[b_wex[iw]])
                    for t4 in range(4):
                        tt = tb * 4 + t4
                        for half in range(2):
                            for kc in range(KC):
                                S.emit("tensor", lambda e, o=ps_e[half][:, :],
                                       ll=xnT2[:, kc * TO + tt * 128: kc * TO + (tt + 1) * 128],
                                       r=wex[iw][:, kc * D + half * 512: kc * D + half * 512 + 512],
                                       st=(kc == 0):
                                       e.matmul(out=o, lhsT=ll, rhs=r, start=st, stop=False),
                                       reads=[b_xnT2, b_wex[iw]], writes=[b_ps_e[half]])
                            S.emit("tensor", lambda e, o=ps_e[half][:, :], ll=sones[:],
                                   r=brow_sb[0:1, ei * D + half * 512: ei * D + half * 512 + 512]:
                                   e.matmul(out=o, lhsT=ll, rhs=r, start=False, stop=True),
                                   reads=[b_brsb, b_ones], writes=[b_ps_e[half]])
                            S.emit("vector", lambda e,
                                   o=g_acc[:, t4 * D + half * 512: t4 * D + (half + 1) * 512],
                                   i=ps_e[half][:, :], sc_=we_a[:, tt * 8 + ei: tt * 8 + ei + 1],
                                   g=g_acc[:, t4 * D + half * 512: t4 * D + (half + 1) * 512]:
                                   e.scalar_tensor_tensor(out=o, in0=i, scalar=sc_, in1=g,
                                                          op0=ALU.mult, op1=ALU.add),
                                   reads=[b_ps_e[half], b_we, b_gacc], writes=[b_gacc])
                for t4 in range(4):
                    tt = tb * 4 + t4
                    i2 = t4 % 2
                    S.dma("sync", xot[i2][:], x_own[tt * 128:(tt + 1) * 128, :],
                          writes=[b_xot[i2]])
                    S.dma("sync", xnre[i2][:], xn_d[tt * 128:(tt + 1) * 128, :],
                          writes=[b_xnre[i2]])
                    S.emit("scalar", lambda e, o=gs_t[:], i=g_acc[:, t4 * D:(t4 + 1) * D]:
                           e.activation(out=o, in_=i, func=AF.Silu),
                           reads=[b_gacc], writes=[b_gs])
                    S.emit("vector", lambda e, o=sv_t[:],
                           a=s_fin[:, tt * D:(tt + 1) * D], b_=xnre[i2][:]:
                           e.tensor_add(out=o, in0=a, in1=b_),
                           reads=[b_s_fin, b_xnre[i2]], writes=[b_sv])
                    S.emit("vector", lambda e, o=gs_t[:], a=sv_t[:], b_=gs_t[:]:
                           e.tensor_mul(out=o, in0=a, in1=b_),
                           reads=[b_sv, b_gs], writes=[b_gs])
                    S.emit("vector", lambda e, o=xot[i2][:], a=xot[i2][:], b_=gs_t[:]:
                           e.tensor_add(out=o, in0=a, in1=b_),
                           reads=[b_gs, b_xot[i2]], writes=[b_xot[i2]])
                    S.dma("sync", xo[tt * 128:(tt + 1) * 128, :], xot[i2][:],
                          reads=[b_xot[i2]], writes=[Buf(f"xod{i2}")],
                          sem=f"dma_xo{i2}")
        S.barrier("sync")
        S.build(nc, es)
    return nc


def _l2_in_maps(inputs):
    i = inputs
    W_inT = np.ascontiguousarray(np.asarray(i["L2_attn_W_in"]).T)
    W_outT = np.ascontiguousarray(np.asarray(i["L2_attn_W_out"]).T)
    W_xT = np.ascontiguousarray(np.asarray(i["L2_attn_W_x"]).T)
    conv_w = np.asarray(i["L2_attn_conv_w"])
    conv_b = np.asarray(i["L2_attn_conv_b"])
    W_dt = np.asarray(i["L2_attn_W_dt"])[:, 0]
    b_dt = np.asarray(i["L2_attn_b_dt"])
    A_log = np.asarray(i["L2_attn_A_log"])
    Dsk = np.asarray(i["L2_attn_D"])
    wgq = np.ascontiguousarray(np.asarray(i["L2_ffn_wg"]))
    Wexp = np.ascontiguousarray(np.asarray(i["L2_ffn_W"]).transpose(0, 2, 1))
    brows = np.ascontiguousarray(np.asarray(i["L2_ffn_b"]).reshape(1, 8 * D))
    ident = np.eye(128, dtype=np.float32).astype(ml_dtypes.bfloat16)
    g_row = np.asarray(i["L2_attn_g"]).reshape(1, D)

    def colchunk(v):
        return np.ascontiguousarray(v.reshape(8, 128).T)

    maps = []
    for c in range(8):
        b, half = c // 2, c % 2
        sl = slice(half * 1024, half * 1024 + 1024)
        a_sl = np.ascontiguousarray(
            A_log[sl].reshape(8, 128, 16).transpose(1, 0, 2).reshape(128, 128))
        m = dict(
            g_row=g_row,
            winx=np.ascontiguousarray(W_inT[:, sl]),
            winz=np.ascontiguousarray(W_inT[:, 2048 + half * 1024: 2048 + half * 1024 + 1024]),
            convw=np.ascontiguousarray(conv_w[sl].reshape(8, 128, 4).transpose(1, 0, 2).reshape(128, 32)),
            convb=colchunk(conv_b[sl]),
            wxT=np.ascontiguousarray(W_xT[sl]),
            wdt=colchunk(W_dt[sl]),
            bdt=colchunk(b_dt[sl]),
            a_sl=a_sl,
            d_sl=colchunk(Dsk[sl]),
            woutT=np.ascontiguousarray(W_outT[sl]),
            wg=wgq, wexp=Wexp, brows=brows, ident=ident,
            sel8=(np.arange(8) < c).astype(np.float32).reshape(8, 1),
        )
        t0 = half * TO
        rows_own = np.arange(t0, t0 + TO)
        p = c ^ 1
        m["idxme"] = np.ascontiguousarray(
            (4096 * c + rows_own).reshape(16, 128).T.astype(np.int32))
        m["idxpr"] = np.ascontiguousarray(
            (4096 * p + rows_own).reshape(16, 128).T.astype(np.int32))
        m["idxpar"] = (33 * p + np.arange(33)).astype(np.int32).reshape(33, 1)
        maps.append(m)
    return maps





def build_l3(debug=False):
    nc = bass.Bass("TRN2", target_bir_lowering=False)
    DS = 16
    ECH = 8
    NH = 8

    def din(name, shape, dt=F32):
        return nc.dram_tensor(name, shape, dt, kind="ExternalInput").ap()

    x_own = din("x_own", [TO, D])
    x_oth = din("x_oth", [TO, D])
    x_nat = din("x_nat", [L, D])
    lnw = din("lnw", [1, D])
    lnb = din("lnb", [1, D])
    dirp = {}
    for dn in ("f", "b"):
        dirp[dn] = dict(
            inwx=din(f"inwx_{dn}", [D, D]),
            inwz=din(f"inwz_{dn}", [D, D]),
            convw=din(f"convw_{dn}", [128, 32]),
            convb=din(f"convb_{dn}", [128, 8]),
            xprojT=din(f"xprojT_{dn}", [D, 96]),
            dtwT=din(f"dtwT_{dn}", [64, D]),
            dtb=din(f"dtb_{dn}", [128, 8]),
            alog=din(f"alog_{dn}", [128, 128]),
            dsk=din(f"dsk_{dn}", [128, 8]),
            outw=din(f"outw_{dn}", [D, D]),
            projT=din(f"projT_{dn}", [D, D]),
        )
    projb = din("projb", [1, D])
    wq_i = din("wq_i", [D, 512])
    wk_i = din("wk_i", [D, 512])
    ws_i = din("ws_i", [D, 40])
    wd_i = din("wd_i", [D, 40])
    wvf_i = din("wvf_i", [D, 320])
    wo_i = din("wo_i", [5, 64, D])
    idxme = din("idxme", [128, 16], I32)
    idxpr = din("idxpr", [128, 16], I32)
    idxpar = din("idxpar", [96, 1], I32)
    ident = din("ident", [128, 128], BF16)
    xo = nc.dram_tensor("xo", [TO, D], F32, kind="ExternalOutput").ap()

    def dint(name, shape, dt=BF16, **kw):
        return nc.dram_tensor(name, shape, dt, kind="Internal", **kw).ap()

    wc = {}
    cast_list = [("lnw", lnw), ("lnb", lnb), ("projb", projb), ("wq", wq_i),
                 ("wk", wk_i), ("ws", ws_i), ("wd", wd_i), ("wvf", wvf_i)]
    for dn in ("f", "b"):
        for sub in ("inwx", "inwz", "xprojT", "dtwT", "outw", "projT"):
            cast_list.append((f"{sub}_{dn}", dirp[dn][sub]))
    for nm, srcw in cast_list:
        wc[nm] = dint("wc_" + nm, list(srcw.shape))
    wc["wo"] = dint("wc_wo", [5, 64, D])
    z_d = {dn: dint(f"z_d_{dn}", [D, L]) for dn in ("f", "b")}
    xc_d = {dn: dint(f"xc_d_{dn}", [D, L]) for dn in ("f", "b")}
    ysg_d = {dn: dint(f"ysg_d_{dn}", [D, L]) for dn in ("f", "b")}
    xn_d = dint("xn_d", [TO, D])
    xnT_d = dint("xnT_d", [128, KC * TO])
    xnTf_d = dint("xnTf_d", [128, KC * L])
    v_d3 = dint("v_d3", [L, 512])
    cc_p_in = dint("cc_p_in", [96, L], F32)
    pg_d = dint("pg_d", [96, L], F32)
    cc_p_out = dint("cc_p_out", [8 * 96, L], F32)
    cc_s_in = dint("cc_s_in", [L, D], BF16)
    cc_s_out = dint("cc_s_out", [8 * L, D], BF16)

    S = Sched()
    with ExitStack() as es:
        sb = lambda name, shape, dt=F32: es.enter_context(nc.sbuf_tensor(name, shape, dt))
        sident = sb("sident", [128, 128], BF16)
        sones = sb("sones", [1, 128], BF16)
        sonesf = sb("sonesf", [1, 128])
        b_ident, b_ones, b_onesf = Buf("ident"), Buf("ones"), Buf("onesf")
        S.dma("sync", sident[:], ident, writes=[b_ident])
        S.emit("vector", lambda e: e.memset(sones[:], 1.0), writes=[b_ones])
        S.emit("vector", lambda e: e.memset(sonesf[:], 1.0), writes=[b_onesf])
        setup_consts(S, nc, es, [1e-5, 1e-6, 1.0 / D, 1.0, -1.0, 0.125])
        b_wc = {}
        b_chain = Buf("wcchain")
        for nm, srcw in cast_list:
            b = b_wc[nm] = Buf("wc_" + nm)
            cast_weights(S, nc, [(srcw, wc[nm], b)], chain=b_chain)
        b_wc["wo"] = Buf("wc_wo")
        for ei in range(5):
            cast_weights(S, nc, [(wo_i[ei], wc["wo"][ei], b_wc["wo"])], chain=b_chain)

        xnT_own_g = nc.sbuf_tensor("xnT_own", [128, KC * TO], BF16, side="right")
        xnT_own = xnT_own_g.__enter__()
        b_xnT_own = Buf("xnT_own")
        xnT_oth_g = nc.sbuf_tensor("xnT_oth", [128, KC * TO], BF16, side="right")
        xnT_oth = xnT_oth_g.__enter__()
        b_xnT_oth = Buf("xnT_oth")
        emit_layernorm(S, nc, x_own, x_oth, xnT_own, b_xnT_own, xnT_oth, b_xnT_oth,
                       sident, b_ident, xn_d=xn_d)
        S.dma("sync", xnT_d, xnT_own[:], reads=[b_xnT_own], writes=[Buf("xnT_d")])
        S.barrier()
        xnT_oth_g.__exit__(None, None, None)
        xnT_own_g.__exit__(None, None, None)

        nT_g = nc.sbuf_tensor("nT", [128, KC * L], BF16, side="right")
        nT = nT_g.__enter__()
        b_nT = Buf("nT")
        with ExitStack() as p1:
            sbl = lambda name, shape, dt=F32: p1.enter_context(nc.sbuf_tensor(name, shape, dt))
            psl = lambda name, shape, dt=F32: p1.enter_context(nc.psum_tensor(name, shape, dt))
            wbb = sbl("wbb", [128, 2 * D])
            b_wbb = Buf("wbb")
            rows2 = sbl("rows2", [1, 2 * D], BF16)
            b_r2 = Buf("rows2")
            S.dma("sync", rows2[0:1, 0:D], wc["lnw"], reads=[b_wc["lnw"]], writes=[b_r2])
            S.dma("sync", rows2[0:1, D:2 * D], wc["lnb"], reads=[b_wc["lnb"]], writes=[b_r2])
            ps_b = psl("ps_b", [128, 512])
            b_ps_b = Buf("ps_b")
            for c0 in range(0, 2 * D, 512):
                S.emit("tensor", lambda e, o=ps_b[:, :], ll=sones[:],
                       r=rows2[0:1, c0:c0 + 512]:
                       e.matmul(out=o, lhsT=ll, rhs=r, start=True, stop=True),
                       reads=[b_r2, b_ones], writes=[b_ps_b])
                S.emit("vector", lambda e, o=wbb[:, c0:c0 + 512], i=ps_b[:, :]:
                       e.tensor_copy(out=o, in_=i),
                       reads=[b_ps_b], writes=[b_wbb])
            xin = [sbl(f"n_xin{i}", [128, D]) for i in range(3)]
            b_xin = [Buf(f"n_xin{i}") for i in range(3)]
            sqs = [sbl(f"n_sq{i}", [128, D]) for i in range(2)]
            b_sq = [Buf(f"n_sq{i}") for i in range(2)]
            xnf = [sbl(f"n_xnf{i}", [128, D]) for i in range(2)]
            b_xnf = [Buf(f"n_xnf{i}") for i in range(2)]
            nb2 = [sbl(f"n_nb{i}", [128, D], BF16) for i in range(2)]
            b_nb2 = [Buf(f"n_nb{i}") for i in range(2)]
            xb2 = [sbl(f"n_xb{i}", [128, D], BF16) for i in range(2)]
            b_xb2 = [Buf(f"n_xb{i}") for i in range(2)]
            stat = [sbl(f"n_stat{i}", [128, 8]) for i in range(2)]
            b_stat = [Buf(f"n_stat{i}") for i in range(2)]
            ptr = [psl(f"n_ptr{i}", [128, 1024], BF16) for i in range(2)]
            b_ptr = [Buf(f"n_ptr{i}") for i in range(2)]
            stgx = [sbl(f"n_stgx{i}", [128, 1024], BF16) for i in range(2)]
            b_stgx = [Buf(f"n_stgx{i}") for i in range(2)]

            def ln_tile(xsrc, b_xsrc, st, b_st, sqv, b_sqv, outv, b_outv, eps):
                S.emit("vector", lambda e, o=st[:, 0:1], i=xsrc:
                       e.reduce_sum(out=o, in_=i, axis=AX),
                       reads=[b_xsrc], writes=[b_st])
                S.emit("vector", lambda e, o=st[:, 1:2], i=st[:, 0:1]:
                       e.tensor_scalar_mul(out=o, in0=i, scalar1=1.0 / D),
                       reads=[b_st], writes=[b_st])
                S.emit("vector", lambda e, o=st[:, 0:1], i=st[:, 1:2]:
                       e.tensor_scalar_mul(out=o, in0=i, scalar1=-1.0),
                       reads=[b_st], writes=[b_st])
                S.emit("scalar", lambda e, o=sqv, i=xsrc, bi=st[:, 0:1]:
                       e.activation(out=o, in_=i, func=AF.Square, bias=bi, scale=1.0),
                       reads=[b_xsrc, b_st], writes=[b_sqv])
                S.emit("vector", lambda e, o=st[:, 2:3], i=sqv:
                       e.reduce_sum(out=o, in_=i, axis=AX),
                       reads=[b_sqv], writes=[b_st])
                S.emit("scalar", lambda e, o=st[:, 3:4], i=st[:, 2:3], bi=eps:
                       e.activation(out=o, in_=i, func=AF.Sqrt, bias=bi, scale=1.0 / D),
                       reads=[b_st], writes=[b_st])
                S.emit("vector", lambda e, o=st[:, 2:3], i=st[:, 3:4]:
                       e.reciprocal(out=o, in_=i),
                       reads=[b_st], writes=[b_st])
                S.emit("vector", lambda e, o=outv, i=xsrc, m=st[:, 1:2], r=st[:, 2:3]:
                       e.tensor_scalar(out=o, in0=i, scalar1=m, scalar2=r,
                                       op0=ALU.subtract, op1=ALU.mult),
                       reads=[b_xsrc, b_st], writes=[b_outv])

            for tt in range(-2, NT_A):
                warm = tt < 0
                ttw = max(tt, 0)
                i3, i2 = ttw % 3, (tt + 2) % 2 if warm else ttw % 2
                tt = ttw
                S.dma("sync", xin[i3][:], x_nat[tt * 128:(tt + 1) * 128, :],
                      writes=[b_xin[i3]])
                ln_tile(xin[i3][:], b_xin[i3], stat[i2], b_stat[i2], sqs[i2][:],
                        b_sq[i2], xnf[i2][:], b_xnf[i2], 1e-5)
                ln_tile(xnf[i2][:], b_xnf[i2], stat[i2], b_stat[i2], sqs[i2][:],
                        b_sq[i2], sqs[i2][:], b_sq[i2], 1e-5)
                S.emit("vector", lambda e, o=sqs[i2][:], a=sqs[i2][:], b_=wbb[:, 0:D]:
                       e.tensor_mul(out=o, in0=a, in1=b_),
                       reads=[b_wbb, b_sq[i2]], writes=[b_sq[i2]])
                S.emit("vector", lambda e, o=nb2[i2][:], a=sqs[i2][:], b_=wbb[:, D:2 * D]:
                       e.tensor_add(out=o, in0=a, in1=b_),
                       reads=[b_wbb, b_sq[i2]], writes=[b_nb2[i2]])
                S.emit("vector", lambda e, o=xb2[i2][:], i=xnf[i2][:]:
                       e.tensor_copy(out=o, in_=i),
                       reads=[b_xnf[i2]], writes=[b_xb2[i2]])
                if warm:
                    continue
                for kc in range(KC):
                    S.emit("tensor", lambda e, o=ptr[i2][:, kc * 128:(kc + 1) * 128],
                           i=nb2[i2][:, kc * 128:(kc + 1) * 128], idn=sident[:]:
                           e.transpose(out=o, in_=i, identity=idn),
                           reads=[b_nb2[i2], b_ident], writes=[b_ptr[i2]])
                dview = nT[:].rearrange("p (c t) -> p c t", c=KC)[:, :, tt * 128:(tt + 1) * 128]
                S.emit("vector", lambda e, o=dview,
                       i=ptr[i2][:].rearrange("p (c t) -> p c t", c=KC):
                       e.tensor_copy(out=o, in_=i),
                       reads=[b_ptr[i2]], writes=[b_nT])
                for kc in range(KC):
                    S.emit("tensor", lambda e, o=ptr[i2][:, kc * 128:(kc + 1) * 128],
                           i=xb2[i2][:, kc * 128:(kc + 1) * 128], idn=sident[:]:
                           e.transpose(out=o, in_=i, identity=idn),
                           reads=[b_xb2[i2], b_ident], writes=[b_ptr[i2]])
                S.emit("vector", lambda e, o=stgx[i2][:], i=ptr[i2][:]:
                       e.tensor_copy(out=o, in_=i),
                       reads=[b_ptr[i2]], writes=[b_stgx[i2]])
                S.dma("sync",
                      xnTf_d.rearrange("p (c t) -> p c t", c=KC)[:, :, tt * 128:(tt + 1) * 128],
                      stgx[i2][:].rearrange("p (c t) -> p c t", c=KC),
                      reads=[b_stgx[i2]], writes=[Buf(f"xnTf{i2}")],
                      sem=f"dma_xnTf{i2}")
            S.barrier()

        CW = 4100
        idxp3_i = es.enter_context(nc.sbuf_tensor("idxp3i", [96, 1], I32))
        b_idxp3 = Buf("idxp3i")
        S.dma("sync", idxp3_i[:], idxpar, writes=[b_idxp3])
        mixsmall = ExitStack()
        smx_ = lambda name, shape, dt=F32: mixsmall.enter_context(nc.sbuf_tensor(name, shape, dt))
        h_st = smx_("h_st3", [128, 128])
        b_hst = Buf("h_st3")
        asl_sb = smx_("asl3", [128, 128])
        b_asl = Buf("asl3")
        dtbc = smx_("dtbc3", [128, 8])
        b_dtbc = Buf("dtbc3")
        dsl_sb = smx_("dslk3", [128, 8])
        b_dsl = Buf("dslk3")
        dtw_sb = smx_("dtw_sb3", [64, ECH * 128], BF16)
        b_dtw = Buf("dtw_sb3")
        rowstg = [smx_(f"rowstg3{i}", [1, 1024]) for i in range(2)]
        b_rowstg = [Buf(f"rowstg3{i}") for i in range(2)]
        cvw = smx_("cvw3", [128, 48])
        b_cvw = Buf("cvw3")
        pmines = ExitStack()
        pmine = {dn: pmines.enter_context(nc.sbuf_tensor(f"pm3{dn}", [96, L], F32))
                 for dn in ("f", "b")}
        b_pm = {dn: Buf(f"pm3{dn}") for dn in ("f", "b")}
        dtlow = {dn: pmines.enter_context(nc.sbuf_tensor(f"dtl3{dn}", [64, L], BF16))
                 for dn in ("f", "b")}
        b_dtl = {dn: Buf(f"dtl3{dn}") for dn in ("f", "b")}

        xsT_g = nc.sbuf_tensor("xsT3", [128, ECH * CW], BF16)
        xsT = xsT_g.__enter__()
        b_xsT = Buf("xsT3")
        with ExitStack() as p2:
            sbl = lambda name, shape, dt=F32: p2.enter_context(nc.sbuf_tensor(name, shape, dt))
            psl = lambda name, shape, dt=F32: p2.enter_context(nc.psum_tensor(name, shape, dt))
            wcol = [sbl(f"wcol3{i}", [128, KC * 128], BF16) for i in range(2)]
            b_wcol = [Buf(f"wcol3{i}") for i in range(2)]
            ps_p = [psl(f"ps_p3{i}", [128, 512]) for i in range(2)]
            b_ps_p = [Buf(f"ps_p3{i}") for i in range(2)]
            stg = [sbl(f"stg3p{i}", [128, 512], BF16) for i in range(2)]
            b_stg = [Buf(f"stg3p{i}") for i in range(2)]
            wxs = sbl("wxs3", [128, KC * 96], BF16)
            b_wxs = Buf("wxs3")
            acc = sbl("acc3", [128, L], BF16)
            b_acc = Buf("acc3")
            ps_x = psl("ps_x3", [96, 512])
            b_ps_x = Buf("ps_x3")
            for dn, rev in (("f", False), ("b", True)):
                dp = dirp[dn]
                for which, wkey in ((0, f"inwx_{dn}"), (1, f"inwz_{dn}")):
                    for ec in range(ECH):
                        iw = ec % 2
                        S.dma("sync", wcol[iw][:].rearrange("p (c n) -> p c n", c=KC),
                              wc[wkey][:, ec * 128:(ec + 1) * 128].rearrange(
                                  "(c p) n -> p c n", p=128),
                              reads=[b_wc[wkey]], writes=[b_wcol[iw]])
                        if which == 0:
                            S.emit("vector", lambda e, o=xsT[:, ec * CW:ec * CW + 4]:
                                   e.memset(o, 0.0), writes=[b_xsT])
                        for tb in range(8):
                            ip = tb % 2
                            for kc in range(KC):
                                S.emit("tensor", lambda e, o=ps_p[ip][:, :],
                                       ll=wcol[iw][:, kc * 128:(kc + 1) * 128],
                                       r=nT[:, kc * L + tb * 512: kc * L + (tb + 1) * 512],
                                       st=(kc == 0), sp=(kc == KC - 1):
                                       e.matmul(out=o, lhsT=ll, rhs=r, start=st, stop=sp),
                                       reads=[b_wcol[iw], b_nT], writes=[b_ps_p[ip]])
                            if which == 0:
                                if rev:
                                    out_ap = xsT[:, ec * CW + 4 + L - (tb + 1) * 512:
                                                 ec * CW + 4 + L - tb * 512][:, ::-1]
                                else:
                                    out_ap = xsT[:, ec * CW + 4 + tb * 512:
                                                 ec * CW + 4 + (tb + 1) * 512]
                                S.emit("vector", lambda e, o=out_ap, i=ps_p[ip][:, :]:
                                       e.tensor_copy(out=o, in_=i),
                                       reads=[b_ps_p[ip]], writes=[b_xsT])
                            else:
                                stg_o = stg[ip][:][:, ::-1] if rev else stg[ip][:]
                                S.emit("vector", lambda e, o=stg_o, i=ps_p[ip][:, :]:
                                       e.tensor_copy(out=o, in_=i),
                                       reads=[b_ps_p[ip]], writes=[b_stg[ip]])
                                if rev:
                                    dsl_ = z_d[dn][ec * 128:(ec + 1) * 128,
                                                   L - (tb + 1) * 512: L - tb * 512]
                                else:
                                    dsl_ = z_d[dn][ec * 128:(ec + 1) * 128,
                                                   tb * 512:(tb + 1) * 512]
                                S.dma("sync", dsl_, stg[ip][:], reads=[b_stg[ip]],
                                      writes=[Buf(f"zd{ip}_{dn}")], sem=f"dma_zd{ip}_{dn}")
                S.dma("sync", cvw[:, 0:32], dp["convw"], writes=[b_cvw])
                S.dma("sync", cvw[:, 32:40], dp["convb"], writes=[b_cvw])
                for ec in range(ECH):
                    base = ec * CW
                    S.emit("vector", lambda e, o=acc[:],
                           i=xsT[:, base + 1: base + 1 + L], sc_=cvw[:, ec * 4:ec * 4 + 1]:
                           e.tensor_scalar_mul(out=o, in0=i, scalar1=sc_),
                           reads=[b_xsT, b_cvw], writes=[b_acc])
                    for k in range(1, 4):
                        S.emit("vector", lambda e, o=acc[:],
                               i=xsT[:, base + 1 + k: base + 1 + k + L],
                               sc_=cvw[:, ec * 4 + k:ec * 4 + k + 1], g=acc[:]:
                               e.scalar_tensor_tensor(out=o, in0=i, scalar=sc_, in1=g,
                                                      op0=ALU.mult, op1=ALU.add),
                               reads=[b_xsT, b_cvw, b_acc], writes=[b_acc])
                    S.emit("scalar", lambda e, o=xsT[:, base + 4: base + 4 + L], i=acc[:],
                           bi=cvw[:, 32 + ec:33 + ec]:
                           e.activation(out=o, in_=i, func=AF.Silu, bias=bi),
                           reads=[b_acc, b_cvw], writes=[b_xsT])
                    S.dma("sync", xc_d[dn][ec * 128:(ec + 1) * 128, :],
                          xsT[:, base + 4: base + 4 + L],
                          reads=[b_xsT], writes=[Buf(f"xcd{ec}_{dn}")],
                          sem=f"dma_xcd_{dn}")
                S.dma("sync", wxs[:].rearrange("p (c n) -> p c n", c=KC),
                      wc[f"xprojT_{dn}"].rearrange("(c p) n -> p c n", p=128),
                      reads=[b_wc[f"xprojT_{dn}"]], writes=[b_wxs])
                for tb in range(8):
                    for ec in range(ECH):
                        S.emit("tensor", lambda e, o=ps_x[:, :],
                               ll=wxs[:, ec * 96:(ec + 1) * 96],
                               r=xsT[:, ec * CW + 4 + tb * 512: ec * CW + 4 + (tb + 1) * 512],
                               st=(ec == 0), sp=(ec == ECH - 1):
                               e.matmul(out=o, lhsT=ll, rhs=r, start=st, stop=sp),
                               reads=[b_wxs, b_xsT], writes=[b_ps_x])
                    S.emit("vector", lambda e, o=pmine[dn][:, tb * 512:(tb + 1) * 512],
                           i=ps_x[:, :]:
                           e.tensor_copy(out=o, in_=i),
                           reads=[b_ps_x], writes=[b_pm[dn]])
                S.dma("sync", cc_p_in, pmine[dn][:], reads=[b_pm[dn]],
                      writes=[Buf(f"ccp_{dn}")], sem=f"dma_ccp_{dn}")
                b_ccpo = Buf(f"cc_p_out_{dn}")
                S.emit("gpsimd", lambda e: e.collective_compute(
                    "AllGather", ALU.bypass,
                    replica_groups=[[0, 1, 2, 3, 4, 5, 6, 7]],
                    ins=[cc_p_in.opt()], outs=[cc_p_out.opt()]),
                    after=[S.last_dma.get(f"dma_ccp_{dn}")], writes=[b_ccpo])
                S.emit("gpsimd", lambda e: e.indirect_dma_start(
                    out=acc[0:96, :], out_offset=None, in_=cc_p_out,
                    in_offset=bass.IndirectOffsetOnAxis(idxp3_i[:, 0:1], 0)),
                    reads=[b_ccpo, b_idxp3], writes=[b_acc], dma=True)
                S.emit("vector", lambda e, o=pmine[dn][:], a=pmine[dn][:],
                       b_=acc[0:96, :]:
                       e.tensor_add(out=o, in0=a, in1=b_),
                       reads=[b_acc, b_pm[dn]], writes=[b_pm[dn]])
                S.emit("vector", lambda e, o=dtlow[dn][:], i=pmine[dn][0:64, :]:
                       e.tensor_copy(out=o, in_=i),
                       reads=[b_pm[dn]], writes=[b_dtl[dn]])
                S.barrier()
        xsT_g.__exit__(None, None, None)
        S.barrier()
        nT_g.__exit__(None, None, None)

        with ExitStack() as p3:
            sbl = lambda name, shape, dt=F32: p3.enter_context(nc.sbuf_tensor(name, shape, dt))
            psl = lambda name, shape, dt=F32: p3.enter_context(nc.psum_tensor(name, shape, dt))
            QL = 1024
            ps_f = psl("ps_f3", [128, 512])
            b_ps_f = Buf("ps_f3")
            xcq = sbl("xcq3", [128, ECH * QL], BF16)
            b_xcq = Buf("xcq3")
            raw = sbl("raw3", [128, QL])
            b_raw = Buf("raw3")
            t_a = sbl("t_a3", [128, QL])
            b_ta = Buf("t_a3")
            bcb = sbl("bcb3", [128, 2 * QL])
            b_bcb = Buf("bcb3")
            dtq = sbl("dtq3", [128, ECH * QL], BF16)
            b_dt = Buf("dtq3")
            u_c = sbl("u_c3", [128, ECH * QL], BF16)
            b_u = Buf("u_c3")
            dA_t = sbl("dA3", [128, QL])
            b_dA = Buf("dA3")
            dBx = sbl("dBx3", [128, QL])
            b_dBx = Buf("dBx3")
            h_t = sbl("h_t3", [128, QL])
            b_h = Buf("h_t3")
            t2 = sbl("t23", [128, QL])
            b_t2 = Buf("t23")
            y_q = sbl("y_q3", [128, ECH * QL])
            b_y = Buf("y_q3")
            zch = sbl("zch3", [128, QL], BF16)
            b_zch = Buf("zch3")
            ySgq, b_ysq = zch, b_zch
            for dn, rev in (("f", False), ("b", True)):
                dp = dirp[dn]
                S.dma("sync", dtw_sb[:], wc[f"dtwT_{dn}"], reads=[b_wc[f"dtwT_{dn}"]],
                      writes=[b_dtw])
                S.dma("sync", dtbc[:], dp["dtb"], writes=[b_dtbc])
                S.dma("sync", asl_sb[:], dp["alog"], writes=[b_asl])
                S.emit("scalar", lambda e, o=asl_sb[:], i=asl_sb[:]:
                       e.activation(out=o, in_=i, func=AF.Exp),
                       reads=[b_asl], writes=[b_asl])
                S.emit("vector", lambda e, o=asl_sb[:], i=asl_sb[:]:
                       e.tensor_scalar_mul(out=o, in0=i, scalar1=-1.0),
                       reads=[b_asl], writes=[b_asl])
                S.dma("sync", dsl_sb[:], dp["dsk"], writes=[b_dsl])
                S.emit("vector", lambda e: e.memset(h_st[:], 0.0), writes=[b_hst])
                for tq in range(4):
                    q0 = tq * QL
                    S.dma("sync", xcq[:].rearrange("p (c n) -> p c n", c=ECH),
                          xc_d[dn][:, q0:q0 + QL].rearrange("(c p) n -> p c n", p=128),
                          writes=[b_xcq])
                    for ec in range(ECH):
                        for c0 in range(0, QL, 512):
                            S.emit("tensor", lambda e, o=ps_f[:, :],
                                   ll=dtw_sb[:, ec * 128:(ec + 1) * 128],
                                   r=dtlow[dn][:, q0 + c0: q0 + c0 + 512]:
                                   e.matmul(out=o, lhsT=ll, rhs=r, start=True, stop=True),
                                   reads=[b_dtw, b_dtl[dn]], writes=[b_ps_f])
                            S.emit("vector", lambda e, o=raw[:, c0:c0 + 512],
                                   i=ps_f[:, :], sc_=dtbc[:, ec:ec + 1]:
                                   e.tensor_scalar_add(out=o, in0=i, scalar1=sc_),
                                   reads=[b_ps_f, b_dtbc], writes=[b_raw])
                        S.emit("scalar", lambda e, o=t_a[:], i=raw[:]:
                               e.activation(out=o, in_=i, func=AF.Abs),
                               reads=[b_raw], writes=[b_ta])
                        S.emit("scalar", lambda e, o=t_a[:], i=t_a[:]:
                               e.activation(out=o, in_=i, func=AF.Exp, scale=-1.0),
                               reads=[b_ta], writes=[b_ta])
                        S.emit("scalar", lambda e, o=t_a[:], i=t_a[:]:
                               e.activation(out=o, in_=i, func=AF.Ln, bias=1.0),
                               reads=[b_ta], writes=[b_ta])
                        S.emit("vector", lambda e, o=raw[:], i=raw[:]:
                               e.tensor_relu(out=o, in_=i),
                               reads=[b_raw], writes=[b_raw])
                        S.emit("vector", lambda e, o=dtq[:, ec * QL:(ec + 1) * QL],
                               a=raw[:], b_=t_a[:]:
                               e.tensor_add(out=o, in0=a, in1=b_),
                               reads=[b_raw, b_ta], writes=[b_dt])
                        S.emit("vector", lambda e, o=u_c[:, ec * QL:(ec + 1) * QL],
                               a=dtq[:, ec * QL:(ec + 1) * QL],
                               b_=xcq[:, ec * QL:(ec + 1) * QL]:
                               e.tensor_mul(out=o, in0=a, in1=b_),
                               reads=[b_dt, b_xcq], writes=[b_u])
                    for s in range(DS):
                        for half, roff in ((0, 64 + s), (1, 80 + s)):
                            S.dma("sync", rowstg[half][:],
                                  pmine[dn][roff:roff + 1, q0:q0 + QL],
                                  reads=[b_pm[dn]], writes=[b_rowstg[half]])
                            for c0 in range(0, QL, 512):
                                S.emit("tensor", lambda e, o=ps_f[:, :], ll=sonesf[:],
                                       r=rowstg[half][0:1, c0:c0 + 512]:
                                       e.matmul(out=o, lhsT=ll, rhs=r, start=True,
                                                stop=True),
                                       reads=[b_rowstg[half], b_onesf], writes=[b_ps_f])
                                S.emit("vector", lambda e,
                                       o=bcb[:, half * QL + c0: half * QL + c0 + 512],
                                       i=ps_f[:, :]:
                                       e.tensor_copy(out=o, in_=i),
                                       reads=[b_ps_f], writes=[b_bcb])
                        for ec in range(ECH):
                            S.emit("scalar", lambda e, o=dA_t[:],
                                   i=dtq[:, ec * QL:(ec + 1) * QL],
                                   sc_=asl_sb[:, ec * 16 + s: ec * 16 + s + 1]:
                                   e.activation(out=o, in_=i, func=AF.Exp, scale=sc_),
                                   reads=[b_dt, b_asl], writes=[b_dA])
                            S.emit("vector", lambda e, o=dBx[:],
                                   a=u_c[:, ec * QL:(ec + 1) * QL], b_=bcb[:, 0:QL]:
                                   e.tensor_mul(out=o, in0=a, in1=b_),
                                   reads=[b_u, b_bcb], writes=[b_dBx])
                            S.emit("vector", lambda e, o=h_t[:], a=dA_t[:], b_=dBx[:],
                                   ini=h_st[:, ec * 16 + s: ec * 16 + s + 1]:
                                   e.tensor_tensor_scan(out=o, data0=a, data1=b_,
                                                        initial=ini, op0=ALU.mult,
                                                        op1=ALU.add),
                                   reads=[b_dA, b_dBx, b_hst], writes=[b_h])
                            S.emit("vector", lambda e,
                                   o=h_st[:, ec * 16 + s: ec * 16 + s + 1],
                                   i=h_t[:, QL - 1:QL]:
                                   e.tensor_copy(out=o, in_=i),
                                   reads=[b_h], writes=[b_hst])
                            S.emit("vector", lambda e, o=t2[:], a=h_t[:],
                                   b_=bcb[:, QL:2 * QL]:
                                   e.tensor_mul(out=o, in0=a, in1=b_),
                                   reads=[b_h, b_bcb], writes=[b_t2])
                            if s == 0:
                                S.emit("vector", lambda e, o=y_q[:, ec * QL:(ec + 1) * QL],
                                       i=t2[:]:
                                       e.tensor_copy(out=o, in_=i),
                                       reads=[b_t2], writes=[b_y])
                            else:
                                S.emit("vector", lambda e, o=y_q[:, ec * QL:(ec + 1) * QL],
                                       a=t2[:], b_=y_q[:, ec * QL:(ec + 1) * QL]:
                                       e.tensor_add(out=o, in0=a, in1=b_),
                                       reads=[b_t2, b_y], writes=[b_y])
                    for ec in range(ECH):
                        S.dma("sync", zch[:], z_d[dn][ec * 128:(ec + 1) * 128, q0:q0 + QL],
                              writes=[b_zch])
                        S.emit("scalar", lambda e, o=t_a[:], i=zch[:]:
                               e.activation(out=o, in_=i, func=AF.Silu),
                               reads=[b_zch], writes=[b_ta])
                        S.emit("vector", lambda e, o=t2[:],
                               i=xcq[:, ec * QL:(ec + 1) * QL],
                               sc_=dsl_sb[:, ec:ec + 1], g=y_q[:, ec * QL:(ec + 1) * QL]:
                               e.scalar_tensor_tensor(out=o, in0=i, scalar=sc_, in1=g,
                                                      op0=ALU.mult, op1=ALU.add),
                               reads=[b_xcq, b_dsl, b_y], writes=[b_t2])
                        ysq_o = ySgq[:][:, ::-1] if rev else ySgq[:]
                        S.emit("vector", lambda e, o=ysq_o, a=t2[:], b_=t_a[:]:
                               e.tensor_mul(out=o, in0=a, in1=b_),
                               reads=[b_t2, b_ta], writes=[b_ysq])
                        if rev:
                            dsl2 = ysg_d[dn][ec * 128:(ec + 1) * 128,
                                             L - q0 - QL: L - q0]
                        else:
                            dsl2 = ysg_d[dn][ec * 128:(ec + 1) * 128, q0:q0 + QL]
                        S.dma("sync", dsl2, ySgq[:], reads=[b_ysq],
                              writes=[Buf(f"ysg{ec}_{dn}")], sem=f"dma_ysg_{dn}")
                S.barrier()
        pmines.close()
        mixsmall.close()

        s_fin = sb("s_fin", [128, NT_O * D], BF16)
        b_s_fin = Buf("s_fin")
        with ExitStack() as p4:
            sbl = lambda name, shape, dt=F32: p4.enter_context(nc.sbuf_tensor(name, shape, dt))
            psl = lambda name, shape, dt=F32: p4.enter_context(nc.psum_tensor(name, shape, dt))
            mT = {dn: sbl(f"mT_{dn}", [128, ECH * D], BF16) for dn in ("f", "b")}
            b_mT = {dn: Buf(f"mT_{dn}") for dn in ("f", "b")}
            wrow = [sbl(f"wrow{i}", [128, KC * 128], BF16) for i in range(2)]
            b_wrow = [Buf(f"wrow{i}") for i in range(2)]
            prj = sbl("prj", [128, KC * D], BF16)
            b_prj = Buf("prj")
            ps_m = [psl(f"ps_m{i}", [128, 512]) for i in range(2)]
            b_ps_m = [Buf(f"ps_m{i}") for i in range(2)]
            for dn in ("f", "b"):
                S.dma("sync", prj[:].rearrange("p (c n) -> p c n", c=KC),
                      wc[f"projT_{dn}"].rearrange("(c p) n -> p c n", p=128),
                      reads=[b_wc[f"projT_{dn}"]], writes=[b_prj])
                for jc in range(ECH):
                    iw = jc % 2
                    S.dma("sync", wrow[iw][:].rearrange("p (c n) -> p c n", c=KC),
                          wc[f"outw_{dn}"][:, jc * 128:(jc + 1) * 128].rearrange(
                              "(c p) n -> p c n", p=128),
                          reads=[b_wc[f"outw_{dn}"]], writes=[b_wrow[iw]])
                    for half in range(2):
                        ip = half
                        for kc in range(KC):
                            S.emit("tensor", lambda e, o=ps_m[ip][:, :],
                                   ll=wrow[iw][:, kc * 128:(kc + 1) * 128],
                                   r=prj[:, kc * D + half * 512: kc * D + half * 512 + 512],
                                   st=(kc == 0), sp=(kc == KC - 1):
                                   e.matmul(out=o, lhsT=ll, rhs=r, start=st, stop=sp),
                                   reads=[b_wrow[iw], b_prj], writes=[b_ps_m[ip]])
                        S.emit("vector", lambda e,
                               o=mT[dn][:, jc * D + half * 512: jc * D + half * 512 + 512],
                               i=ps_m[ip][:, :]:
                               e.tensor_copy(out=o, in_=i),
                               reads=[b_ps_m[ip]], writes=[b_mT[dn]])
            pb_bf = sbl("pb_bf", [1, D], BF16)
            b_pbbf = Buf("pb_bf")
            S.dma("sync", pb_bf[:], wc["projb"], reads=[b_wc["projb"]], writes=[b_pbbf])
            ytile = [sbl(f"ytile{i}", [128, KC * 128], BF16) for i in range(2)]
            b_yt = [Buf(f"ytile{i}") for i in range(2)]
            stg4 = [sbl(f"stg4{i}", [128, 512], BF16) for i in range(2)]
            b_stg4 = [Buf(f"stg4{i}") for i in range(2)]
            for tt in range(NT_A):
                it = tt % 2
                for di, dn in enumerate(("f", "b")):
                    S.dma("sync", ytile[di][:].rearrange("p (c n) -> p c n", c=KC),
                          ysg_d[dn][:, tt * 128:(tt + 1) * 128].rearrange(
                              "(c p) n -> p c n", p=128),
                          writes=[b_yt[di]])
                for half in range(2):
                    ip = half
                    for di, dn in enumerate(("f", "b")):
                        for ec in range(ECH):
                            S.emit("tensor", lambda e, o=ps_m[ip][:, :],
                                   ll=ytile[di][:, ec * 128:(ec + 1) * 128],
                                   r=mT[dn][:, ec * D + half * 512: ec * D + half * 512 + 512],
                                   st=(di == 0 and ec == 0):
                                   e.matmul(out=o, lhsT=ll, rhs=r, start=st, stop=False),
                                   reads=[b_yt[di], b_mT[dn]], writes=[b_ps_m[ip]])
                    S.emit("tensor", lambda e, o=ps_m[ip][:, :], ll=sones[:],
                           r=pb_bf[0:1, half * 512:(half + 1) * 512]:
                           e.matmul(out=o, lhsT=ll, rhs=r, start=False, stop=True),
                           reads=[b_pbbf, b_ones], writes=[b_ps_m[ip]])
                    S.emit("vector", lambda e, o=stg4[ip][:], i=ps_m[ip][:, :]:
                           e.tensor_copy(out=o, in_=i),
                           reads=[b_ps_m[ip]], writes=[b_stg4[ip]])
                    S.dma("sync", cc_s_in[tt * 128:(tt + 1) * 128,
                                          half * 512:(half + 1) * 512],
                          stg4[ip][:], reads=[b_stg4[ip]],
                          writes=[Buf(f"ccs{ip}3")], sem=f"dma_ccs3{ip}")
            S.barrier()
        b_ccso = Buf("cc_s_out")
        S.emit("gpsimd", lambda e: e.collective_compute(
            "AllGather", ALU.bypass,
            replica_groups=[[0, 1, 2, 3, 4, 5, 6, 7]],
            ins=[cc_s_in.opt()], outs=[cc_s_out.opt()]),
            writes=[b_ccso])
        with ExitStack() as p5:
            sbl = lambda name, shape, dt=F32: p5.enter_context(nc.sbuf_tensor(name, shape, dt))
            idxs = sbl("idxs3", [128, 32], I32)
            b_idxs = Buf("idxs3")
            S.dma("sync", idxs[:, 0:16], idxme, writes=[b_idxs])
            S.dma("sync", idxs[:, 16:32], idxpr, writes=[b_idxs])
            sga = [sbl(f"sga3{i}", [128, D], BF16) for i in range(2)]
            b_sga = [Buf(f"sga3{i}") for i in range(2)]
            for tt in range(NT_O):
                for j in range(2):
                    S.emit("gpsimd", lambda e, o=sga[j][:],
                           ix=idxs[:, j * 16 + tt: j * 16 + tt + 1]:
                           e.indirect_dma_start(
                               out=o, out_offset=None, in_=cc_s_out,
                               in_offset=bass.IndirectOffsetOnAxis(ix, 0)),
                           reads=[b_ccso, b_idxs], writes=[b_sga[j]], dma=True)
                S.emit("vector", lambda e, o=s_fin[:, tt * D:(tt + 1) * D],
                       a=sga[0][:], b_=sga[1][:]:
                       e.tensor_add(out=o, in0=a, in1=b_),
                       reads=[b_sga[0], b_sga[1]], writes=[b_s_fin])
            S.barrier()

        with ExitStack() as p6:
            sbl = lambda name, shape, dt=F32: p6.enter_context(nc.sbuf_tensor(name, shape, dt))
            psl = lambda name, shape, dt=F32: p6.enter_context(nc.psum_tensor(name, shape, dt))
            xnT2 = sbl("xnT2", [128, KC * TO], BF16)
            b_xnT2 = Buf("xnT2")
            S.dma("sync", xnT2[:], xnT_d, writes=[b_xnT2])
            o_tok = sbl("o_tok3", [128, NT_O * 512], BF16)
            b_o_tok = Buf("o_tok3")
            mo_all = sbl("mo_all", [128, NT_O * 40])
            b_mo = Buf("mo_all")
            p6ab = ExitStack()
            sab_ = lambda name, shape, dt=F32: p6ab.enter_context(nc.sbuf_tensor(name, shape, dt))
            qT = sab_("qT3", [128, 4 * TO], BF16)
            b_qT = Buf("qT3")
            kT = sab_("kT3", [128, 4 * L], BF16)
            b_kT = Buf("kT3")
            wqk = sab_("wqk", [128, KC * 128], BF16)
            b_wqk = Buf("wqk")
            p6a = ExitStack()
            psa_ = lambda name, shape, dt=F32: p6a.enter_context(nc.psum_tensor(name, shape, dt))
            ps_a = [psa_(f"ps_a{i}", [128, 512]) for i in range(2)]
            b_ps_a = [Buf(f"ps_a{i}") for i in range(2)]
            lhsx = [sab_(f"lhsx{i}", [128, KC * 128], BF16) for i in range(2)]
            b_lhsx = [Buf(f"lhsx{i}") for i in range(2)]
            rhx = [sab_(f"rhx{i}", [128, KC * 512], BF16) for i in range(2)]
            b_rhx = [Buf(f"rhx{i}") for i in range(2)]
            for which, wkey, dst, b_dst, ntb in ((0, "wq", qT, b_qT, 4),
                                                (1, "wk", kT, b_kT, 8)):
                for ec in range(4):
                    iw = ec % 2
                    S.dma("sync", wqk[:].rearrange("p (c n) -> p c n", c=KC),
                          wc[wkey][:, ec * 128:(ec + 1) * 128].rearrange(
                              "(c p) n -> p c n", p=128),
                          reads=[b_wc[wkey]], writes=[b_wqk])
                    TT_ = TO if which == 0 else L
                    for tb in range(ntb):
                        ip = tb % 2
                        if which == 1:
                            S.dma("sync", rhx[ip][:].rearrange("p (c n) -> p c n", c=KC),
                                  xnTf_d.rearrange("p (c t) -> p c t", c=KC)[
                                      :, :, tb * 512:(tb + 1) * 512],
                                  writes=[b_rhx[ip]])
                        for kc in range(KC):
                            if which == 0:
                                rr_ = xnT2[:, kc * TO + tb * 512: kc * TO + (tb + 1) * 512]
                                rb_ = b_xnT2
                            else:
                                rr_ = rhx[ip][:, kc * 512:(kc + 1) * 512]
                                rb_ = b_rhx[ip]
                            S.emit("tensor", lambda e, o=ps_a[ip][:, :],
                                   ll=wqk[:, kc * 128:(kc + 1) * 128], r=rr_,
                                   st=(kc == 0), sp=(kc == KC - 1):
                                   e.matmul(out=o, lhsT=ll, rhs=r, start=st, stop=sp),
                                   reads=[b_wqk, rb_], writes=[b_ps_a[ip]])
                        S.emit("vector", lambda e,
                               o=dst[:, ec * TT_ + tb * 512: ec * TT_ + (tb + 1) * 512],
                               i=ps_a[ip][:, :]:
                               e.tensor_copy(out=o, in_=i),
                               reads=[b_ps_a[ip]], writes=[b_dst])
            wsd = sab_("wsd", [128, KC * 80], BF16)
            b_wsd = Buf("wsd")
            S.dma("sync", wsd[:, 0:KC * 40].rearrange("p (c n) -> p c n", c=KC),
                  wc["ws"].rearrange("(c p) n -> p c n", p=128),
                  reads=[b_wc["ws"]], writes=[b_wsd])
            S.dma("sync", wsd[:, KC * 40:KC * 80].rearrange("p (c n) -> p c n", c=KC),
                  wc["wd"].rearrange("(c p) n -> p c n", p=128),
                  reads=[b_wc["wd"]], writes=[b_wsd])
            wvs = sab_("wvs", [128, KC * 320], BF16)
            b_wvs = Buf("wvs")
            S.dma("sync", wvs[:].rearrange("p (c n) -> p c n", c=KC),
                  wc["wvf"].rearrange("(c p) n -> p c n", p=128),
                  reads=[b_wc["wvf"]], writes=[b_wvs])
            ps_g = psa_("ps_g3", [128, 40])
            b_ps_g = Buf("ps_g3")
            ps_xv = psa_("ps_xv", [128, 320])
            b_ps_xv = Buf("ps_xv")
            glv = sab_("glv", [128, 40])
            b_glv = Buf("glv")
            wvt = sab_("wvt", [128, 48])
            b_wvt = Buf("wvt")
            vstg = sab_("vstg", [128, 512], BF16)
            b_vstg = Buf("vstg")

            def top2(tile_ap, b_tile, out_ap, b_out, sig):
                S.emit("vector", lambda e, o=wvt[:, 0:8],
                       i=tile_ap.rearrange("p (h e) -> p h e", h=NH):
                       e.reduce_max(out=o, in_=i, axis=AX),
                       reads=[b_tile], writes=[b_wvt])
                for h in range(NH):
                    S.emit("vector", lambda e, o=wvt[:, 8 + h * 5:8 + (h + 1) * 5],
                           i=tile_ap[:, h * 5:(h + 1) * 5], sc_=wvt[:, h:h + 1]:
                           e.tensor_single_scalar(out=o, in_=i, scalar=sc_, op=ALU.is_ge),
                           reads=[b_tile, b_wvt], writes=[b_wvt])
                    S.emit("vector", lambda e, o=wvt[:, 8 + h * 5:8 + (h + 1) * 5],
                           i=wvt[:, 8 + h * 5:8 + (h + 1) * 5],
                           p=tile_ap[:, h * 5:(h + 1) * 5]:
                           e.scalar_tensor_tensor(out=o, in0=i, scalar=-1e30, in1=p,
                                                  op0=ALU.mult, op1=ALU.add),
                           reads=[b_tile, b_wvt], writes=[b_wvt])
                    S.emit("vector", lambda e, o=wvt[:, h:h + 1],
                           i=wvt[:, 8 + h * 5:8 + (h + 1) * 5]:
                           e.reduce_max(out=o, in_=i, axis=AX),
                           reads=[b_wvt], writes=[b_wvt])
                    S.emit("vector", lambda e, o=out_ap[:, h * 5:(h + 1) * 5],
                           i=tile_ap[:, h * 5:(h + 1) * 5], sc_=wvt[:, h:h + 1]:
                           e.tensor_single_scalar(out=o, in_=i, scalar=sc_, op=ALU.is_ge),
                           reads=[b_tile, b_wvt], writes=[b_out])
                if sig:
                    S.emit("scalar", lambda e, o=wvt[:, 8:48], i=tile_ap:
                           e.activation(out=o, in_=i, func=AF.Sigmoid),
                           reads=[b_tile], writes=[b_wvt])
                    S.emit("vector", lambda e, o=out_ap, a=out_ap, b_=wvt[:, 8:48]:
                           e.tensor_mul(out=o, in0=a, in1=b_),
                           reads=[b_wvt, b_out], writes=[b_out])

            for tt in range(NT_A):
                it = tt % 2
                S.dma("sync", lhsx[it][:].rearrange("p (c n) -> p c n", c=KC),
                      xnTf_d.rearrange("p (c t) -> p c t", c=KC)[
                          :, :, tt * 128:(tt + 1) * 128],
                      writes=[b_lhsx[it]])
                for kc in range(KC):
                    S.emit("tensor", lambda e, o=ps_g[:, :],
                           ll=lhsx[it][:, kc * 128:(kc + 1) * 128],
                           r=wsd[:, kc * 40:(kc + 1) * 40],
                           st=(kc == 0), sp=(kc == KC - 1):
                           e.matmul(out=o, lhsT=ll, rhs=r, start=st, stop=sp),
                           reads=[b_lhsx[it], b_wsd], writes=[b_ps_g])
                S.emit("vector", lambda e, o=glv[:], i=ps_g[:, :]:
                       e.tensor_copy(out=o, in_=i),
                       reads=[b_ps_g], writes=[b_glv])
                top2(glv[:], b_glv, wvt[:, 0:40] if False else glv[:], b_glv, True)
                for kc in range(KC):
                    S.emit("tensor", lambda e, o=ps_xv[:, :],
                           ll=lhsx[it][:, kc * 128:(kc + 1) * 128],
                           r=wvs[:, kc * 320:(kc + 1) * 320],
                           st=(kc == 0), sp=(kc == KC - 1):
                           e.matmul(out=o, lhsT=ll, rhs=r, start=st, stop=sp),
                           reads=[b_lhsx[it], b_wvs], writes=[b_ps_xv])
                for h in range(NH):
                    for ei in range(5):
                        if ei == 0:
                            S.emit("vector", lambda e, o=vstg[:, h * 64:(h + 1) * 64],
                                   i=ps_xv[:, 0:64], sc_=glv[:, h * 5:h * 5 + 1]:
                                   e.tensor_scalar_mul(out=o, in0=i, scalar1=sc_),
                                   reads=[b_ps_xv, b_glv], writes=[b_vstg])
                        else:
                            S.emit("vector", lambda e, o=vstg[:, h * 64:(h + 1) * 64],
                                   i=ps_xv[:, ei * 64:(ei + 1) * 64],
                                   sc_=glv[:, h * 5 + ei:h * 5 + ei + 1],
                                   g=vstg[:, h * 64:(h + 1) * 64]:
                                   e.scalar_tensor_tensor(out=o, in0=i, scalar=sc_,
                                                          in1=g, op0=ALU.mult,
                                                          op1=ALU.add),
                                   reads=[b_ps_xv, b_glv, b_vstg], writes=[b_vstg])
                S.dma("sync", v_d3[tt * 128:(tt + 1) * 128, :], vstg[:],
                      reads=[b_vstg], writes=[Buf("vd3")], sem="dma_vd3")
            for tt in range(NT_O):
                for kc in range(KC):
                    S.emit("tensor", lambda e, o=ps_g[:, :],
                           ll=xnT2[:, kc * TO + tt * 128: kc * TO + (tt + 1) * 128],
                           r=wsd[:, KC * 40 + kc * 40: KC * 40 + (kc + 1) * 40],
                           st=(kc == 0), sp=(kc == KC - 1):
                           e.matmul(out=o, lhsT=ll, rhs=r, start=st, stop=sp),
                           reads=[b_xnT2, b_wsd], writes=[b_ps_g])
                S.emit("vector", lambda e, o=glv[:], i=ps_g[:, :]:
                       e.tensor_copy(out=o, in_=i),
                       reads=[b_ps_g], writes=[b_glv])
                top2(glv[:], b_glv, mo_all[:, tt * 40:(tt + 1) * 40], b_mo, False)
            S.barrier()
            p6a.close()

            v_h = [sab_(f"v_h3{i}", [128, 32 * 66], BF16) for i in range(2)]
            b_vh = [Buf(f"v_h3{i}") for i in range(2)]
            p6b = ExitStack()
            psb_ = lambda name, shape, dt=F32: p6b.enter_context(nc.psum_tensor(name, shape, dt))
            ps_sc = psb_("ps_sc3", [128, 2048])
            b_ps_sc = Buf("ps_sc3")
            ps_pt = psb_("ps_pt3", [128, 1024], BF16)
            b_ps_pt = Buf("ps_pt3")
            ps_o = psb_("ps_o3", [128, 66])
            b_ps_o = Buf("ps_o3")
            p_sb = sab_("p_sb3", [128, 2048], BF16)
            b_p_sb = Buf("p_sb3")
            ptT = sab_("ptT3", [128, 32 * 512], BF16)
            b_ptT = Buf("ptT3")
            mx = sab_("mx3", [128, 16])
            b_mx = Buf("mx3")
            for h in range(NH):
                ih = h % 2
                S.dma("sync",
                      v_h[ih][:].rearrange("p (c n) -> p c n", c=32)[:, :, 0:64],
                      v_d3[:, h * 64:(h + 1) * 64].rearrange("(c p) n -> p c n", p=128),
                      writes=[b_vh[ih]])
                S.emit("vector", lambda e,
                       o=v_h[ih][:].rearrange("p (c n) -> p c n", c=32)[:, :, 64:65]:
                       e.memset(o, 1.0), writes=[b_vh[ih]])
                kchunk, krow = h // 2, (h % 2) * 64
                for qtg in range(4):
                    for qi in range(4):
                        qt = qtg * 4 + qi
                        qsl = qT[krow:krow + 64,
                                 kchunk * TO + qt * 128: kchunk * TO + (qt + 1) * 128]
                        for half in range(2):
                            for nb in range(4):
                                j0 = half * 2048 + nb * 512
                                S.emit("tensor", lambda e,
                                       o=ps_sc[:, nb * 512:(nb + 1) * 512],
                                       ll=qsl,
                                       r=kT[krow:krow + 64, kchunk * L + j0: kchunk * L + j0 + 512]:
                                       e.matmul(out=o, lhsT=ll, rhs=r, start=True, stop=True),
                                       reads=[b_qT, b_kT], writes=[b_ps_sc])
                            S.emit("vector", lambda e, o=mx[:, half:half + 1], i=ps_sc[:]:
                                   e.reduce_max(out=o, in_=i, axis=AX),
                                   reads=[b_ps_sc], writes=[b_mx])
                        S.emit("vector", lambda e, o=mx[:, 2:3], a=mx[:, 0:1], b_=mx[:, 1:2]:
                               e.tensor_max(out=o, in0=a, in1=b_),
                               reads=[b_mx], writes=[b_mx])
                        S.emit("vector", lambda e, o=mx[:, 3:4], i=mx[:, 2:3]:
                               e.tensor_scalar_mul(out=o, in0=i, scalar1=-0.125),
                               reads=[b_mx], writes=[b_mx])
                        for half in range(2):
                            for nb in range(4):
                                j0 = half * 2048 + nb * 512
                                S.emit("tensor", lambda e,
                                       o=ps_sc[:, nb * 512:(nb + 1) * 512],
                                       ll=qsl,
                                       r=kT[krow:krow + 64, kchunk * L + j0: kchunk * L + j0 + 512]:
                                       e.matmul(out=o, lhsT=ll, rhs=r, start=True, stop=True),
                                       reads=[b_qT, b_kT], writes=[b_ps_sc])
                            for nb in range(4):
                                S.emit("scalar", lambda e,
                                       o=p_sb[:, nb * 512:(nb + 1) * 512],
                                       i=ps_sc[:, nb * 512:(nb + 1) * 512],
                                       bi=mx[:, 3:4]:
                                       e.activation(out=o, in_=i, func=AF.Exp, bias=bi,
                                                    scale=0.125),
                                       reads=[b_ps_sc, b_mx], writes=[b_p_sb])
                            for jc8 in range(2):
                                for j8 in range(8):
                                    jl = jc8 * 8 + j8
                                    S.emit("tensor", lambda e,
                                           o=ps_pt[:, j8 * 128:(j8 + 1) * 128],
                                           i=p_sb[:, jl * 128:(jl + 1) * 128],
                                           idn=sident[:]:
                                           e.transpose(out=o, in_=i, identity=idn),
                                           reads=[b_p_sb, b_ident], writes=[b_ps_pt])
                                ov = ptT[:].rearrange("p (c q) -> p c q", c=32)[
                                    :, half * 16 + jc8 * 8: half * 16 + (jc8 + 1) * 8,
                                    qi * 128:(qi + 1) * 128]
                                S.emit("vector", lambda e, o=ov,
                                       i=ps_pt[:].rearrange("p (c q) -> p c q", c=8):
                                       e.tensor_copy(out=o, in_=i),
                                       reads=[b_ps_pt], writes=[b_ptT])
                        for jc in range(32):
                            S.emit("tensor", lambda e, o=ps_o[:, 0:65],
                                   ll=ptT[:, jc * 512 + qi * 128: jc * 512 + (qi + 1) * 128],
                                   r=v_h[ih][:, jc * 66: jc * 66 + 65],
                                   st=(jc == 0), sp=(jc == 31):
                                   e.matmul(out=o, lhsT=ll, rhs=r, start=st, stop=sp),
                                   reads=[b_vh[ih], b_ptT], writes=[b_ps_o])
                        S.emit("vector", lambda e, o=mx[:, 5:6], i=ps_o[:, 64:65]:
                               e.reciprocal(out=o, in_=i),
                               reads=[b_ps_o], writes=[b_mx])
                        S.emit("vector", lambda e,
                               o=o_tok[:, qt * 512 + h * 64: qt * 512 + (h + 1) * 64],
                               i=ps_o[:, 0:64], sc_=mx[:, 5:6]:
                               e.tensor_scalar_mul(out=o, in0=i, scalar1=sc_),
                               reads=[b_ps_o, b_mx], writes=[b_o_tok])
            S.barrier()
            p6b.close()
            p6ab.close()
            wosb = sbl("wosb", [64, 5 * D], BF16)
            b_wosb = Buf("wosb")
            for ei in range(5):
                S.dma("sync", wosb[:, ei * D:(ei + 1) * D], wc["wo"][ei],
                      reads=[b_wc["wo"]], writes=[b_wosb])
            zv = sbl("zv", [128, 64], BF16)
            b_zv = Buf("zv")
            ps_zt = psl("ps_zt", [64, 128], BF16)
            b_ps_zt = Buf("ps_zt")
            zvT = sbl("zvT", [64, 128], BF16)
            b_zvT = Buf("zvT")
            ps_out = [psl(f"ps_out{i}", [128, 512]) for i in range(2)]
            b_ps_out = [Buf(f"ps_out{i}") for i in range(2)]
            gr_f = sbl("gr_f3", [128, D])
            b_gr = Buf("gr_f3")
            gs_t = sbl("gs_t3", [128, D])
            b_gs = Buf("gs_t3")
            sv_t = sbl("sv_t3", [128, D])
            b_sv = Buf("sv_t3")
            xot = [sbl(f"xot3{i}", [128, D]) for i in range(2)]
            b_xot = [Buf(f"xot3{i}") for i in range(2)]
            xnre = [sbl(f"xnre3{i}", [128, D], BF16) for i in range(2)]
            b_xnre = [Buf(f"xnre3{i}") for i in range(2)]
            for tt in range(NT_O):
                i2 = tt % 2
                for ei in range(5):
                    for h in range(NH):
                        if h == 0:
                            S.emit("vector", lambda e, o=zv[:],
                                   i=o_tok[:, tt * 512 + 0 * 64: tt * 512 + 64],
                                   sc_=mo_all[:, tt * 40 + 0 * 5 + ei: tt * 40 + 0 * 5 + ei + 1]:
                                   e.tensor_scalar_mul(out=o, in0=i, scalar1=sc_),
                                   reads=[b_o_tok, b_mo], writes=[b_zv])
                        else:
                            S.emit("vector", lambda e, o=zv[:],
                                   i=o_tok[:, tt * 512 + h * 64: tt * 512 + (h + 1) * 64],
                                   sc_=mo_all[:, tt * 40 + h * 5 + ei: tt * 40 + h * 5 + ei + 1],
                                   g=zv[:]:
                                   e.scalar_tensor_tensor(out=o, in0=i, scalar=sc_,
                                                          in1=g, op0=ALU.mult,
                                                          op1=ALU.add),
                                   reads=[b_o_tok, b_mo, b_zv], writes=[b_zv])
                    S.emit("tensor", lambda e, o=ps_zt[:, :], i=zv[:], idn=sident[:]:
                           e.transpose(out=o, in_=i, identity=idn),
                           reads=[b_zv, b_ident], writes=[b_ps_zt])
                    S.emit("vector", lambda e, o=zvT[:], i=ps_zt[:, :]:
                           e.tensor_copy(out=o, in_=i),
                           reads=[b_ps_zt], writes=[b_zvT])
                    for half in range(2):
                        S.emit("tensor", lambda e, o=ps_out[half][:, :], ll=zvT[:],
                               r=wosb[:, ei * D + half * 512: ei * D + half * 512 + 512],
                               st=(ei == 0), sp=(ei == 4):
                               e.matmul(out=o, lhsT=ll, rhs=r, start=st, stop=sp),
                               reads=[b_zvT, b_wosb], writes=[b_ps_out[half]])
                for half in range(2):
                    S.emit("vector", lambda e, o=gr_f[:, half * 512:(half + 1) * 512],
                           i=ps_out[half][:, :]:
                           e.tensor_copy(out=o, in_=i),
                           reads=[b_ps_out[half]], writes=[b_gr])
                S.dma("sync", xot[i2][:], x_own[tt * 128:(tt + 1) * 128, :],
                      writes=[b_xot[i2]])
                S.dma("sync", xnre[i2][:], xn_d[tt * 128:(tt + 1) * 128, :],
                      writes=[b_xnre[i2]])
                S.emit("scalar", lambda e, o=gs_t[:], i=gr_f[:]:
                       e.activation(out=o, in_=i, func=AF.Silu),
                       reads=[b_gr], writes=[b_gs])
                S.emit("vector", lambda e, o=sv_t[:],
                       a=s_fin[:, tt * D:(tt + 1) * D], b_=xnre[i2][:]:
                       e.tensor_add(out=o, in0=a, in1=b_),
                       reads=[b_s_fin, b_xnre[i2]], writes=[b_sv])
                S.emit("vector", lambda e, o=gs_t[:], a=sv_t[:], b_=gs_t[:]:
                       e.tensor_mul(out=o, in0=a, in1=b_),
                       reads=[b_sv, b_gs], writes=[b_gs])
                S.emit("vector", lambda e, o=xot[i2][:], a=xot[i2][:], b_=gs_t[:]:
                       e.tensor_add(out=o, in0=a, in1=b_),
                       reads=[b_gs, b_xot[i2]], writes=[b_xot[i2]])
                S.dma("sync", xo[tt * 128:(tt + 1) * 128, :], xot[i2][:],
                      reads=[b_xot[i2]], writes=[Buf(f"xod3{i2}")],
                      sem=f"dma_xo{i2}")
        S.barrier("sync")
        S.build(nc, es)
    return nc


def _l3_in_maps(inputs):
    i = inputs
    ident = np.eye(128, dtype=np.float32).astype(ml_dtypes.bfloat16)
    shared = dict(
        lnw=np.asarray(i["L3_attn_ln_w"]).reshape(1, D),
        lnb=np.asarray(i["L3_attn_ln_b"]).reshape(1, D),
        wq_i=np.ascontiguousarray(np.asarray(i["L3_ffn_Wq"])),
        wk_i=np.ascontiguousarray(np.asarray(i["L3_ffn_Wk"])),
        ws_i=np.ascontiguousarray(np.asarray(i["L3_ffn_Ws"])),
        wd_i=np.ascontiguousarray(np.asarray(i["L3_ffn_Wd"])),
        wvf_i=np.ascontiguousarray(
            np.asarray(i["L3_ffn_Wv"]).transpose(1, 0, 2).reshape(D, 320)),
        wo_i=np.ascontiguousarray(np.asarray(i["L3_ffn_Wo"])),
        ident=ident,
    )
    projWT = np.ascontiguousarray(np.asarray(i["L3_attn_proj_W"]).T)

    def colchunk(v):
        return np.ascontiguousarray(v.reshape(8, 128).T)

    maps = []
    for c in range(8):
        b, half = c // 2, c % 2
        sl = slice(half * 1024, half * 1024 + 1024)
        m = dict(shared)
        for dn, pref in (("f", "f_"), ("b", "b_")):
            inWT = np.ascontiguousarray(np.asarray(i[f"L3_attn_{pref}inW"]).T)
            m[f"inwx_{dn}"] = np.ascontiguousarray(inWT[:, sl])
            m[f"inwz_{dn}"] = np.ascontiguousarray(
                inWT[:, 2048 + half * 1024: 2048 + half * 1024 + 1024])
            cw = np.asarray(i[f"L3_attn_{pref}convW"]).reshape(2048, 4)[sl]
            m[f"convw_{dn}"] = np.ascontiguousarray(
                cw.reshape(8, 128, 4).transpose(1, 0, 2).reshape(128, 32))
            m[f"convb_{dn}"] = colchunk(np.asarray(i[f"L3_attn_{pref}convb"])[sl])
            m[f"xprojT_{dn}"] = np.ascontiguousarray(
                np.asarray(i[f"L3_attn_{pref}xprojW"]).T[sl])
            m[f"dtwT_{dn}"] = np.ascontiguousarray(
                np.asarray(i[f"L3_attn_{pref}dtW"]).T[:, sl])
            m[f"dtb_{dn}"] = colchunk(np.asarray(i[f"L3_attn_{pref}dtb"])[sl])
            m[f"alog_{dn}"] = np.ascontiguousarray(
                np.asarray(i[f"L3_attn_{pref}A_log"])[sl].reshape(
                    8, 128, 16).transpose(1, 0, 2).reshape(128, 128))
            m[f"dsk_{dn}"] = colchunk(np.asarray(i[f"L3_attn_{pref}Dskip"])[sl])
            m[f"outw_{dn}"] = np.ascontiguousarray(
                np.asarray(i[f"L3_attn_{pref}outW"])[:, sl])
            m[f"projT_{dn}"] = np.ascontiguousarray(
                projWT[0:1024] if dn == "f" else projWT[1024:2048])
        m["projb"] = (np.asarray(i["L3_attn_proj_b"]).reshape(1, D)
                      if half == 0 else np.zeros((1, D), np.float32))
        t0 = half * TO
        rows_own = np.arange(t0, t0 + TO)
        p = c ^ 1
        m["idxme"] = np.ascontiguousarray(
            (4096 * c + rows_own).reshape(16, 128).T.astype(np.int32))
        m["idxpr"] = np.ascontiguousarray(
            (4096 * p + rows_own).reshape(16, 128).T.astype(np.int32))
        m["idxpar"] = (96 * p + np.arange(96)).astype(np.int32).reshape(96, 1)
        maps.append(m)
    return maps


def _rope_tables():
    half = 32
    inv = (1.0 / 10000.0 ** (np.arange(half, dtype=np.float64) / half))
    ang = np.arange(L, dtype=np.float64)[:, None] * inv[None, :]
    cos = np.cos(ang).astype(np.float32)
    sin = np.sin(ang).astype(np.float32)
    cosr = np.tile(cos, (1, D // half))
    sinr = np.tile(sin, (1, D // half))
    return cosr, sinr


_QK_PERM = np.concatenate([np.concatenate([np.arange(0, 64, 2) + 64 * h,
                                           np.arange(1, 64, 2) + 64 * h])
                           for h in range(16)])


def _l0_in_maps(inputs):
    x = np.asarray(inputs["L0_attn_x"], dtype=np.float32)
    wqkv = np.asarray(inputs["L0_attn_w_qkv"])
    bqkv = np.asarray(inputs["L0_attn_b_qkv"])
    perm = np.concatenate([_QK_PERM, 1024 + _QK_PERM, 2048 + np.arange(1024)])
    wqkvT = np.ascontiguousarray(wqkv[perm].T)
    bqkv_p = np.ascontiguousarray(bqkv[perm]).reshape(1, 3 * D)
    gq = np.ascontiguousarray(
        np.asarray(inputs["L0_attn_q_gamma"]).reshape(-1)[_QK_PERM]).reshape(1, D)
    gk = np.ascontiguousarray(
        np.asarray(inputs["L0_attn_k_gamma"]).reshape(-1)[_QK_PERM]).reshape(1, D)
    woutT = np.ascontiguousarray(np.asarray(inputs["L0_attn_w_out"]).T)
    bout = np.asarray(inputs["L0_attn_b_out"]).reshape(1, D)
    sgT = np.ascontiguousarray(np.asarray(inputs["L0_ffn_sg"]).T)
    suT = np.ascontiguousarray(np.asarray(inputs["L0_ffn_su"]).T)
    sdT = np.ascontiguousarray(np.asarray(inputs["L0_ffn_sd"]).T)
    rgT = np.ascontiguousarray(np.asarray(inputs["L0_ffn_rg"]).transpose(0, 2, 1))
    ruT = np.ascontiguousarray(np.asarray(inputs["L0_ffn_ru"]).transpose(0, 2, 1))
    rdT = np.ascontiguousarray(np.asarray(inputs["L0_ffn_rd"]).transpose(0, 2, 1))
    wrT = np.ascontiguousarray(np.asarray(inputs["L0_ffn_wr"]).T)
    rbv = np.asarray(inputs["L0_ffn_rb"]).reshape(1, 7)
    ident = np.eye(128, dtype=np.float32).astype(ml_dtypes.bfloat16)
    cosr, sinr = _rope_tables()

    shared = dict(wqkvT=wqkvT, bqkv=bqkv_p, gq=gq, gk=gk, woutT=woutT, bout=bout,
                  sgT=sgT, suT=suT, sdT=sdT, rgT=rgT, ruT=ruT, rdT=rdT,
                  wrT=wrT, rb=rbv, ident=ident)
    maps = []
    for c in range(8):
        b, half = c // 2, c % 2
        t0 = half * TO
        rows_own = np.arange(t0, t0 + TO)
        rows_oth = np.arange((1 - half) * TO, (1 - half) * TO + TO)
        m = dict(shared)
        m["x_own"] = np.ascontiguousarray(x[b, rows_own])
        m["x_oth"] = np.ascontiguousarray(x[b, rows_oth])
        order = np.concatenate([rows_own, rows_oth])
        m["cosr"] = np.ascontiguousarray(cosr[order])
        m["sinr"] = np.ascontiguousarray(sinr[order])
        maps.append(m)
    return maps


_NC_CACHE = {}


def _run_layer(idx, inputs, x):
    builders = {1: build_l1, 2: build_l2, 3: build_l3}
    mappers = {1: _l1_in_maps, 2: _l2_in_maps, 3: _l3_in_maps}
    key = f"l{idx}"
    if key not in _NC_CACHE:
        _NC_CACHE[key] = builders[idx]()
    nc = _NC_CACHE[key]
    if idx == 1:
        shared = mappers[idx](inputs)
        maps = _mk_layer_maps(shared, x)
    else:
        maps = mappers[idx](inputs)
        for c in range(8):
            b, half = c // 2, c % 2
            t0 = half * TO
            maps[c]["x_own"] = np.ascontiguousarray(x[b, t0:t0 + TO])
            maps[c]["x_oth"] = np.ascontiguousarray(x[b, TO - t0:2 * TO - t0])
            maps[c]["x_nat"] = np.ascontiguousarray(x[b])
    res = run_bass_kernel_spmd(nc, maps, core_ids=list(range(8)))
    out = np.empty((B, L, D), dtype=np.float32)
    for c in range(8):
        b, half = c // 2, c % 2
        out[b, half * TO:(half + 1) * TO] = res.results[c]["xo"]
    return out


def _run_l0(inputs):
    if "l0" not in _NC_CACHE:
        _NC_CACHE["l0"] = build_l0()
    nc = _NC_CACHE["l0"]
    maps = _l0_in_maps(inputs)
    res = run_bass_kernel_spmd(nc, maps, core_ids=list(range(8)))
    out = np.empty((B, L, D), dtype=np.float32)
    for c in range(8):
        b, half = c // 2, c % 2
        out[b, half * TO:(half + 1) * TO] = res.results[c]["xo"]
    return out


def kernel(**inputs):
    x = _run_l0(inputs)
    x = _run_layer(1, inputs, x)
    x = _run_layer(2, inputs, x)
    x = _run_layer(3, inputs, x)
    return x


if __name__ == "__main__":
    print("kernel module ok")
